# version 39
# baseline (speedup 1.0000x reference)
import numpy as np
from contextlib import ExitStack

import concourse.bass as bass
import concourse.mybir as mybir
from concourse.bass_utils import run_bass_kernel_spmd

F32 = mybir.dt.float32
BF16 = mybir.dt.bfloat16
AF = mybir.ActivationFunctionType
ALU = mybir.AluOpType

NCORES = 8
TCORE = 2048
SLAB = 512
NSLAB = TCORE // SLAB
HALF = 256
NGRP_H = HALF // 8
NBATCH_H = NGRP_H // 4
D = 1024
NCH = 8
EPS = 1e-5

DEBUG = 0

ENGS = ("pe", "dve", "act", "pool", "sp")
DMASEMS = ("ldc", "ldw1", "ldq0", "ldq1", "ldwo", "ldx0", "ldx1", "st0", "st1",
           "dbg0", "dbg1", "dbg2", "dbg3", "dbg4", "dbg5", "dbg6", "dbg7", "dbg8")


class Sched:

    def __init__(self):
        self.q = {e: [] for e in ENGS}
        self.counts = {}
        for e in ENGS:
            self.counts[e] = 0
        for e in DMASEMS:
            self.counts[e] = 0

    def op(self, eng, fn, deps=(), sem=None, inc=1):
        sem = sem or eng
        dl = []
        for d in deps:
            if d is None:
                continue
            if isinstance(d, list):
                dl.extend(x for x in d if x is not None)
            else:
                dl.append(d)
        self.counts[sem] += inc
        tok = (sem, self.counts[sem])
        self.q[eng].append((dl, fn, sem, inc))
        return tok

    def replay(self, engobj, eng, sems):
        marks = {}
        for deps, fn, sem, inc in self.q[eng]:
            for (ps, pc) in deps:
                if marks.get(ps, 0) < pc:
                    engobj.wait_ge(sems[ps], pc)
                    marks[ps] = pc
            inst = fn(engobj)
            if inst is not None and inc:
                inst.then_inc(sems[sem], inc)


def build_nc():
    nc = bass.Bass("TRN2", target_bir_lowering=False)

    xt = nc.dram_tensor("xt", [D, TCORE], F32, kind="ExternalInput").ap()
    wqkv = nc.dram_tensor("wqkv", [D, 3 * D], F32, kind="ExternalInput").ap()
    w1t = nc.dram_tensor("w1t", [D, D], F32, kind="ExternalInput").ap()
    wout = nc.dram_tensor("wout", [D, D], F32, kind="ExternalInput").ap()
    upat = nc.dram_tensor("upat", [50, D], F32, kind="ExternalInput").ap()
    lup = nc.dram_tensor("lup", [50, 4], F32, kind="ExternalInput").ap()
    ldn = nc.dram_tensor("ldn", [4, 100], F32, kind="ExternalInput").ap()
    dnaux = nc.dram_tensor("dnaux", [100, D], F32, kind="ExternalInput").ap()
    bqd = nc.dram_tensor("bqd", [128, 24], F32, kind="ExternalInput").ap()
    b1d = nc.dram_tensor("b1d", [128, 8], F32, kind="ExternalInput").ap()
    bod = nc.dram_tensor("bod", [128, 8], F32, kind="ExternalInput").ap()
    maskd = nc.dram_tensor("maskd", [128, 128], F32, kind="ExternalInput").ap()
    identd = nc.dram_tensor("identd", [128, 64], F32, kind="ExternalInput").ap()
    yt = nc.dram_tensor("yt", [D, TCORE], F32, kind="ExternalOutput").ap()
    if DEBUG:
        dbg_y = nc.dram_tensor("dbg_y", [128, NCH * SLAB], F32, kind="ExternalOutput").ap()
        dbg_h2 = nc.dram_tensor("dbg_h2", [128, NCH * SLAB], BF16, kind="ExternalOutput").ap()
        dbg_qg = nc.dram_tensor("dbg_qg", [64, NGRP_H * 128], BF16, kind="ExternalOutput").ap()
        dbg_kg = nc.dram_tensor("dbg_kg", [64, NGRP_H * 128], BF16, kind="ExternalOutput").ap()
        dbg_vg = nc.dram_tensor("dbg_vg", [64, NGRP_H * 128], BF16, kind="ExternalOutput").ap()
        dbg_c2 = nc.dram_tensor("dbg_c2", [128, 8 * SLAB], BF16, kind="ExternalOutput").ap()
        dbg_z = nc.dram_tensor("dbg_z", [128, NCH * SLAB], F32, kind="ExternalOutput").ap()
        dbg_xh = nc.dram_tensor("dbg_xh", [128, NCH * SLAB], BF16, kind="ExternalOutput").ap()
        dbg_rows = nc.dram_tensor("dbg_rows", [128, 2048], F32, kind="ExternalOutput").ap()

    s = Sched()
    with ExitStack() as ctx:
        al = lambda *a, **k: ctx.enter_context(nc.sbuf_tensor(*a, **k))
        alp = lambda *a, **k: ctx.enter_context(nc.psum_tensor(*a, **k))
        stg = al("stg", [128, 3072], F32)
        rowb = al("rowb", [128, 1024], BF16)
        wqkvb = al("wqkvb", [128, 8 * 3072], BF16)
        weffb = al("weffb", [128, 8 * 1024], BF16)
        woutb = al("woutb", [128, 8 * 1024], BF16)
        rbuf = [al(f"rbuf{i}", [128, NCH * SLAB], F32) for i in range(3)]
        xhat = al("xhat", [128, NCH * SLAB], BF16)
        x2b = al("x2b", [128, NCH * SLAB], BF16)
        hbuf = al("hbuf", [128, NCH * SLAB], BF16)
        qg = al("qg", [64, NGRP_H * 128], BF16)
        kg = al("kg", [64, NGRP_H * 128], BF16)
        vg = al("vg", [64, NGRP_H * 128], BF16)
        ebuf = [al(f"ebuf{i}", [128, 512], BF16) for i in range(2)]
        vb1 = [al(f"vb1_{i}", [128, 4 * 65], BF16) for i in range(2)]
        ctxg = [al(f"ctxg{i}", [128, 256], BF16) for i in range(2)]
        c2sb = al("c2sb", [128, 8 * SLAB], BF16)
        bq = al("bq", [128, 24], F32)
        b1 = al("b1", [128, 8], F32)
        bo = al("bo", [128, 8], F32)
        maskf = al("maskf", [128, 128], F32)
        maskb = al("maskb", [128, 128], BF16)
        identf = al("identf", [128, 64], F32)
        idb = al("idb", [128, 64], BF16)
        ones128 = al("ones128", [128, 128], BF16)
        epsc = al("epsc", [128, 1], F32)
        ksc = al("ksc", [128, 1], F32)
        nhc = al("nhc", [128, 1], F32)
        ivdc = al("ivdc", [128, 1], F32)
        rcol = [al(f"rcol{i}", [128, 4], F32) for i in range(2)]
        d4 = [al(f"d4_{i}", [128, 256], BF16) for i in range(2)]
        lora_up = al("lora_up", [50, 4], F32)
        lora_dn = al("lora_dn", [4, 100], F32)
        pbank = [alp(f"pbank{i}", [128, 512], F32) for i in range(2)]
        pstat = [alp(f"pstat{i}", [128, 512], F32) for i in range(2)]
        pscore = alp("pscore", [128, 512], F32)
        pctx = alp("pctx", [128, 512], F32)
        pvgp = alp("pvgp", [128, 1024], BF16)
        pc2 = alp("pc2", [128, 512], F32)

        sem_handles = {}
        for name in ENGS + DMASEMS:
            sem_handles[name] = ctx.enter_context(nc.semaphore(name))
        block = ctx.enter_context(nc.Block())

        c2f32 = c2sb[:].bitcast(F32)
        dna = c2f32[0:100, 0:1024]
        upa = c2f32[0:64, 1024:2048]
        w1t_sl2 = xhat[:].bitcast(F32)[:, 1024:2048]
        rows_m = stg[:, 0:512]
        rows_rstd = stg[:, 512:1024]
        rows_t = stg[:, 1024:1536]
        rows_v = stg[:, 1536:2048]

        def dma(dst, src, sem, deps=()):
            return s.op("sp", lambda e, dst=dst, src=src: e.dma_start(out=dst, in_=src),
                        deps=deps, sem=sem, inc=16)

        t_ones = s.op("dve", lambda e: e.memset(ones128[:], 1.0))
        t_eps = s.op("dve", lambda e: e.memset(epsc[:], EPS))
        t_ksc = s.op("dve", lambda e: e.memset(ksc[:], 0.125))
        t_nh = s.op("dve", lambda e: e.memset(nhc[:], -0.5))
        t_ivd = s.op("dve", lambda e: e.memset(ivdc[:], 1.0 / 1024.0))
        t_v1 = [s.op("dve", lambda e, i=i: e.memset(
            vb1[i][:].rearrange("p (g c) -> p g c", c=65)[:, :, 64:65], 1.0)) for i in range(2)]
        d_lora = [dma(dst, srcd, fam) for dst, srcd, fam in
                  ((lora_up[:], lup, "dbg0"), (lora_dn[:], ldn, "dbg1"),
                   (dna[0:100, :], dnaux, "dbg2"), (upa[0:50, :], upat, "dbg3"))]
        for dst, srcd in ((bq[:], bqd), (b1[:], b1d), (bo[:], bod),
                          (maskf[:], maskd), (identf[:], identd)):
            t_ldc = dma(dst, srcd, "ldc")
        d_b = [t_ldc] * 3
        t_maskb = s.op("dve", lambda e: e.tensor_copy(out=maskb[:], in_=maskf[:]),
                       deps=[t_ldc])
        t_idb = s.op("dve", lambda e: e.tensor_copy(out=idb[:], in_=identf[:]),
                     deps=[t_ldc])
        t_bq = s.op("dve", lambda e: e.tensor_scalar(
            out=bq[:, 8:16], in0=bq[:, 8:16], scalar1=0.125, scalar2=None,
            op0=ALU.mult), deps=[t_ldc])
        t_b1 = s.op("dve", lambda e: e.tensor_scalar(
            out=b1[:], in0=b1[:], scalar1=0.5, scalar2=None, op0=ALU.mult),
            deps=[t_ldc])

        xt_v = xt.rearrange("(c p) n -> p c n", p=128)
        yt_v = yt.rearrange("(c p) n -> p c n", p=128)
        t_x = [None] * NSLAB
        t_x[0] = dma(rbuf[0][:].rearrange("p (c n) -> p c n", n=SLAB),
                     xt_v[:, :, 0:SLAB], "ldx0")

        A_sl = stg[0:4, 0:1024]
        B_sl = stg[0:100, 1024:2048]
        w1t_sl = stg[:, 2048:3072]

        t_A_mm = s.op("pe", lambda e: (
            e.matmul(out=pbank[0][0:4, :], lhsT=lora_up[:], rhs=upa[0:50, 0:512],
                     start=True, stop=True),
            e.matmul(out=pbank[1][0:4, :], lhsT=lora_up[:], rhs=upa[0:50, 512:1024],
                     start=True, stop=True))[-1], deps=d_lora[:2] + [d_lora[3]])
        t_A = s.op("dve", lambda e: (
            e.tensor_copy(out=A_sl[:, 0:512], in_=pbank[0][0:4, :]),
            e.tensor_copy(out=A_sl[:, 512:1024], in_=pbank[1][0:4, :]))[-1],
            deps=[t_A_mm])
        t_B_mm = s.op("pe", lambda e: (
            e.matmul(out=pbank[0][0:100, :], lhsT=lora_dn[:], rhs=A_sl[:, 0:512],
                     start=True, stop=True),
            e.matmul(out=pbank[1][0:100, :], lhsT=lora_dn[:], rhs=A_sl[:, 512:1024],
                     start=True, stop=True))[-1], deps=[t_A, d_lora[1]])
        t_B = s.op("dve", lambda e: (
            e.tensor_copy(out=B_sl[:, 0:512], in_=pbank[0][0:100, :]),
            e.tensor_copy(out=B_sl[:, 512:1024], in_=pbank[1][0:100, :]))[-1],
            deps=[t_B_mm])
        t_weff_parts = []
        for kc in range(NCH):
            slw = w1t_sl if kc % 2 == 0 else w1t_sl2
            war = [t_weff_parts[kc - 2]] if kc >= 2 else []
            d_w1 = dma(slw, w1t.rearrange("(c p) m -> c p m", p=128)[kc],
                       "ldw1" if kc % 2 == 0 else "ldq1", deps=war)
            t_w1h = s.op("dve", lambda e, slw=slw: e.tensor_scalar(
                out=slw, in0=slw, scalar1=0.5, scalar2=None, op0=ALU.mult),
                deps=[d_w1])
            cbk0 = [pbank[0], pbank[1], pscore, pctx][(2 * kc) % 4]
            cbk1 = [pbank[0], pbank[1], pscore, pctx][(2 * kc + 1) % 4]
            t_c = s.op("pe", lambda e, kc=kc, cbk0=cbk0, cbk1=cbk1: (
                e.matmul(out=cbk0[:], lhsT=dna[0:100, kc * 128:(kc + 1) * 128],
                         rhs=B_sl[:, 0:512], start=True, stop=True),
                e.matmul(out=cbk1[:], lhsT=dna[0:100, kc * 128:(kc + 1) * 128],
                         rhs=B_sl[:, 512:1024], start=True, stop=True))[-1],
                deps=[t_B, d_lora[2]] + ([t_weff_parts[kc - 2]] if kc >= 2 else []))
            t_we = s.op("dve", lambda e, kc=kc, slw=slw, cbk0=cbk0, cbk1=cbk1: (
                e.scalar_tensor_tensor(
                    out=weffb[:, kc * 1024:kc * 1024 + 512], in0=cbk0[:],
                    scalar=0.5, in1=slw[:, 0:512], op0=ALU.mult, op1=ALU.add),
                e.scalar_tensor_tensor(
                    out=weffb[:, kc * 1024 + 512:(kc + 1) * 1024], in0=cbk1[:],
                    scalar=0.5, in1=slw[:, 512:1024], op0=ALU.mult, op1=ALU.add))[-1],
                deps=[t_c, t_w1h])
            t_weff_parts.append(t_we)
        t_weffb = t_weff_parts[-1]

        wqkv_v = wqkv.rearrange("(c p) m -> c p m", p=128)
        wout_v = wout.rearrange("(c p) m -> c p m", p=128)
        for kc in range(NCH):
            t_wqkvb = s.op("pool", lambda e, kc=kc: e.dma_start(
                out=wqkvb[:, kc * 3072:(kc + 1) * 3072], in_=wqkv_v[kc]),
                deps=[t_ldc] if kc == 0 else [], sem="ldq0", inc=16)
        for kc in range(NCH):
            t_woutb = s.op("pool", lambda e, kc=kc: e.dma_start(
                out=woutb[:, kc * 1024:(kc + 1) * 1024], in_=wout_v[kc]),
                deps=[], sem="ldwo", inc=16)

        bank_free = {"pscore": [t_woutb], "pstat0": [t_B_mm], "pstat1": [t_B_mm],
                     "pctx": [t_woutb], "pbank0": [t_woutb], "pbank1": [t_woutb]}
        proj_banks = [("pbank0", pbank[0]), ("pbank1", pbank[1]),
                      ("pscore", pscore), ("pctx", pctx)]
        t_xhat_free = [t_ones]
        t_hbuf_free = [t_ones]
        t_rowb_free = [t_ones]
        t_rows_free = [t_woutb]
        t_qkvg_free = [t_ones]
        t_ebuf_free = [[t_ones], [t_ones]]
        t_vb1_free = [[t_v1[0]], [t_v1[1]]]
        t_ctxg_free = [[t_ones], [t_ones]]
        t_rcol_free = [[t_ones], [t_ones]]
        t_d4_free = [[t_ones], [t_ones]]
        t_pvgp_free = [t_ones]
        t_pc2_free = [t_ones]
        t_c2_free = [t_weffb]
        t_store = [None] * NSLAB

        ln_calls = [0]
        t_x2b_free = [t_ones]

        def ln_front_start():
            return dict(cast=[], x2=[], mm_s=None, mm_q=None)

        def ln_front_square(st, c):
            nonlocal t_x2b_free
            t_c = st["cast"][c]
            t_q = s.op("dve", lambda e, c=c: e.tensor_tensor(
                out=x2b[:, c * SLAB:(c + 1) * SLAB],
                in0=xhat[:, c * SLAB:(c + 1) * SLAB],
                in1=xhat[:, c * SLAB:(c + 1) * SLAB], op=ALU.mult),
                deps=([t_c] if not isinstance(t_c, list) else t_c)
                     + (list(t_x2b_free) if c == 0 else []))
            st["x2"].append(t_q)

        def ln_front_chunk(st, c, tok, src, pre_cast=None, sq=True):
            nonlocal t_xhat_free
            if pre_cast is not None:
                t_c = pre_cast
            else:
                extra = list(t_xhat_free) if c == 0 else []
                t_c = s.op("dve", lambda e, c=c: e.tensor_copy(
                    out=xhat[:, c * SLAB:(c + 1) * SLAB],
                    in_=src[:, c * SLAB:(c + 1) * SLAB]),
                    deps=[tok] + extra)
            st["cast"].append(t_c)
            if sq:
                ln_front_square(st, c)

        def ln_front_finish(st, which="sq"):
            for c in range(NCH):
                cst = st["cast"][c]
                dep_s = (cst if isinstance(cst, list) else [cst]) + [t_ones]
                if c == 0:
                    dep_s += bank_free["pstat0"]
                if "s" in which:
                    st["mm_s"] = s.op("pe", lambda e, c=c: e.matmul(
                        out=pstat[0][:], lhsT=ones128[:],
                        rhs=xhat[:, c * SLAB:(c + 1) * SLAB],
                        start=(c == 0), stop=(c == NCH - 1), skip_group_check=True),
                        deps=dep_s, inc=(1 if c == NCH - 1 else 0))
            for c in range(NCH):
                if "q" not in which:
                    break
                dep_q = [st["x2"][c]]
                if c == 0:
                    dep_q += bank_free["pstat1"]
                if True:
                    st["mm_q"] = s.op("pe", lambda e, c=c: e.matmul(
                        out=pstat[1][:], lhsT=ones128[:],
                        rhs=x2b[:, c * SLAB:(c + 1) * SLAB],
                        start=(c == 0), stop=(c == NCH - 1), skip_group_check=True),
                        deps=dep_q, inc=(1 if c == NCH - 1 else 0))
            return st

        def ln_front(chunk_toks, src):
            if not isinstance(chunk_toks, list):
                chunk_toks = [chunk_toks] * NCH
            st = ln_front_start()
            for c in range(NCH):
                ln_front_chunk(st, c, chunk_toks[c], src)
            ln_front_finish(st)
            return st

        def ln_back(fr, dst=None, dst_war=None):
            nonlocal t_xhat_free, t_rowb_free, t_rows_free, t_x2b_free
            ln_calls[0] += 1
            t_mm_s, t_mm_q = fr["mm_s"], fr["mm_q"]
            t_cast = fr["cast"][-1]
            if isinstance(t_cast, list):
                t_cast = t_cast[-1]
            t_rm = s.op("dve", lambda e: e.tensor_scalar(
                out=rowb[:, 0:512], in0=pstat[0][:], scalar1=1.0 / 1024.0,
                scalar2=None, op0=ALU.mult), deps=[t_mm_s] + t_rowb_free)
            t_msq = s.op("dve", lambda e: e.tensor_tensor(
                out=rows_t, in0=rowb[:, 0:512], in1=rowb[:, 0:512],
                op=ALU.mult), deps=[t_rm] + t_rows_free)
            t_var = s.op("dve", lambda e: e.scalar_tensor_tensor(
                out=rows_v, in0=pstat[1][:], scalar=ivdc[:, 0:1], in1=rows_t,
                op0=ALU.mult, op1=ALU.subtract),
                deps=[t_mm_q, t_msq] + t_rows_free)
            t_ln = s.op("act", lambda e: e.activation(
                out=rows_m, in_=rows_v, func=AF.Ln, bias=epsc[:, 0:1], scale=1.0),
                deps=[t_var, t_eps] + t_rows_free)
            t_rstd = s.op("act", lambda e: e.activation(
                out=rowb[:, 512:1024], in_=rows_m, func=AF.Exp, bias=0.0,
                scale=nhc[:, 0:1]), deps=[t_ln, t_nh] + t_rowb_free)
            xv = xhat[:].rearrange("p (c n) -> p c n", n=SLAB)
            dstb = hbuf if dst is None else dst
            war = list(t_hbuf_free) if dst_war is None else list(dst_war)
            hv = dstb[:].rearrange("p (c n) -> p c n", n=SLAB)
            t_apps = []
            t_subs = []
            for hh in range(2):
                ho = hh * HALF
                mb = rowb[:, ho:ho + HALF].unsqueeze(1).broadcast_to([128, NCH, HALF])
                t_sub = s.op("dve", lambda e, ho=ho, mb=mb: e.tensor_tensor(
                    out=hv[:, :, ho:ho + HALF], in0=xv[:, :, ho:ho + HALF],
                    in1=mb, op=ALU.subtract),
                    deps=[t_rm, t_cast] + (war if hh == 0 else []))
                t_subs.append(t_sub)
            for hh in range(2):
                ho = hh * HALF
                rb = rowb[:, 512 + ho:512 + ho + HALF].unsqueeze(1).broadcast_to(
                    [128, NCH, HALF])
                t_app = s.op("dve", lambda e, ho=ho, rb=rb: e.tensor_tensor(
                    out=hv[:, :, ho:ho + HALF], in0=hv[:, :, ho:ho + HALF],
                    in1=rb, op=ALU.mult), deps=[t_subs[hh], t_rstd])
                t_apps.append(t_app)
            bank_free["pstat0"] = [t_rm, t_msq]
            bank_free["pstat1"] = [t_var]
            t_xhat_free = [t_subs[-1]]
            t_x2b_free = [t_mm_q]
            t_rowb_free = [t_subs[-1], t_apps[-1]]
            t_rows_free = [t_var, t_ln, t_rstd]
            return t_apps

        def projection(t_h, t_w, wbuf, wbase_stride, nout, bias, t_bias, resid_in,
                       t_resid_in, resid_out, t_resid_out_war, operand=None,
                       ln_next_src=None, ln_next_tok=None, pe_mid=None):
            toks = []
            t_mm = None
            fr = ln_front_start() if ln_next_src is not None else None
            nb = len(proj_banks)
            opnd = hbuf if operand is None else operand
            for oc in range(nout):
                bname, bk = proj_banks[oc % nb]
                t_mm = s.op("pe", lambda e, oc=oc, bk=bk: [
                    e.matmul(out=bk[:],
                             lhsT=wbuf[:, kc * wbase_stride + oc * 128:
                                       kc * wbase_stride + oc * 128 + 128],
                             rhs=opnd[:, kc * SLAB:(kc + 1) * SLAB],
                             start=(kc == 0), stop=(kc == NCH - 1))
                    for kc in range(NCH)][-1],
                    deps=[t_h, t_w] + bank_free[bname])
                t_pc = None
                if fr is not None and ln_next_tok is None:
                    xw = list(t_xhat_free) if oc == 0 else []
                    t_pc = s.op("dve", lambda e, oc=oc, bk=bk: e.scalar_tensor_tensor(
                        out=xhat[:, oc * SLAB:(oc + 1) * SLAB],
                        in0=bk[:], scalar=bias[:, oc:oc + 1],
                        in1=resid_in[:, oc * SLAB:(oc + 1) * SLAB],
                        op0=ALU.add, op1=ALU.add),
                        deps=[t_mm, t_resid_in, t_bias] + xw)
                t_ev = s.op("dve", lambda e, oc=oc, bk=bk: e.scalar_tensor_tensor(
                    out=resid_out[:, oc * SLAB:(oc + 1) * SLAB],
                    in0=bk[:], scalar=bias[:, oc:oc + 1],
                    in1=resid_in[:, oc * SLAB:(oc + 1) * SLAB],
                    op0=ALU.add, op1=ALU.add),
                    deps=[t_mm, t_resid_in, t_bias] + t_resid_out_war)
                bank_free[bname] = [t_ev]
                toks.append(t_ev)
                if fr is not None:
                    ln_front_chunk(fr, oc,
                                   (ln_next_tok if ln_next_tok is not None else t_ev),
                                   ln_next_src, pre_cast=t_pc)
                if pe_mid is not None and oc == 1:
                    pe_mid(0)
                if pe_mid is not None and oc == 3:
                    pe_mid(1)
            if fr is not None:
                ln_front_finish(fr)
            return toks, t_mm, fr

        def qkv_and_attention(t_h, slab, resid_y, t_y, resid_z, z_war, bo_, t_ln3_src):
            nonlocal t_qkvg_free, t_pvgp_free, t_pc2_free
            nonlocal t_ebuf_free, t_vb1_free, t_ctxg_free, t_rcol_free, t_d4_free
            last_qkv_mm = [None]
            dbg_qkv_toks = []
            t_c2 = [None]
            c2_half = {}
            ev2 = [None] * NCH
            ev2_h1 = [None] * NCH
            pc3 = [[None] * NCH, [None] * NCH]
            fr3 = ln_front_start()
            nb = len(proj_banks)

            def wout_half(half, c2toks, with_ln):
                hoff = half * HALF
                for oc in range(NCH):
                    bname, bk = proj_banks[oc % nb]
                    t_mm = s.op("pe", lambda e, oc=oc, bk=bk, hoff=hoff: [
                        e.matmul(out=bk[:, 0:HALF],
                                 lhsT=woutb[:, j * 1024 + oc * 128:j * 1024 + oc * 128 + 128],
                                 rhs=c2sb[:, j * SLAB + hoff:j * SLAB + hoff + HALF],
                                 start=(j == 0), stop=(j == NCH - 1))
                        for j in range(NCH)][-1],
                        deps=c2toks + [t_woutb] + bank_free[bname])
                    if with_ln or half == 0:
                        xw = list(t_xhat_free) if (half == 0 and oc == 0) else []
                        t_pc = s.op("dve", lambda e, oc=oc, bk=bk, hoff=hoff:
                                    e.scalar_tensor_tensor(
                            out=xhat[:, oc * SLAB + hoff:oc * SLAB + hoff + HALF],
                            in0=bk[:, 0:HALF], scalar=bo_[:, oc:oc + 1],
                            in1=resid_y[:, oc * SLAB + hoff:oc * SLAB + hoff + HALF],
                            op0=ALU.add, op1=ALU.add),
                            deps=[t_mm, t_y, t_ldc] + xw)
                        pc3[half][oc] = t_pc
                    t_ev = s.op("dve", lambda e, oc=oc, bk=bk, hoff=hoff:
                                e.scalar_tensor_tensor(
                        out=resid_z[:, oc * SLAB + hoff:oc * SLAB + hoff + HALF],
                        in0=bk[:, 0:HALF], scalar=bo_[:, oc:oc + 1],
                        in1=resid_y[:, oc * SLAB + hoff:oc * SLAB + hoff + HALF],
                        op0=ALU.add, op1=ALU.add),
                        deps=[t_mm, t_y, t_ldc] + z_war)
                    bank_free[bname] = [t_ev]
                    if half == 0:
                        ev2_h1[oc] = t_ev
                    else:
                        ev2[oc] = t_ev
                        if with_ln:
                            ln_front_chunk(fr3, oc, None, t_ln3_src,
                                           pre_cast=[pc3[0][oc], pc3[1][oc]])
                if half == 1 and with_ln:
                    ln_front_finish(fr3)

            qkv_banks = proj_banks + [("pstat0", pstat[0]), ("pstat1", pstat[1])]
            for half in range(2):
                hoff = half * HALF
                ev_toks_q, ev_toks_kv = [], []
                for oc in range(24):
                    bname, bk = qkv_banks[oc % 6]
                    t_mm = s.op("pe", lambda e, oc=oc, bk=bk, hoff=hoff: [
                        e.matmul(out=bk[:, 0:HALF],
                                 lhsT=wqkvb[:, kc * 3072 + oc * 128:
                                            kc * 3072 + oc * 128 + 128],
                                 rhs=hbuf[:, kc * SLAB + hoff:kc * SLAB + hoff + HALF],
                                 start=(kc == 0), stop=(kc == NCH - 1))
                        for kc in range(NCH)][-1],
                        deps=[t_h[half] if isinstance(t_h, list) else t_h,
                              t_wqkvb] + bank_free[bname])
                    last_qkv_mm[0] = t_mm
                    kind = oc // 8
                    c = oc % 8
                    if kind == 0:
                        dst = qg
                        offs = (8 * c, 64 + 8 * c)
                    elif kind == 1:
                        dst = kg
                        offs = (16 * c, 16 * c + 8)
                    else:
                        dst = vg
                        offs = (16 * c, 16 * c + 8)
                    dstv = dst[:].rearrange("d (g c) -> d g c", c=128)
                    if kind == 0:
                        t_ev = s.op("act", lambda e, bk=bk, oc=oc, dstv=dstv, offs=offs: (
                            e.activation(out=dstv[:, :, offs[0]:offs[0] + 8],
                                         in_=bk[0:64, 0:HALF].rearrange(
                                             "d (g t) -> d g t", t=8),
                                         func=AF.Identity, bias=bq[0:64, oc:oc + 1],
                                         scale=1.0),
                            e.activation(out=dstv[:, :, offs[1]:offs[1] + 8],
                                         in_=bk[64:128, 0:HALF].rearrange(
                                             "d (g t) -> d g t", t=8),
                                         func=AF.Identity, bias=bq[64:128, oc:oc + 1],
                                         scale=1.0))[-1],
                            deps=[t_mm, t_bq] + t_qkvg_free, inc=2)
                        ev_toks_q.append(t_ev)
                    else:
                        sc = ksc[:, 0:1] if kind == 1 else 1.0
                        t_ev = s.op("act", lambda e, bk=bk, oc=oc, dstv=dstv,
                                    offs=offs, sc=sc: (
                            e.activation(out=dstv[:, :, offs[0]:offs[0] + 8],
                                         in_=bk[0:64, 0:HALF].rearrange(
                                             "d (g t) -> d g t", t=8),
                                         func=AF.Identity, bias=bq[0:64, oc:oc + 1],
                                         scale=(sc if isinstance(sc, float)
                                                else ksc[0:64, 0:1])),
                            e.activation(out=dstv[:, :, offs[1]:offs[1] + 8],
                                         in_=bk[64:128, 0:HALF].rearrange(
                                             "d (g t) -> d g t", t=8),
                                         func=AF.Identity, bias=bq[64:128, oc:oc + 1],
                                         scale=(sc if isinstance(sc, float)
                                                else ksc[64:128, 0:1])))[-1],
                            deps=[t_mm, t_bq, t_ksc] + t_qkvg_free, inc=2)
                        ev_toks_kv.append(t_ev)
                    bank_free[bname] = [t_ev]
                t_q_ready = ev_toks_q[-1]
                t_kv_ready = ev_toks_kv[-1]
                if DEBUG and slab == 0 and half == 0:
                    t_dq = dma(dbg_qg, qg[:], "dbg2", deps=[t_q_ready])
                    t_dk = dma(dbg_kg, kg[:], "dbg3", deps=[t_kv_ready])
                    t_dv = dma(dbg_vg, vg[:], "dbg4", deps=[t_kv_ready])
                    dbg_qkv_toks.extend([t_dq, t_dk, t_dv])

                if half == 1:
                    wout_half(0, c2_half[0], with_ln=False)

                sc_banks = [(pscore, "pscore"), (pstat[0], "pstat0")]
                ctx_banks = [(pctx, "pctx"), (pstat[1], "pstat1")]
                t_sc = {}
                t_exp = {}
                t_mask = {}
                t_vt = {}
                t_vev = {}
                t_ctx = {}
                t_rc = {}
                t_d4 = {}
                t_cge = {}
                t_ct = {}
                half_c2 = []
                NB = NBATCH_H
                for b in range(NB + 2):
                    par = b % 2
                    if b < NB:
                        g0 = b * 4
                        sb, sbname = sc_banks[par]
                        war_sc = list(bank_free[sbname])
                        t_sc[b] = s.op("pe", lambda e, g0=g0, sb=sb: [
                            e.matmul(out=sb[:, i * 128:(i + 1) * 128],
                                     lhsT=kg[:, (g0 + i) * 128:(g0 + i + 1) * 128],
                                     rhs=qg[:, (g0 + i) * 128:(g0 + i + 1) * 128],
                                     start=True, stop=True) for i in range(4)][-1],
                            deps=[t_q_ready, t_kv_ready] + war_sc, inc=4)
                        war_vt = [t_vev[b - 1]] if b >= 1 else t_pvgp_free
                        t_vt[b] = s.op("pe", lambda e, g0=g0: [
                            e.transpose(out=pvgp[:, i * 64:(i + 1) * 64],
                                        in_=vg[:, (g0 + i) * 128:(g0 + i + 1) * 128],
                                        identity=idb[0:64, :]) for i in range(4)][-1],
                            deps=[t_kv_ready, t_idb] + war_vt, inc=4)
                        war_e = [t_ctx[b - 2]] if b >= 2 else t_ebuf_free[par]
                        t_exp[b] = s.op("act", lambda e, par=par, sb=sb: e.activation(
                            out=ebuf[par][:], in_=sb[:], func=AF.Exp),
                            deps=[t_sc[b]] + war_e)
                        bank_free[sbname] = [t_exp[b]]
                        t_mask[b] = s.op("dve", lambda e, par=par: e.tensor_tensor(
                            out=ebuf[par][:].rearrange("p (g c) -> p g c", c=128),
                            in0=ebuf[par][:].rearrange("p (g c) -> p g c", c=128),
                            in1=maskb[:].unsqueeze(1).broadcast_to([128, 4, 128]),
                            op=ALU.mult), deps=[t_exp[b], t_maskb])
                        war_v = [t_ctx[b - 2]] if b >= 2 else t_vb1_free[par]
                        t_vev[b] = s.op("dve", lambda e, par=par: e.tensor_copy(
                            out=vb1[par][:].rearrange("p (g c) -> p g c", c=65)[:, :, 0:64],
                            in_=pvgp[:, 0:256].rearrange("p (g c) -> p g c", c=64)),
                            deps=[t_vt[b]] + war_v)
                    bb = b - 1
                    if 0 <= bb < NB:
                        par = bb % 2
                        cb, cbname = ctx_banks[par]
                        war_c = list(bank_free[cbname])
                        t_ctx[bb] = s.op("pe", lambda e, par=par, cb=cb: [
                            e.matmul(out=cb[:, i * 65:(i + 1) * 65],
                                     lhsT=ebuf[par][:, i * 128:(i + 1) * 128],
                                     rhs=vb1[par][:, i * 65:(i + 1) * 65],
                                     start=True, stop=True) for i in range(4)][-1],
                            deps=[t_mask[bb], t_vev[bb]] + war_c, inc=4)
                        t_rc[bb] = s.op("dve", lambda e, par=par, cb=cb: e.reciprocal(
                            out=rcol[par][:], in_=cb[:, 0:260].rearrange(
                                "p (g c) -> p g c", c=65)[:, :, 64]),
                            deps=[t_ctx[bb]] + ([t_d4[bb - 2]] if bb >= 2
                                                else t_rcol_free[par]))
                        t_d4[bb] = s.op("dve", lambda e, par=par: [
                            e.tensor_scalar(out=d4[par][:, i * 64:(i + 1) * 64],
                                            in0=idb[:], scalar1=rcol[par][:, i:i + 1],
                                            scalar2=None, op0=ALU.mult)
                            for i in range(4)][-1],
                            deps=[t_rc[bb], t_idb] + ([t_ct[bb - 2]] if bb >= 2
                                                       else t_d4_free[par]),
                            inc=4)
                        war_g = [t_ct[bb - 2]] if bb >= 2 else t_ctxg_free[par]
                        t_cge[bb] = s.op("act", lambda e, par=par, cb=cb: e.activation(
                            out=ctxg[par][:].rearrange("p (g c) -> p g c", c=64),
                            in_=cb[:, 0:260].rearrange("p (g c) -> p g c", c=65)[:, :, 0:64],
                            func=AF.Identity, bias=0.0, scale=1.0),
                            deps=[t_ctx[bb]] + war_g)
                        bank_free[cbname] = [t_rc[bb], t_cge[bb]]
                    bc = b - 2
                    if 0 <= bc < NB:
                        par = bc % 2
                        war_t = [t_c2[0]] if t_c2[0] is not None else t_pc2_free
                        t_ct[bc] = s.op("pe", lambda e, par=par: [
                            e.matmul(out=pc2[(0 if pp == 0 else 64):(64 if pp == 0 else 128),
                                             i * 64:(i + 1) * 64],
                                     lhsT=ctxg[par][(0 if pp == 0 else 64):(64 if pp == 0 else 128),
                                                    i * 64:(i + 1) * 64],
                                     rhs=d4[par][(0 if pp == 0 else 64):(64 if pp == 0 else 128),
                                                 i * 64:(i + 1) * 64],
                                     start=True, stop=True,
                                     tile_position=(None if pp == 0 else (64, 64)))
                            for i in range(4) for pp in range(2)][-1],
                            deps=[t_cge[bc], t_d4[bc]] + war_t, inc=8)
                        coff = hoff + bc * 32
                        t_c2[0] = s.op("dve", lambda e, coff=coff: e.tensor_copy(
                            out=c2sb[:].rearrange("p (j n) -> p j n", n=SLAB)[
                                :, :, coff:coff + 32].rearrange("p j (g t) -> p j g t", t=8),
                            in_=pc2[:, 0:256].rearrange("p (g j t) -> p j g t", g=4, j=8, t=8)),
                            deps=[t_ct[bc]] + t_c2_free)
                        half_c2.append(t_c2[0])
                t_pvgp_free = [t_vev[NB - 1]]
                t_pc2_free = [t_c2[0]]
                for p in range(2):
                    lastb = NB - 2 + p
                    t_ebuf_free[p] = [t_ctx[lastb]]
                    t_vb1_free[p] = [t_ctx[lastb]]
                    t_ctxg_free[p] = [t_ct[lastb]]
                    t_rcol_free[p] = [t_d4[lastb]]
                    t_d4_free[p] = [t_ct[lastb]]
                t_qkvg_free = [t_ct[NB - 1]] + dbg_qkv_toks
                dbg_qkv_toks = []
                c2_half[half] = [half_c2[-1]]
            wout_half(1, c2_half[1], with_ln=True)
            return ev2, fr3, last_qkv_mm[0], c2_half

        ev3_prev = []
        fr1_next = None
        t_h1_next = None
        for sl in range(NSLAB):
            A = rbuf[sl % 3]
            B = rbuf[(sl + 2) % 3]
            tA = t_x[sl]
            b_war = list(ev3_prev)
            if sl == 0:
                fr1 = ln_front(tA, A)
                t_h1 = ln_back(fr1, dst=c2sb, dst_war=t_c2_free)
            else:
                t_h1 = t_h1_next
            ev1, t_mm1, fr2 = projection(t_h1, t_weffb, weffb, 1024, NCH, b1, t_b1,
                                         A, tA, B, b_war, operand=c2sb,
                                         ln_next_src=B)
            t_y = ev1[-1]
            t_c2_free = [t_mm1]
            if DEBUG and sl == 0:
                t_d = dma(dbg_y, B[:], "dbg0", deps=ev1)
                ev1 = ev1 + [t_d]
            t_h2 = ln_back(fr2)
            if DEBUG and sl == 0:
                t_dh = dma(dbg_h2, hbuf[:], "dbg1", deps=[t_h2])
                t_hbuf_free_extra = [t_dh]
            else:
                t_hbuf_free_extra = []
            ev2, fr3, t_mmq, c2_half = qkv_and_attention(
                t_h2, sl, B, t_y, A, ev1, bo, A)
            t_hbuf_free = [t_mmq] + t_hbuf_free_extra
            t_z = ev2[-1]
            t_c2_free = list(ev2)
            if DEBUG and sl == 0:
                t_dc2 = dma(dbg_c2, c2sb[:], "dbg5", deps=c2_half[1])
                t_dz = dma(dbg_z, A[:], "dbg6", deps=ev2)
                t_c2_free = t_c2_free + [t_dc2]
                ev2 = ev2 + [t_dz]
            t_h3 = ln_back(fr3)
            if sl + 1 < NSLAB:
                pf_deps = [t_store[sl - 1]] if sl >= 1 else []
                t_x[sl + 1] = dma(rbuf[(sl + 1) % 3][:].rearrange(
                    "p (c n) -> p c n", n=SLAB),
                    xt_v[:, :, (sl + 1) * SLAB:(sl + 2) * SLAB],
                    "ldx%d" % ((sl + 1) % 2), deps=pf_deps)
            pe_mid = None
            if sl + 1 < NSLAB:
                nxt = rbuf[(sl + 1) % 3]
                fr1n = ln_front_start()
                for c in range(NCH):
                    ln_front_chunk(fr1n, c, t_x[sl + 1], nxt, sq=False)
                for c in range(NCH):
                    ln_front_square(fr1n, c)
                h1n_holder = []

                def pe_mid(phase, fr1n=fr1n):
                    if phase == 0:
                        ln_front_finish(fr1n, which="s")
                    else:
                        ln_front_finish(fr1n, which="q")
            ev3, t_mm3, _ = projection(t_h3, t_weffb, weffb, 1024, NCH, b1, t_b1,
                                       A, t_z, B, ev2, pe_mid=pe_mid)
            if sl + 1 < NSLAB:
                t_h1_next = ln_back(fr1n, dst=c2sb, dst_war=t_c2_free)
            t_hbuf_free = [t_mm3]
            ev3_prev = ev3
            t_store[sl] = dma(
                yt_v[:, :, sl * SLAB:(sl + 1) * SLAB],
                B[:].rearrange("p (c n) -> p c n", n=SLAB),
                "st%d" % (sl % 2), deps=ev3)

        fin_deps = [("st0", s.counts["st0"]), ("st1", s.counts["st1"])]
        if DEBUG:
            fin_deps += [(d, s.counts[d]) for d in DMASEMS if d.startswith("dbg")
                         and s.counts[d] > 0]
        s.op("sp", lambda e: None, deps=fin_deps, inc=0)

        @block.sync
        def _(sync):
            s.replay(sync, "sp", sem_handles)

        @block.vector
        def _(vector):
            s.replay(vector, "dve", sem_handles)

        @block.tensor
        def _(tensor):
            s.replay(tensor, "pe", sem_handles)

        @block.scalar
        def _(scalar):
            s.replay(scalar, "act", sem_handles)

        @block.gpsimd
        def _(gpsimd):
            s.replay(gpsimd, "pool", sem_handles)

    return nc


def _host_inputs(L0_attn_x, L0_attn_W_qkv, L0_attn_b_qkv, L0_attn_W_out, L0_attn_b_out,
                 L0_ffn_W, L0_ffn_b, L0_ffn_lora_down, L0_ffn_lora_up,
                 L0_ffn_down_aux, L0_ffn_up_aux):
    x = np.asarray(L0_attn_x, dtype=np.float32).reshape(NCORES * TCORE, D)
    kindv, hv, dv = np.meshgrid(np.arange(3), np.arange(16), np.arange(64),
                                indexing="ij")
    perm = (192 * hv + 64 * kindv + dv).reshape(-1)
    wqkv = np.ascontiguousarray(np.asarray(L0_attn_W_qkv, dtype=np.float32)[:, perm])
    w1t = np.ascontiguousarray(np.asarray(L0_ffn_W, dtype=np.float32).T)
    wout = np.ascontiguousarray(np.asarray(L0_attn_W_out, dtype=np.float32))
    upat = np.ascontiguousarray(np.asarray(L0_ffn_up_aux, dtype=np.float32).T)
    lup = np.ascontiguousarray(np.asarray(L0_ffn_lora_up, dtype=np.float32))
    ldn = np.ascontiguousarray(np.asarray(L0_ffn_lora_down, dtype=np.float32))
    dnaux = np.ascontiguousarray(np.asarray(L0_ffn_down_aux, dtype=np.float32))
    bqd = np.ascontiguousarray(
        np.asarray(L0_attn_b_qkv, dtype=np.float32)[perm].reshape(24, 128).T)
    b1d = np.ascontiguousarray(
        np.asarray(L0_ffn_b, dtype=np.float32).reshape(8, 128).T)
    bod = np.ascontiguousarray(
        np.asarray(L0_attn_b_out, dtype=np.float32).reshape(8, 128).T)
    mask = np.zeros((128, 128), dtype=np.float32)
    for g in range(16):
        for t in range(8):
            for p in range(2):
                for j in range(8):
                    mask[g * 8 + t, p * 64 + j * 8 + t] = 1.0
    ident = np.zeros((128, 64), dtype=np.float32)
    ident[0:64] = np.eye(64, dtype=np.float32)
    ident[64:128] = np.eye(64, dtype=np.float32)
    shared = dict(wqkv=wqkv, w1t=w1t, wout=wout, upat=upat, lup=lup, ldn=ldn,
                  dnaux=dnaux, bqd=bqd, b1d=b1d, bod=bod, maskd=mask, identd=ident)
    in_maps = []
    for c in range(NCORES):
        shard = x[c * TCORE:(c + 1) * TCORE]
        xt = np.ascontiguousarray(shard.T)
        m = dict(shared)
        m["xt"] = xt
        in_maps.append(m)
    return in_maps


def kernel(L0_attn_x, L0_attn_W_qkv, L0_attn_b_qkv, L0_attn_W_out, L0_attn_b_out,
           L0_ffn_W, L0_ffn_b, L0_ffn_lora_down, L0_ffn_lora_up,
           L0_ffn_down_aux, L0_ffn_up_aux):
    in_maps = _host_inputs(L0_attn_x, L0_attn_W_qkv, L0_attn_b_qkv, L0_attn_W_out,
                           L0_attn_b_out, L0_ffn_W, L0_ffn_b, L0_ffn_lora_down,
                           L0_ffn_lora_up, L0_ffn_down_aux, L0_ffn_up_aux)
    nc = build_nc()
    res = run_bass_kernel_spmd(nc, in_maps, core_ids=list(range(NCORES)))
    outs = []
    for c in range(NCORES):
        yt = res.results[c]["yt"]
        outs.append(np.ascontiguousarray(yt.T))
    full = np.concatenate(outs, axis=0)
    return full.reshape(np.asarray(L0_attn_x).shape).astype(np.float32)


# version 40
# speedup vs baseline: 1.0108x; 1.0108x over previous
import numpy as np
from contextlib import ExitStack

import concourse.bass as bass
import concourse.mybir as mybir
from concourse.bass_utils import run_bass_kernel_spmd

F32 = mybir.dt.float32
BF16 = mybir.dt.bfloat16
AF = mybir.ActivationFunctionType
ALU = mybir.AluOpType

NCORES = 8
TCORE = 2048
SLAB = 512
NSLAB = TCORE // SLAB
HALF = 256
NGRP_H = HALF // 8
NBATCH_H = NGRP_H // 4
D = 1024
NCH = 8
EPS = 1e-5

DEBUG = 0

ENGS = ("pe", "dve", "act", "pool", "sp")
DMASEMS = ("ldc", "ldw1", "ldq0", "ldq1", "ldwo", "ldx0", "ldx1", "st0", "st1",
           "dbg0", "dbg1", "dbg2", "dbg3", "dbg4", "dbg5", "dbg6", "dbg7", "dbg8")


class Sched:

    def __init__(self):
        self.q = {e: [] for e in ENGS}
        self.counts = {}
        for e in ENGS:
            self.counts[e] = 0
        for e in DMASEMS:
            self.counts[e] = 0

    def op(self, eng, fn, deps=(), sem=None, inc=1):
        sem = sem or eng
        dl = []
        for d in deps:
            if d is None:
                continue
            if isinstance(d, list):
                dl.extend(x for x in d if x is not None)
            else:
                dl.append(d)
        self.counts[sem] += inc
        tok = (sem, self.counts[sem])
        self.q[eng].append((dl, fn, sem, inc))
        return tok

    def replay(self, engobj, eng, sems):
        marks = {}
        for deps, fn, sem, inc in self.q[eng]:
            for (ps, pc) in deps:
                if marks.get(ps, 0) < pc:
                    engobj.wait_ge(sems[ps], pc)
                    marks[ps] = pc
            inst = fn(engobj)
            if inst is not None and inc:
                inst.then_inc(sems[sem], inc)


def build_nc():
    nc = bass.Bass("TRN2", target_bir_lowering=False)

    xt = nc.dram_tensor("xt", [D, TCORE], F32, kind="ExternalInput").ap()
    wqkv = nc.dram_tensor("wqkv", [D, 3 * D], F32, kind="ExternalInput").ap()
    w1t = nc.dram_tensor("w1t", [D, D], F32, kind="ExternalInput").ap()
    wout = nc.dram_tensor("wout", [D, D], F32, kind="ExternalInput").ap()
    upat = nc.dram_tensor("upat", [50, D], F32, kind="ExternalInput").ap()
    lup = nc.dram_tensor("lup", [50, 4], F32, kind="ExternalInput").ap()
    ldn = nc.dram_tensor("ldn", [4, 100], F32, kind="ExternalInput").ap()
    dnaux = nc.dram_tensor("dnaux", [100, D], F32, kind="ExternalInput").ap()
    bqd = nc.dram_tensor("bqd", [128, 24], F32, kind="ExternalInput").ap()
    b1d = nc.dram_tensor("b1d", [128, 8], F32, kind="ExternalInput").ap()
    bod = nc.dram_tensor("bod", [128, 8], F32, kind="ExternalInput").ap()
    maskd = nc.dram_tensor("maskd", [128, 128], F32, kind="ExternalInput").ap()
    identd = nc.dram_tensor("identd", [128, 64], F32, kind="ExternalInput").ap()
    yt = nc.dram_tensor("yt", [D, TCORE], F32, kind="ExternalOutput").ap()
    if DEBUG:
        dbg_y = nc.dram_tensor("dbg_y", [128, NCH * SLAB], F32, kind="ExternalOutput").ap()
        dbg_h2 = nc.dram_tensor("dbg_h2", [128, NCH * SLAB], BF16, kind="ExternalOutput").ap()
        dbg_qg = nc.dram_tensor("dbg_qg", [64, NGRP_H * 128], BF16, kind="ExternalOutput").ap()
        dbg_kg = nc.dram_tensor("dbg_kg", [64, NGRP_H * 128], BF16, kind="ExternalOutput").ap()
        dbg_vg = nc.dram_tensor("dbg_vg", [64, NGRP_H * 128], BF16, kind="ExternalOutput").ap()
        dbg_c2 = nc.dram_tensor("dbg_c2", [128, 8 * SLAB], BF16, kind="ExternalOutput").ap()
        dbg_z = nc.dram_tensor("dbg_z", [128, NCH * SLAB], F32, kind="ExternalOutput").ap()
        dbg_xh = nc.dram_tensor("dbg_xh", [128, NCH * SLAB], BF16, kind="ExternalOutput").ap()
        dbg_rows = nc.dram_tensor("dbg_rows", [128, 2048], F32, kind="ExternalOutput").ap()

    s = Sched()
    with ExitStack() as ctx:
        al = lambda *a, **k: ctx.enter_context(nc.sbuf_tensor(*a, **k))
        alp = lambda *a, **k: ctx.enter_context(nc.psum_tensor(*a, **k))
        stg = al("stg", [128, 3072], F32)
        rowb = al("rowb", [128, 1024], BF16)
        wqkvb = al("wqkvb", [128, 8 * 3072], BF16)
        weffb = al("weffb", [128, 8 * 1024], BF16)
        woutb = al("woutb", [128, 8 * 1024], BF16)
        rbuf = [al(f"rbuf{i}", [128, NCH * SLAB], F32) for i in range(3)]
        xhat = al("xhat", [128, NCH * SLAB], BF16)
        x2b = al("x2b", [128, NCH * SLAB], BF16)
        hbuf = al("hbuf", [128, NCH * SLAB], BF16)
        qg = al("qg", [64, NGRP_H * 128], BF16)
        kg = al("kg", [64, NGRP_H * 128], BF16)
        vg = al("vg", [64, NGRP_H * 128], BF16)
        ebuf = [al(f"ebuf{i}", [128, 512], BF16) for i in range(2)]
        vb1 = [al(f"vb1_{i}", [128, 4 * 65], BF16) for i in range(2)]
        ctxg = [al(f"ctxg{i}", [128, 256], BF16) for i in range(2)]
        c2sb = al("c2sb", [128, 8 * SLAB], BF16)
        bq = al("bq", [128, 24], F32)
        b1 = al("b1", [128, 8], F32)
        bo = al("bo", [128, 8], F32)
        maskf = al("maskf", [128, 128], F32)
        maskb = al("maskb", [128, 128], BF16)
        identf = al("identf", [128, 64], F32)
        idb = al("idb", [128, 64], BF16)
        ones128 = al("ones128", [128, 128], BF16)
        epsc = al("epsc", [128, 1], F32)
        ksc = al("ksc", [128, 1], F32)
        nhc = al("nhc", [128, 1], F32)
        ivdc = al("ivdc", [128, 1], F32)
        rcol = [al(f"rcol{i}", [128, 4], F32) for i in range(2)]
        d4 = [al(f"d4_{i}", [128, 256], BF16) for i in range(2)]
        lora_up = al("lora_up", [50, 4], F32)
        lora_dn = al("lora_dn", [4, 100], F32)
        pbank = [alp(f"pbank{i}", [128, 512], F32) for i in range(2)]
        pstat = [alp(f"pstat{i}", [128, 512], F32) for i in range(2)]
        pscore = alp("pscore", [128, 512], F32)
        pctx = alp("pctx", [128, 512], F32)
        pvgp = alp("pvgp", [128, 1024], BF16)
        pc2 = alp("pc2", [128, 512], F32)

        sem_handles = {}
        for name in ENGS + DMASEMS:
            sem_handles[name] = ctx.enter_context(nc.semaphore(name))
        block = ctx.enter_context(nc.Block())

        c2f32 = c2sb[:].bitcast(F32)
        dna = c2f32[0:100, 0:1024]
        upa = c2f32[0:64, 1024:2048]
        w1t_sl2 = xhat[:].bitcast(F32)[:, 1024:2048]
        rows_m = stg[:, 0:512]
        rows_rstd = stg[:, 512:1024]
        rows_t = stg[:, 1024:1536]
        rows_v = stg[:, 1536:2048]

        def dma(dst, src, sem, deps=()):
            return s.op("sp", lambda e, dst=dst, src=src: e.dma_start(out=dst, in_=src),
                        deps=deps, sem=sem, inc=16)

        t_ones = s.op("dve", lambda e: e.memset(ones128[:], 1.0))
        t_eps = s.op("dve", lambda e: e.memset(epsc[:], EPS))
        t_ksc = s.op("dve", lambda e: e.memset(ksc[:], 0.125))
        t_nh = s.op("dve", lambda e: e.memset(nhc[:], -0.5))
        t_ivd = s.op("dve", lambda e: e.memset(ivdc[:], 1.0 / 1024.0))
        t_v1 = [s.op("dve", lambda e, i=i: e.memset(
            vb1[i][:].rearrange("p (g c) -> p g c", c=65)[:, :, 64:65], 1.0)) for i in range(2)]
        d_lora = [dma(dst, srcd, fam) for dst, srcd, fam in
                  ((lora_up[:], lup, "dbg0"), (lora_dn[:], ldn, "dbg1"),
                   (dna[0:100, :], dnaux, "dbg2"), (upa[0:50, :], upat, "dbg3"))]
        for dst, srcd in ((bq[:], bqd), (b1[:], b1d), (bo[:], bod),
                          (maskf[:], maskd), (identf[:], identd)):
            t_ldc = dma(dst, srcd, "ldc")
        d_b = [t_ldc] * 3
        t_maskb = s.op("dve", lambda e: e.tensor_copy(out=maskb[:], in_=maskf[:]),
                       deps=[t_ldc])
        t_idb = s.op("dve", lambda e: e.tensor_copy(out=idb[:], in_=identf[:]),
                     deps=[t_ldc])
        t_bq = s.op("dve", lambda e: e.tensor_scalar(
            out=bq[:, 8:16], in0=bq[:, 8:16], scalar1=0.125, scalar2=None,
            op0=ALU.mult), deps=[t_ldc])
        t_b1 = s.op("dve", lambda e: e.tensor_scalar(
            out=b1[:], in0=b1[:], scalar1=0.5, scalar2=None, op0=ALU.mult),
            deps=[t_ldc])

        xt_v = xt.rearrange("(c p) n -> p c n", p=128)
        yt_v = yt.rearrange("(c p) n -> p c n", p=128)
        t_x = [None] * NSLAB
        t_x[0] = dma(rbuf[0][:].rearrange("p (c n) -> p c n", n=SLAB),
                     xt_v[:, :, 0:SLAB], "ldx0")

        A_sl = stg[0:4, 0:1024]
        B_sl = stg[0:100, 1024:2048]
        w1t_sl = stg[:, 2048:3072]

        t_A_mm = s.op("pe", lambda e: (
            e.matmul(out=pbank[0][0:4, :], lhsT=lora_up[:], rhs=upa[0:50, 0:512],
                     start=True, stop=True),
            e.matmul(out=pbank[1][0:4, :], lhsT=lora_up[:], rhs=upa[0:50, 512:1024],
                     start=True, stop=True))[-1], deps=d_lora[:2] + [d_lora[3]])
        t_A = s.op("dve", lambda e: (
            e.tensor_copy(out=A_sl[:, 0:512], in_=pbank[0][0:4, :]),
            e.tensor_copy(out=A_sl[:, 512:1024], in_=pbank[1][0:4, :]))[-1],
            deps=[t_A_mm])
        t_B_mm = s.op("pe", lambda e: (
            e.matmul(out=pbank[0][0:100, :], lhsT=lora_dn[:], rhs=A_sl[:, 0:512],
                     start=True, stop=True),
            e.matmul(out=pbank[1][0:100, :], lhsT=lora_dn[:], rhs=A_sl[:, 512:1024],
                     start=True, stop=True))[-1], deps=[t_A, d_lora[1]])
        t_B = s.op("dve", lambda e: (
            e.tensor_copy(out=B_sl[:, 0:512], in_=pbank[0][0:100, :]),
            e.tensor_copy(out=B_sl[:, 512:1024], in_=pbank[1][0:100, :]))[-1],
            deps=[t_B_mm])
        t_weff_parts = []
        for kc in range(NCH):
            slw = w1t_sl if kc % 2 == 0 else w1t_sl2
            war = [t_weff_parts[kc - 2]] if kc >= 2 else []
            d_w1 = dma(slw, w1t.rearrange("(c p) m -> c p m", p=128)[kc],
                       "ldw1" if kc % 2 == 0 else "ldq1", deps=war)
            t_w1h = s.op("dve", lambda e, slw=slw: e.tensor_scalar(
                out=slw, in0=slw, scalar1=0.5, scalar2=None, op0=ALU.mult),
                deps=[d_w1])
            cbk0 = [pbank[0], pbank[1], pscore, pctx][(2 * kc) % 4]
            cbk1 = [pbank[0], pbank[1], pscore, pctx][(2 * kc + 1) % 4]
            t_c = s.op("pe", lambda e, kc=kc, cbk0=cbk0, cbk1=cbk1: (
                e.matmul(out=cbk0[:], lhsT=dna[0:100, kc * 128:(kc + 1) * 128],
                         rhs=B_sl[:, 0:512], start=True, stop=True),
                e.matmul(out=cbk1[:], lhsT=dna[0:100, kc * 128:(kc + 1) * 128],
                         rhs=B_sl[:, 512:1024], start=True, stop=True))[-1],
                deps=[t_B, d_lora[2]] + ([t_weff_parts[kc - 2]] if kc >= 2 else []))
            t_we = s.op("dve", lambda e, kc=kc, slw=slw, cbk0=cbk0, cbk1=cbk1: (
                e.scalar_tensor_tensor(
                    out=weffb[:, kc * 1024:kc * 1024 + 512], in0=cbk0[:],
                    scalar=0.5, in1=slw[:, 0:512], op0=ALU.mult, op1=ALU.add),
                e.scalar_tensor_tensor(
                    out=weffb[:, kc * 1024 + 512:(kc + 1) * 1024], in0=cbk1[:],
                    scalar=0.5, in1=slw[:, 512:1024], op0=ALU.mult, op1=ALU.add))[-1],
                deps=[t_c, t_w1h])
            t_weff_parts.append(t_we)
        t_weffb = t_weff_parts[-1]

        wqkv_v = wqkv.rearrange("(c p) m -> c p m", p=128)
        wout_v = wout.rearrange("(c p) m -> c p m", p=128)
        for kc in range(NCH):
            t_wqkvb = s.op("pool", lambda e, kc=kc: e.dma_start(
                out=wqkvb[:, kc * 3072:(kc + 1) * 3072], in_=wqkv_v[kc]),
                deps=[t_ldc] if kc == 0 else [], sem="ldq0", inc=16)
        for kc in range(NCH):
            t_woutb = s.op("pool", lambda e, kc=kc: e.dma_start(
                out=woutb[:, kc * 1024:(kc + 1) * 1024], in_=wout_v[kc]),
                deps=[t_weffb] if kc == 0 else [], sem="ldwo", inc=16)

        bank_free = {"pscore": [t_woutb], "pstat0": [t_B_mm], "pstat1": [t_B_mm],
                     "pctx": [t_woutb], "pbank0": [t_woutb], "pbank1": [t_woutb]}
        proj_banks = [("pbank0", pbank[0]), ("pbank1", pbank[1]),
                      ("pscore", pscore), ("pctx", pctx)]
        t_xhat_free = [t_ones]
        t_hbuf_free = [t_ones]
        t_rowb_free = [t_ones]
        t_rows_free = [t_woutb]
        t_qkvg_free = [t_ones]
        t_ebuf_free = [[t_ones], [t_ones]]
        t_vb1_free = [[t_v1[0]], [t_v1[1]]]
        t_ctxg_free = [[t_ones], [t_ones]]
        t_rcol_free = [[t_ones], [t_ones]]
        t_d4_free = [[t_ones], [t_ones]]
        t_pvgp_free = [t_ones]
        t_pc2_free = [t_ones]
        t_c2_free = [t_weffb]
        t_store = [None] * NSLAB

        ln_calls = [0]
        t_x2b_free = [t_ones]

        def ln_front_start():
            return dict(cast=[], x2=[], mm_s=None, mm_q=None)

        def ln_front_square(st, c):
            nonlocal t_x2b_free
            t_c = st["cast"][c]
            t_q = s.op("dve", lambda e, c=c: e.tensor_tensor(
                out=x2b[:, c * SLAB:(c + 1) * SLAB],
                in0=xhat[:, c * SLAB:(c + 1) * SLAB],
                in1=xhat[:, c * SLAB:(c + 1) * SLAB], op=ALU.mult),
                deps=([t_c] if not isinstance(t_c, list) else t_c)
                     + (list(t_x2b_free) if c == 0 else []))
            st["x2"].append(t_q)

        def ln_front_chunk(st, c, tok, src, pre_cast=None, sq=True):
            nonlocal t_xhat_free
            if pre_cast is not None:
                t_c = pre_cast
            else:
                extra = list(t_xhat_free) if c == 0 else []
                t_c = s.op("dve", lambda e, c=c: e.tensor_copy(
                    out=xhat[:, c * SLAB:(c + 1) * SLAB],
                    in_=src[:, c * SLAB:(c + 1) * SLAB]),
                    deps=[tok] + extra)
            st["cast"].append(t_c)
            if sq:
                ln_front_square(st, c)

        def ln_front_finish(st, which="sq"):
            for c in range(NCH):
                cst = st["cast"][c]
                dep_s = (cst if isinstance(cst, list) else [cst]) + [t_ones]
                if c == 0:
                    dep_s += bank_free["pstat0"]
                if "s" in which:
                    st["mm_s"] = s.op("pe", lambda e, c=c: e.matmul(
                        out=pstat[0][:], lhsT=ones128[:],
                        rhs=xhat[:, c * SLAB:(c + 1) * SLAB],
                        start=(c == 0), stop=(c == NCH - 1), skip_group_check=True),
                        deps=dep_s, inc=(1 if c == NCH - 1 else 0))
            for c in range(NCH):
                if "q" not in which:
                    break
                dep_q = [st["x2"][c]]
                if c == 0:
                    dep_q += bank_free["pstat1"]
                if True:
                    st["mm_q"] = s.op("pe", lambda e, c=c: e.matmul(
                        out=pstat[1][:], lhsT=ones128[:],
                        rhs=x2b[:, c * SLAB:(c + 1) * SLAB],
                        start=(c == 0), stop=(c == NCH - 1), skip_group_check=True),
                        deps=dep_q, inc=(1 if c == NCH - 1 else 0))
            return st

        def ln_front(chunk_toks, src):
            if not isinstance(chunk_toks, list):
                chunk_toks = [chunk_toks] * NCH
            st = ln_front_start()
            for c in range(NCH):
                ln_front_chunk(st, c, chunk_toks[c], src)
            ln_front_finish(st)
            return st

        def ln_back(fr, dst=None, dst_war=None):
            nonlocal t_xhat_free, t_rowb_free, t_rows_free, t_x2b_free
            ln_calls[0] += 1
            t_mm_s, t_mm_q = fr["mm_s"], fr["mm_q"]
            t_cast = fr["cast"][-1]
            if isinstance(t_cast, list):
                t_cast = t_cast[-1]
            t_rm = s.op("dve", lambda e: e.tensor_scalar(
                out=rowb[:, 0:512], in0=pstat[0][:], scalar1=1.0 / 1024.0,
                scalar2=None, op0=ALU.mult), deps=[t_mm_s] + t_rowb_free)
            t_msq = s.op("dve", lambda e: e.tensor_tensor(
                out=rows_t, in0=rowb[:, 0:512], in1=rowb[:, 0:512],
                op=ALU.mult), deps=[t_rm] + t_rows_free)
            t_var = s.op("dve", lambda e: e.scalar_tensor_tensor(
                out=rows_v, in0=pstat[1][:], scalar=ivdc[:, 0:1], in1=rows_t,
                op0=ALU.mult, op1=ALU.subtract),
                deps=[t_mm_q, t_msq] + t_rows_free)
            t_ln = s.op("act", lambda e: e.activation(
                out=rows_m, in_=rows_v, func=AF.Ln, bias=epsc[:, 0:1], scale=1.0),
                deps=[t_var, t_eps] + t_rows_free)
            t_rstd = s.op("act", lambda e: e.activation(
                out=rowb[:, 512:1024], in_=rows_m, func=AF.Exp, bias=0.0,
                scale=nhc[:, 0:1]), deps=[t_ln, t_nh] + t_rowb_free)
            xv = xhat[:].rearrange("p (c n) -> p c n", n=SLAB)
            dstb = hbuf if dst is None else dst
            war = list(t_hbuf_free) if dst_war is None else list(dst_war)
            hv = dstb[:].rearrange("p (c n) -> p c n", n=SLAB)
            t_apps = []
            t_subs = []
            for hh in range(2):
                ho = hh * HALF
                mb = rowb[:, ho:ho + HALF].unsqueeze(1).broadcast_to([128, NCH, HALF])
                t_sub = s.op("dve", lambda e, ho=ho, mb=mb: e.tensor_tensor(
                    out=hv[:, :, ho:ho + HALF], in0=xv[:, :, ho:ho + HALF],
                    in1=mb, op=ALU.subtract),
                    deps=[t_rm, t_cast] + (war if hh == 0 else []))
                t_subs.append(t_sub)
            for hh in range(2):
                ho = hh * HALF
                rb = rowb[:, 512 + ho:512 + ho + HALF].unsqueeze(1).broadcast_to(
                    [128, NCH, HALF])
                t_app = s.op("dve", lambda e, ho=ho, rb=rb: e.tensor_tensor(
                    out=hv[:, :, ho:ho + HALF], in0=hv[:, :, ho:ho + HALF],
                    in1=rb, op=ALU.mult), deps=[t_subs[hh], t_rstd])
                t_apps.append(t_app)
            bank_free["pstat0"] = [t_rm, t_msq]
            bank_free["pstat1"] = [t_var]
            t_xhat_free = [t_subs[-1]]
            t_x2b_free = [t_mm_q]
            t_rowb_free = [t_subs[-1], t_apps[-1]]
            t_rows_free = [t_var, t_ln, t_rstd]
            return t_apps

        def projection(t_h, t_w, wbuf, wbase_stride, nout, bias, t_bias, resid_in,
                       t_resid_in, resid_out, t_resid_out_war, operand=None,
                       ln_next_src=None, ln_next_tok=None, pe_mid=None):
            toks = []
            t_mm = None
            fr = ln_front_start() if ln_next_src is not None else None
            nb = len(proj_banks)
            opnd = hbuf if operand is None else operand
            for oc in range(nout):
                bname, bk = proj_banks[oc % nb]
                t_mm = s.op("pe", lambda e, oc=oc, bk=bk: [
                    e.matmul(out=bk[:],
                             lhsT=wbuf[:, kc * wbase_stride + oc * 128:
                                       kc * wbase_stride + oc * 128 + 128],
                             rhs=opnd[:, kc * SLAB:(kc + 1) * SLAB],
                             start=(kc == 0), stop=(kc == NCH - 1))
                    for kc in range(NCH)][-1],
                    deps=[t_h, t_w] + bank_free[bname])
                t_pc = None
                if fr is not None and ln_next_tok is None:
                    xw = list(t_xhat_free) if oc == 0 else []
                    t_pc = s.op("dve", lambda e, oc=oc, bk=bk: e.scalar_tensor_tensor(
                        out=xhat[:, oc * SLAB:(oc + 1) * SLAB],
                        in0=bk[:], scalar=bias[:, oc:oc + 1],
                        in1=resid_in[:, oc * SLAB:(oc + 1) * SLAB],
                        op0=ALU.add, op1=ALU.add),
                        deps=[t_mm, t_resid_in, t_bias] + xw)
                t_ev = s.op("dve", lambda e, oc=oc, bk=bk: e.scalar_tensor_tensor(
                    out=resid_out[:, oc * SLAB:(oc + 1) * SLAB],
                    in0=bk[:], scalar=bias[:, oc:oc + 1],
                    in1=resid_in[:, oc * SLAB:(oc + 1) * SLAB],
                    op0=ALU.add, op1=ALU.add),
                    deps=[t_mm, t_resid_in, t_bias] + t_resid_out_war)
                bank_free[bname] = [t_ev]
                toks.append(t_ev)
                if fr is not None:
                    ln_front_chunk(fr, oc,
                                   (ln_next_tok if ln_next_tok is not None else t_ev),
                                   ln_next_src, pre_cast=t_pc)
                if pe_mid is not None and oc == 1:
                    pe_mid(0)
                if pe_mid is not None and oc == 3:
                    pe_mid(1)
            if fr is not None:
                ln_front_finish(fr)
            return toks, t_mm, fr

        def qkv_and_attention(t_h, slab, resid_y, t_y, resid_z, z_war, bo_, t_ln3_src):
            nonlocal t_qkvg_free, t_pvgp_free, t_pc2_free
            nonlocal t_ebuf_free, t_vb1_free, t_ctxg_free, t_rcol_free, t_d4_free
            last_qkv_mm = [None]
            dbg_qkv_toks = []
            t_c2 = [None]
            c2_half = {}
            ev2 = [None] * NCH
            ev2_h1 = [None] * NCH
            pc3 = [[None] * NCH, [None] * NCH]
            fr3 = ln_front_start()
            nb = len(proj_banks)

            def wout_half(half, c2toks, with_ln):
                hoff = half * HALF
                for oc in range(NCH):
                    bname, bk = proj_banks[oc % nb]
                    t_mm = s.op("pe", lambda e, oc=oc, bk=bk, hoff=hoff: [
                        e.matmul(out=bk[:, 0:HALF],
                                 lhsT=woutb[:, j * 1024 + oc * 128:j * 1024 + oc * 128 + 128],
                                 rhs=c2sb[:, j * SLAB + hoff:j * SLAB + hoff + HALF],
                                 start=(j == 0), stop=(j == NCH - 1))
                        for j in range(NCH)][-1],
                        deps=c2toks + [t_woutb] + bank_free[bname])
                    if with_ln or half == 0:
                        xw = list(t_xhat_free) if (half == 0 and oc == 0) else []
                        t_pc = s.op("dve", lambda e, oc=oc, bk=bk, hoff=hoff:
                                    e.scalar_tensor_tensor(
                            out=xhat[:, oc * SLAB + hoff:oc * SLAB + hoff + HALF],
                            in0=bk[:, 0:HALF], scalar=bo_[:, oc:oc + 1],
                            in1=resid_y[:, oc * SLAB + hoff:oc * SLAB + hoff + HALF],
                            op0=ALU.add, op1=ALU.add),
                            deps=[t_mm, t_y, t_ldc] + xw)
                        pc3[half][oc] = t_pc
                    t_ev = s.op("dve", lambda e, oc=oc, bk=bk, hoff=hoff:
                                e.scalar_tensor_tensor(
                        out=resid_z[:, oc * SLAB + hoff:oc * SLAB + hoff + HALF],
                        in0=bk[:, 0:HALF], scalar=bo_[:, oc:oc + 1],
                        in1=resid_y[:, oc * SLAB + hoff:oc * SLAB + hoff + HALF],
                        op0=ALU.add, op1=ALU.add),
                        deps=[t_mm, t_y, t_ldc] + z_war)
                    bank_free[bname] = [t_ev]
                    if half == 0:
                        ev2_h1[oc] = t_ev
                    else:
                        ev2[oc] = t_ev
                        if with_ln:
                            ln_front_chunk(fr3, oc, None, t_ln3_src,
                                           pre_cast=[pc3[0][oc], pc3[1][oc]])
                if half == 1 and with_ln:
                    ln_front_finish(fr3)

            qkv_banks = proj_banks + [("pstat0", pstat[0]), ("pstat1", pstat[1])]
            for half in range(2):
                hoff = half * HALF
                ev_toks_q, ev_toks_kv = [], []
                for oc in range(24):
                    bname, bk = qkv_banks[oc % 6]
                    t_mm = s.op("pe", lambda e, oc=oc, bk=bk, hoff=hoff: [
                        e.matmul(out=bk[:, 0:HALF],
                                 lhsT=wqkvb[:, kc * 3072 + oc * 128:
                                            kc * 3072 + oc * 128 + 128],
                                 rhs=hbuf[:, kc * SLAB + hoff:kc * SLAB + hoff + HALF],
                                 start=(kc == 0), stop=(kc == NCH - 1))
                        for kc in range(NCH)][-1],
                        deps=[t_h[half] if isinstance(t_h, list) else t_h,
                              t_wqkvb] + bank_free[bname])
                    last_qkv_mm[0] = t_mm
                    kind = oc // 8
                    c = oc % 8
                    if kind == 0:
                        dst = qg
                        offs = (8 * c, 64 + 8 * c)
                    elif kind == 1:
                        dst = kg
                        offs = (16 * c, 16 * c + 8)
                    else:
                        dst = vg
                        offs = (16 * c, 16 * c + 8)
                    dstv = dst[:].rearrange("d (g c) -> d g c", c=128)
                    if kind == 0:
                        t_ev = s.op("act", lambda e, bk=bk, oc=oc, dstv=dstv, offs=offs: (
                            e.activation(out=dstv[:, :, offs[0]:offs[0] + 8],
                                         in_=bk[0:64, 0:HALF].rearrange(
                                             "d (g t) -> d g t", t=8),
                                         func=AF.Identity, bias=bq[0:64, oc:oc + 1],
                                         scale=1.0),
                            e.activation(out=dstv[:, :, offs[1]:offs[1] + 8],
                                         in_=bk[64:128, 0:HALF].rearrange(
                                             "d (g t) -> d g t", t=8),
                                         func=AF.Identity, bias=bq[64:128, oc:oc + 1],
                                         scale=1.0))[-1],
                            deps=[t_mm, t_bq] + t_qkvg_free, inc=2)
                        ev_toks_q.append(t_ev)
                    else:
                        sc = ksc[:, 0:1] if kind == 1 else 1.0
                        t_ev = s.op("act", lambda e, bk=bk, oc=oc, dstv=dstv,
                                    offs=offs, sc=sc: (
                            e.activation(out=dstv[:, :, offs[0]:offs[0] + 8],
                                         in_=bk[0:64, 0:HALF].rearrange(
                                             "d (g t) -> d g t", t=8),
                                         func=AF.Identity, bias=bq[0:64, oc:oc + 1],
                                         scale=(sc if isinstance(sc, float)
                                                else ksc[0:64, 0:1])),
                            e.activation(out=dstv[:, :, offs[1]:offs[1] + 8],
                                         in_=bk[64:128, 0:HALF].rearrange(
                                             "d (g t) -> d g t", t=8),
                                         func=AF.Identity, bias=bq[64:128, oc:oc + 1],
                                         scale=(sc if isinstance(sc, float)
                                                else ksc[64:128, 0:1])))[-1],
                            deps=[t_mm, t_bq, t_ksc] + t_qkvg_free, inc=2)
                        ev_toks_kv.append(t_ev)
                    bank_free[bname] = [t_ev]
                t_q_ready = ev_toks_q[-1]
                t_kv_ready = ev_toks_kv[-1]
                if DEBUG and slab == 0 and half == 0:
                    t_dq = dma(dbg_qg, qg[:], "dbg2", deps=[t_q_ready])
                    t_dk = dma(dbg_kg, kg[:], "dbg3", deps=[t_kv_ready])
                    t_dv = dma(dbg_vg, vg[:], "dbg4", deps=[t_kv_ready])
                    dbg_qkv_toks.extend([t_dq, t_dk, t_dv])

                if half == 1:
                    wout_half(0, c2_half[0], with_ln=False)

                sc_banks = [(pscore, "pscore"), (pstat[0], "pstat0")]
                ctx_banks = [(pctx, "pctx"), (pstat[1], "pstat1")]
                t_sc = {}
                t_exp = {}
                t_mask = {}
                t_vt = {}
                t_vev = {}
                t_ctx = {}
                t_rc = {}
                t_d4 = {}
                t_cge = {}
                t_ct = {}
                half_c2 = []
                NB = NBATCH_H
                for b in range(NB + 2):
                    par = b % 2
                    if b < NB:
                        g0 = b * 4
                        sb, sbname = sc_banks[par]
                        war_sc = list(bank_free[sbname])
                        t_sc[b] = s.op("pe", lambda e, g0=g0, sb=sb: [
                            e.matmul(out=sb[:, i * 128:(i + 1) * 128],
                                     lhsT=kg[:, (g0 + i) * 128:(g0 + i + 1) * 128],
                                     rhs=qg[:, (g0 + i) * 128:(g0 + i + 1) * 128],
                                     start=True, stop=True) for i in range(4)][-1],
                            deps=[t_q_ready, t_kv_ready] + war_sc, inc=4)
                        war_vt = [t_vev[b - 1]] if b >= 1 else t_pvgp_free
                        t_vt[b] = s.op("pe", lambda e, g0=g0: [
                            e.transpose(out=pvgp[:, i * 64:(i + 1) * 64],
                                        in_=vg[:, (g0 + i) * 128:(g0 + i + 1) * 128],
                                        identity=idb[0:64, :]) for i in range(4)][-1],
                            deps=[t_kv_ready, t_idb] + war_vt, inc=4)
                        war_e = [t_ctx[b - 2]] if b >= 2 else t_ebuf_free[par]
                        t_exp[b] = s.op("act", lambda e, par=par, sb=sb: e.activation(
                            out=ebuf[par][:], in_=sb[:], func=AF.Exp),
                            deps=[t_sc[b]] + war_e)
                        bank_free[sbname] = [t_exp[b]]
                        t_mask[b] = s.op("dve", lambda e, par=par: e.tensor_tensor(
                            out=ebuf[par][:].rearrange("p (g c) -> p g c", c=128),
                            in0=ebuf[par][:].rearrange("p (g c) -> p g c", c=128),
                            in1=maskb[:].unsqueeze(1).broadcast_to([128, 4, 128]),
                            op=ALU.mult), deps=[t_exp[b], t_maskb])
                        war_v = [t_ctx[b - 2]] if b >= 2 else t_vb1_free[par]
                        t_vev[b] = s.op("dve", lambda e, par=par: e.tensor_copy(
                            out=vb1[par][:].rearrange("p (g c) -> p g c", c=65)[:, :, 0:64],
                            in_=pvgp[:, 0:256].rearrange("p (g c) -> p g c", c=64)),
                            deps=[t_vt[b]] + war_v)
                    bb = b - 1
                    if 0 <= bb < NB:
                        par = bb % 2
                        cb, cbname = ctx_banks[par]
                        war_c = list(bank_free[cbname])
                        t_ctx[bb] = s.op("pe", lambda e, par=par, cb=cb: [
                            e.matmul(out=cb[:, i * 65:(i + 1) * 65],
                                     lhsT=ebuf[par][:, i * 128:(i + 1) * 128],
                                     rhs=vb1[par][:, i * 65:(i + 1) * 65],
                                     start=True, stop=True) for i in range(4)][-1],
                            deps=[t_mask[bb], t_vev[bb]] + war_c, inc=4)
                        t_rc[bb] = s.op("dve", lambda e, par=par, cb=cb: e.reciprocal(
                            out=rcol[par][:], in_=cb[:, 0:260].rearrange(
                                "p (g c) -> p g c", c=65)[:, :, 64]),
                            deps=[t_ctx[bb]] + ([t_d4[bb - 2]] if bb >= 2
                                                else t_rcol_free[par]))
                        t_d4[bb] = s.op("dve", lambda e, par=par: [
                            e.tensor_scalar(out=d4[par][:, i * 64:(i + 1) * 64],
                                            in0=idb[:], scalar1=rcol[par][:, i:i + 1],
                                            scalar2=None, op0=ALU.mult)
                            for i in range(4)][-1],
                            deps=[t_rc[bb], t_idb] + ([t_ct[bb - 2]] if bb >= 2
                                                       else t_d4_free[par]),
                            inc=4)
                        war_g = [t_ct[bb - 2]] if bb >= 2 else t_ctxg_free[par]
                        t_cge[bb] = s.op("act", lambda e, par=par, cb=cb: e.activation(
                            out=ctxg[par][:].rearrange("p (g c) -> p g c", c=64),
                            in_=cb[:, 0:260].rearrange("p (g c) -> p g c", c=65)[:, :, 0:64],
                            func=AF.Identity, bias=0.0, scale=1.0),
                            deps=[t_ctx[bb]] + war_g)
                        bank_free[cbname] = [t_rc[bb], t_cge[bb]]
                    bc = b - 2
                    if 0 <= bc < NB:
                        par = bc % 2
                        war_t = [t_c2[0]] if t_c2[0] is not None else t_pc2_free
                        t_ct[bc] = s.op("pe", lambda e, par=par: [
                            e.matmul(out=pc2[(0 if pp == 0 else 64):(64 if pp == 0 else 128),
                                             i * 64:(i + 1) * 64],
                                     lhsT=ctxg[par][(0 if pp == 0 else 64):(64 if pp == 0 else 128),
                                                    i * 64:(i + 1) * 64],
                                     rhs=d4[par][(0 if pp == 0 else 64):(64 if pp == 0 else 128),
                                                 i * 64:(i + 1) * 64],
                                     start=True, stop=True,
                                     tile_position=(None if pp == 0 else (64, 64)))
                            for i in range(4) for pp in range(2)][-1],
                            deps=[t_cge[bc], t_d4[bc]] + war_t, inc=8)
                        coff = hoff + bc * 32
                        t_c2[0] = s.op("dve", lambda e, coff=coff: e.tensor_copy(
                            out=c2sb[:].rearrange("p (j n) -> p j n", n=SLAB)[
                                :, :, coff:coff + 32].rearrange("p j (g t) -> p j g t", t=8),
                            in_=pc2[:, 0:256].rearrange("p (g j t) -> p j g t", g=4, j=8, t=8)),
                            deps=[t_ct[bc]] + t_c2_free)
                        half_c2.append(t_c2[0])
                t_pvgp_free = [t_vev[NB - 1]]
                t_pc2_free = [t_c2[0]]
                for p in range(2):
                    lastb = NB - 2 + p
                    t_ebuf_free[p] = [t_ctx[lastb]]
                    t_vb1_free[p] = [t_ctx[lastb]]
                    t_ctxg_free[p] = [t_ct[lastb]]
                    t_rcol_free[p] = [t_d4[lastb]]
                    t_d4_free[p] = [t_ct[lastb]]
                t_qkvg_free = [t_ct[NB - 1]] + dbg_qkv_toks
                dbg_qkv_toks = []
                c2_half[half] = [half_c2[-1]]
            wout_half(1, c2_half[1], with_ln=True)
            return ev2, fr3, last_qkv_mm[0], c2_half

        ev3_prev = []
        fr1_next = None
        t_h1_next = None
        for sl in range(NSLAB):
            A = rbuf[sl % 3]
            B = rbuf[(sl + 2) % 3]
            tA = t_x[sl]
            b_war = list(ev3_prev)
            if sl == 0:
                fr1 = ln_front(tA, A)
                t_h1 = ln_back(fr1, dst=c2sb, dst_war=t_c2_free)
            else:
                t_h1 = t_h1_next
            ev1, t_mm1, fr2 = projection(t_h1, t_weffb, weffb, 1024, NCH, b1, t_b1,
                                         A, tA, B, b_war, operand=c2sb,
                                         ln_next_src=B)
            t_y = ev1[-1]
            t_c2_free = [t_mm1]
            if DEBUG and sl == 0:
                t_d = dma(dbg_y, B[:], "dbg0", deps=ev1)
                ev1 = ev1 + [t_d]
            t_h2 = ln_back(fr2)
            if DEBUG and sl == 0:
                t_dh = dma(dbg_h2, hbuf[:], "dbg1", deps=[t_h2])
                t_hbuf_free_extra = [t_dh]
            else:
                t_hbuf_free_extra = []
            ev2, fr3, t_mmq, c2_half = qkv_and_attention(
                t_h2, sl, B, t_y, A, ev1, bo, A)
            t_hbuf_free = [t_mmq] + t_hbuf_free_extra
            t_z = ev2[-1]
            t_c2_free = list(ev2)
            if DEBUG and sl == 0:
                t_dc2 = dma(dbg_c2, c2sb[:], "dbg5", deps=c2_half[1])
                t_dz = dma(dbg_z, A[:], "dbg6", deps=ev2)
                t_c2_free = t_c2_free + [t_dc2]
                ev2 = ev2 + [t_dz]
            t_h3 = ln_back(fr3)
            if sl + 1 < NSLAB:
                pf_deps = [t_store[sl - 1]] if sl >= 1 else []
                t_x[sl + 1] = dma(rbuf[(sl + 1) % 3][:].rearrange(
                    "p (c n) -> p c n", n=SLAB),
                    xt_v[:, :, (sl + 1) * SLAB:(sl + 2) * SLAB],
                    "ldx%d" % ((sl + 1) % 2), deps=pf_deps)
            pe_mid = None
            if sl + 1 < NSLAB:
                nxt = rbuf[(sl + 1) % 3]
                fr1n = ln_front_start()
                for c in range(NCH):
                    ln_front_chunk(fr1n, c, t_x[sl + 1], nxt, sq=False)
                for c in range(NCH):
                    ln_front_square(fr1n, c)
                h1n_holder = []

                def pe_mid(phase, fr1n=fr1n):
                    if phase == 0:
                        ln_front_finish(fr1n, which="s")
                    else:
                        ln_front_finish(fr1n, which="q")
            ev3, t_mm3, _ = projection(t_h3, t_weffb, weffb, 1024, NCH, b1, t_b1,
                                       A, t_z, B, ev2, pe_mid=pe_mid)
            if sl + 1 < NSLAB:
                t_h1_next = ln_back(fr1n, dst=c2sb, dst_war=t_c2_free)
            t_hbuf_free = [t_mm3]
            ev3_prev = ev3
            t_store[sl] = dma(
                yt_v[:, :, sl * SLAB:(sl + 1) * SLAB],
                B[:].rearrange("p (c n) -> p c n", n=SLAB),
                "st%d" % (sl % 2), deps=ev3)

        fin_deps = [("st0", s.counts["st0"]), ("st1", s.counts["st1"])]
        if DEBUG:
            fin_deps += [(d, s.counts[d]) for d in DMASEMS if d.startswith("dbg")
                         and s.counts[d] > 0]
        s.op("sp", lambda e: None, deps=fin_deps, inc=0)

        @block.sync
        def _(sync):
            s.replay(sync, "sp", sem_handles)

        @block.vector
        def _(vector):
            s.replay(vector, "dve", sem_handles)

        @block.tensor
        def _(tensor):
            s.replay(tensor, "pe", sem_handles)

        @block.scalar
        def _(scalar):
            s.replay(scalar, "act", sem_handles)

        @block.gpsimd
        def _(gpsimd):
            s.replay(gpsimd, "pool", sem_handles)

    return nc


def _host_inputs(L0_attn_x, L0_attn_W_qkv, L0_attn_b_qkv, L0_attn_W_out, L0_attn_b_out,
                 L0_ffn_W, L0_ffn_b, L0_ffn_lora_down, L0_ffn_lora_up,
                 L0_ffn_down_aux, L0_ffn_up_aux):
    x = np.asarray(L0_attn_x, dtype=np.float32).reshape(NCORES * TCORE, D)
    kindv, hv, dv = np.meshgrid(np.arange(3), np.arange(16), np.arange(64),
                                indexing="ij")
    perm = (192 * hv + 64 * kindv + dv).reshape(-1)
    wqkv = np.ascontiguousarray(np.asarray(L0_attn_W_qkv, dtype=np.float32)[:, perm])
    w1t = np.ascontiguousarray(np.asarray(L0_ffn_W, dtype=np.float32).T)
    wout = np.ascontiguousarray(np.asarray(L0_attn_W_out, dtype=np.float32))
    upat = np.ascontiguousarray(np.asarray(L0_ffn_up_aux, dtype=np.float32).T)
    lup = np.ascontiguousarray(np.asarray(L0_ffn_lora_up, dtype=np.float32))
    ldn = np.ascontiguousarray(np.asarray(L0_ffn_lora_down, dtype=np.float32))
    dnaux = np.ascontiguousarray(np.asarray(L0_ffn_down_aux, dtype=np.float32))
    bqd = np.ascontiguousarray(
        np.asarray(L0_attn_b_qkv, dtype=np.float32)[perm].reshape(24, 128).T)
    b1d = np.ascontiguousarray(
        np.asarray(L0_ffn_b, dtype=np.float32).reshape(8, 128).T)
    bod = np.ascontiguousarray(
        np.asarray(L0_attn_b_out, dtype=np.float32).reshape(8, 128).T)
    mask = np.zeros((128, 128), dtype=np.float32)
    for g in range(16):
        for t in range(8):
            for p in range(2):
                for j in range(8):
                    mask[g * 8 + t, p * 64 + j * 8 + t] = 1.0
    ident = np.zeros((128, 64), dtype=np.float32)
    ident[0:64] = np.eye(64, dtype=np.float32)
    ident[64:128] = np.eye(64, dtype=np.float32)
    shared = dict(wqkv=wqkv, w1t=w1t, wout=wout, upat=upat, lup=lup, ldn=ldn,
                  dnaux=dnaux, bqd=bqd, b1d=b1d, bod=bod, maskd=mask, identd=ident)
    in_maps = []
    for c in range(NCORES):
        shard = x[c * TCORE:(c + 1) * TCORE]
        xt = np.ascontiguousarray(shard.T)
        m = dict(shared)
        m["xt"] = xt
        in_maps.append(m)
    return in_maps


def kernel(L0_attn_x, L0_attn_W_qkv, L0_attn_b_qkv, L0_attn_W_out, L0_attn_b_out,
           L0_ffn_W, L0_ffn_b, L0_ffn_lora_down, L0_ffn_lora_up,
           L0_ffn_down_aux, L0_ffn_up_aux):
    in_maps = _host_inputs(L0_attn_x, L0_attn_W_qkv, L0_attn_b_qkv, L0_attn_W_out,
                           L0_attn_b_out, L0_ffn_W, L0_ffn_b, L0_ffn_lora_down,
                           L0_ffn_lora_up, L0_ffn_down_aux, L0_ffn_up_aux)
    nc = build_nc()
    res = run_bass_kernel_spmd(nc, in_maps, core_ids=list(range(NCORES)))
    outs = []
    for c in range(NCORES):
        yt = res.results[c]["yt"]
        outs.append(np.ascontiguousarray(yt.T))
    full = np.concatenate(outs, axis=0)
    return full.reshape(np.asarray(L0_attn_x).shape).astype(np.float32)


# version 41
# speedup vs baseline: 1.0198x; 1.0089x over previous
import numpy as np
from contextlib import ExitStack

import concourse.bass as bass
import concourse.mybir as mybir
from concourse.bass_utils import run_bass_kernel_spmd

F32 = mybir.dt.float32
BF16 = mybir.dt.bfloat16
AF = mybir.ActivationFunctionType
ALU = mybir.AluOpType

NCORES = 8
TCORE = 2048
SLAB = 512
NSLAB = TCORE // SLAB
HALF = 256
NGRP_H = HALF // 8
NBATCH_H = NGRP_H // 4
D = 1024
NCH = 8
EPS = 1e-5

DEBUG = 0

ENGS = ("pe", "dve", "act", "pool", "sp")
DMASEMS = ("ldc", "ldw1", "ldq0", "ldq1", "ldwo", "ldx0", "ldx1", "st0", "st1",
           "dbg0", "dbg1", "dbg2", "dbg3", "dbg4", "dbg5", "dbg6", "dbg7", "dbg8")


class Sched:

    def __init__(self):
        self.q = {e: [] for e in ENGS}
        self.counts = {}
        for e in ENGS:
            self.counts[e] = 0
        for e in DMASEMS:
            self.counts[e] = 0

    def op(self, eng, fn, deps=(), sem=None, inc=1):
        sem = sem or eng
        dl = []
        for d in deps:
            if d is None:
                continue
            if isinstance(d, list):
                dl.extend(x for x in d if x is not None)
            else:
                dl.append(d)
        self.counts[sem] += inc
        tok = (sem, self.counts[sem])
        self.q[eng].append((dl, fn, sem, inc))
        return tok

    def replay(self, engobj, eng, sems):
        marks = {}
        for deps, fn, sem, inc in self.q[eng]:
            for (ps, pc) in deps:
                if marks.get(ps, 0) < pc:
                    engobj.wait_ge(sems[ps], pc)
                    marks[ps] = pc
            inst = fn(engobj)
            if inst is not None and inc:
                inst.then_inc(sems[sem], inc)


def build_nc():
    nc = bass.Bass("TRN2", target_bir_lowering=False)

    xt = nc.dram_tensor("xt", [D, TCORE], F32, kind="ExternalInput").ap()
    wqkv = nc.dram_tensor("wqkv", [D, 3 * D], F32, kind="ExternalInput").ap()
    w1t = nc.dram_tensor("w1t", [D, D], F32, kind="ExternalInput").ap()
    wout = nc.dram_tensor("wout", [D, D], F32, kind="ExternalInput").ap()
    upat = nc.dram_tensor("upat", [50, D], F32, kind="ExternalInput").ap()
    lup = nc.dram_tensor("lup", [50, 4], F32, kind="ExternalInput").ap()
    ldn = nc.dram_tensor("ldn", [4, 100], F32, kind="ExternalInput").ap()
    dnaux = nc.dram_tensor("dnaux", [100, D], F32, kind="ExternalInput").ap()
    bqd = nc.dram_tensor("bqd", [128, 24], F32, kind="ExternalInput").ap()
    b1d = nc.dram_tensor("b1d", [128, 8], F32, kind="ExternalInput").ap()
    bod = nc.dram_tensor("bod", [128, 8], F32, kind="ExternalInput").ap()
    maskd = nc.dram_tensor("maskd", [128, 128], F32, kind="ExternalInput").ap()
    identd = nc.dram_tensor("identd", [128, 64], F32, kind="ExternalInput").ap()
    yt = nc.dram_tensor("yt", [D, TCORE], F32, kind="ExternalOutput").ap()
    if DEBUG:
        dbg_y = nc.dram_tensor("dbg_y", [128, NCH * SLAB], F32, kind="ExternalOutput").ap()
        dbg_h2 = nc.dram_tensor("dbg_h2", [128, NCH * SLAB], BF16, kind="ExternalOutput").ap()
        dbg_qg = nc.dram_tensor("dbg_qg", [64, NGRP_H * 128], BF16, kind="ExternalOutput").ap()
        dbg_kg = nc.dram_tensor("dbg_kg", [64, NGRP_H * 128], BF16, kind="ExternalOutput").ap()
        dbg_vg = nc.dram_tensor("dbg_vg", [64, NGRP_H * 128], BF16, kind="ExternalOutput").ap()
        dbg_c2 = nc.dram_tensor("dbg_c2", [128, 8 * SLAB], BF16, kind="ExternalOutput").ap()
        dbg_z = nc.dram_tensor("dbg_z", [128, NCH * SLAB], F32, kind="ExternalOutput").ap()
        dbg_xh = nc.dram_tensor("dbg_xh", [128, NCH * SLAB], BF16, kind="ExternalOutput").ap()
        dbg_rows = nc.dram_tensor("dbg_rows", [128, 2048], F32, kind="ExternalOutput").ap()

    s = Sched()
    with ExitStack() as ctx:
        al = lambda *a, **k: ctx.enter_context(nc.sbuf_tensor(*a, **k))
        alp = lambda *a, **k: ctx.enter_context(nc.psum_tensor(*a, **k))
        stg = al("stg", [128, 3072], F32)
        rowb = al("rowb", [128, 1024], BF16)
        wqkvb = al("wqkvb", [128, 8 * 3072], BF16)
        weffb = al("weffb", [128, 8 * 1024], BF16)
        woutb = al("woutb", [128, 8 * 1024], BF16)
        rbuf = [al(f"rbuf{i}", [128, NCH * SLAB], F32) for i in range(3)]
        xhat = al("xhat", [128, NCH * SLAB], BF16)
        x2b = al("x2b", [128, NCH * SLAB], BF16)
        hbuf = al("hbuf", [128, NCH * SLAB], BF16)
        qg = al("qg", [64, NGRP_H * 128], BF16)
        kg = al("kg", [64, NGRP_H * 128], BF16)
        vg = al("vg", [64, NGRP_H * 128], BF16)
        ebuf = [al(f"ebuf{i}", [128, 512], BF16) for i in range(2)]
        vb1 = [al(f"vb1_{i}", [128, 4 * 65], BF16) for i in range(2)]
        ctxg = [al(f"ctxg{i}", [128, 256], BF16) for i in range(2)]
        c2sb = al("c2sb", [128, 8 * SLAB], BF16)
        bq = al("bq", [128, 24], F32)
        b1 = al("b1", [128, 8], F32)
        bo = al("bo", [128, 8], F32)
        maskf = al("maskf", [128, 128], F32)
        maskb = al("maskb", [128, 128], BF16)
        identf = al("identf", [128, 64], F32)
        idb = al("idb", [128, 64], BF16)
        ones128 = al("ones128", [128, 128], BF16)
        epsc = al("epsc", [128, 1], F32)
        ksc = al("ksc", [128, 1], F32)
        nhc = al("nhc", [128, 1], F32)
        ivdc = al("ivdc", [128, 1], F32)
        rcol = [al(f"rcol{i}", [128, 4], F32) for i in range(2)]
        d4 = [al(f"d4_{i}", [128, 256], BF16) for i in range(2)]
        lora_up = al("lora_up", [50, 4], F32)
        lora_dn = al("lora_dn", [4, 100], F32)
        pbank = [alp(f"pbank{i}", [128, 512], F32) for i in range(2)]
        pstat = [alp(f"pstat{i}", [128, 512], F32) for i in range(2)]
        pscore = alp("pscore", [128, 512], F32)
        pctx = alp("pctx", [128, 512], F32)
        pvgp = alp("pvgp", [128, 1024], BF16)
        pc2 = alp("pc2", [128, 512], F32)

        sem_handles = {}
        for name in ENGS + DMASEMS:
            sem_handles[name] = ctx.enter_context(nc.semaphore(name))
        block = ctx.enter_context(nc.Block())

        c2f32 = c2sb[:].bitcast(F32)
        dna = c2f32[0:100, 0:1024]
        upa = c2f32[0:64, 1024:2048]
        w1t_sl2 = xhat[:].bitcast(F32)[:, 1024:2048]
        rows_m = stg[:, 0:512]
        rows_rstd = stg[:, 512:1024]
        rows_t = stg[:, 1024:1536]
        rows_v = stg[:, 1536:2048]

        def dma(dst, src, sem, deps=()):
            return s.op("sp", lambda e, dst=dst, src=src: e.dma_start(out=dst, in_=src),
                        deps=deps, sem=sem, inc=16)

        t_ones = s.op("dve", lambda e: e.memset(ones128[:], 1.0))
        t_eps = s.op("dve", lambda e: e.memset(epsc[:], EPS))
        t_ksc = s.op("dve", lambda e: e.memset(ksc[:], 0.125))
        t_nh = s.op("dve", lambda e: e.memset(nhc[:], -0.5))
        t_ivd = s.op("dve", lambda e: e.memset(ivdc[:], 1.0 / 1024.0))
        t_v1 = [s.op("dve", lambda e, i=i: e.memset(
            vb1[i][:].rearrange("p (g c) -> p g c", c=65)[:, :, 64:65], 1.0)) for i in range(2)]
        d_lora = [dma(dst, srcd, fam) for dst, srcd, fam in
                  ((lora_up[:], lup, "dbg0"), (lora_dn[:], ldn, "dbg1"),
                   (dna[0:100, :], dnaux, "dbg2"), (upa[0:50, :], upat, "dbg3"))]
        for dst, srcd in ((bq[:], bqd), (b1[:], b1d), (bo[:], bod),
                          (maskf[:], maskd), (identf[:], identd)):
            t_ldc = dma(dst, srcd, "ldc")
        d_b = [t_ldc] * 3
        t_maskb = s.op("dve", lambda e: e.tensor_copy(out=maskb[:], in_=maskf[:]),
                       deps=[t_ldc])
        t_idb = s.op("dve", lambda e: e.tensor_copy(out=idb[:], in_=identf[:]),
                     deps=[t_ldc])
        t_bq = s.op("dve", lambda e: e.tensor_scalar(
            out=bq[:, 8:16], in0=bq[:, 8:16], scalar1=0.125, scalar2=None,
            op0=ALU.mult), deps=[t_ldc])
        t_b1 = s.op("dve", lambda e: e.tensor_scalar(
            out=b1[:], in0=b1[:], scalar1=0.5, scalar2=None, op0=ALU.mult),
            deps=[t_ldc])

        xt_v = xt.rearrange("(c p) n -> p c n", p=128)
        yt_v = yt.rearrange("(c p) n -> p c n", p=128)
        t_x = [None] * NSLAB
        t_x[0] = dma(rbuf[0][:].rearrange("p (c n) -> p c n", n=SLAB),
                     xt_v[:, :, 0:SLAB], "ldx0")

        A_sl = stg[0:4, 0:1024]
        B_sl = stg[0:100, 1024:2048]
        w1t_sl = stg[:, 2048:3072]

        t_A_mm = s.op("pe", lambda e: (
            e.matmul(out=pbank[0][0:4, :], lhsT=lora_up[:], rhs=upa[0:50, 0:512],
                     start=True, stop=True),
            e.matmul(out=pbank[1][0:4, :], lhsT=lora_up[:], rhs=upa[0:50, 512:1024],
                     start=True, stop=True))[-1], deps=d_lora[:2] + [d_lora[3]])
        t_A = s.op("dve", lambda e: (
            e.tensor_copy(out=A_sl[:, 0:512], in_=pbank[0][0:4, :]),
            e.tensor_copy(out=A_sl[:, 512:1024], in_=pbank[1][0:4, :]))[-1],
            deps=[t_A_mm])
        t_B_mm = s.op("pe", lambda e: (
            e.matmul(out=pbank[0][0:100, :], lhsT=lora_dn[:], rhs=A_sl[:, 0:512],
                     start=True, stop=True),
            e.matmul(out=pbank[1][0:100, :], lhsT=lora_dn[:], rhs=A_sl[:, 512:1024],
                     start=True, stop=True))[-1], deps=[t_A, d_lora[1]])
        t_B = s.op("dve", lambda e: (
            e.tensor_copy(out=B_sl[:, 0:512], in_=pbank[0][0:100, :]),
            e.tensor_copy(out=B_sl[:, 512:1024], in_=pbank[1][0:100, :]))[-1],
            deps=[t_B_mm])
        t_weff_parts = []
        for kc in range(NCH):
            slw = w1t_sl if kc % 2 == 0 else w1t_sl2
            war = [t_weff_parts[kc - 2]] if kc >= 2 else []
            d_w1 = dma(slw, w1t.rearrange("(c p) m -> c p m", p=128)[kc],
                       "ldw1" if kc % 2 == 0 else "ldq1", deps=war)
            t_w1h = s.op("dve", lambda e, slw=slw: e.tensor_scalar(
                out=slw, in0=slw, scalar1=0.5, scalar2=None, op0=ALU.mult),
                deps=[d_w1])
            cbk0 = [pbank[0], pbank[1], pscore, pctx][(2 * kc) % 4]
            cbk1 = [pbank[0], pbank[1], pscore, pctx][(2 * kc + 1) % 4]
            t_c = s.op("pe", lambda e, kc=kc, cbk0=cbk0, cbk1=cbk1: (
                e.matmul(out=cbk0[:], lhsT=dna[0:100, kc * 128:(kc + 1) * 128],
                         rhs=B_sl[:, 0:512], start=True, stop=True),
                e.matmul(out=cbk1[:], lhsT=dna[0:100, kc * 128:(kc + 1) * 128],
                         rhs=B_sl[:, 512:1024], start=True, stop=True))[-1],
                deps=[t_B, d_lora[2]] + ([t_weff_parts[kc - 2]] if kc >= 2 else []))
            t_we = s.op("dve", lambda e, kc=kc, slw=slw, cbk0=cbk0, cbk1=cbk1: (
                e.scalar_tensor_tensor(
                    out=weffb[:, kc * 1024:kc * 1024 + 512], in0=cbk0[:],
                    scalar=0.5, in1=slw[:, 0:512], op0=ALU.mult, op1=ALU.add),
                e.scalar_tensor_tensor(
                    out=weffb[:, kc * 1024 + 512:(kc + 1) * 1024], in0=cbk1[:],
                    scalar=0.5, in1=slw[:, 512:1024], op0=ALU.mult, op1=ALU.add))[-1],
                deps=[t_c, t_w1h])
            t_weff_parts.append(t_we)
        t_weffb = t_weff_parts[-1]

        wqkv_v = wqkv.rearrange("(c p) m -> c p m", p=128)
        wout_v = wout.rearrange("(c p) m -> c p m", p=128)
        for kc in range(NCH):
            t_wqkvb = s.op("pool", lambda e, kc=kc: e.dma_start(
                out=wqkvb[:, kc * 3072:(kc + 1) * 3072], in_=wqkv_v[kc]),
                deps=[t_ldc] if kc == 0 else [], sem="ldq0", inc=16)
        for kc in range(NCH):
            t_woutb = s.op("pool", lambda e, kc=kc: e.dma_start(
                out=woutb[:, kc * 1024:(kc + 1) * 1024], in_=wout_v[kc]),
                deps=[], sem="ldwo", inc=16)

        bank_free = {"pscore": [t_woutb], "pstat0": [t_B_mm], "pstat1": [t_B_mm],
                     "pctx": [t_woutb], "pbank0": [t_woutb], "pbank1": [t_woutb]}
        proj_banks = [("pbank0", pbank[0]), ("pbank1", pbank[1]),
                      ("pscore", pscore), ("pctx", pctx)]
        t_xhat_free = [t_ones]
        t_hbuf_free = [t_ones]
        t_rowb_free = [t_ones]
        t_rows_free = [t_woutb]
        t_qkvg_free = [t_ones]
        t_ebuf_free = [[t_ones], [t_ones]]
        t_vb1_free = [[t_v1[0]], [t_v1[1]]]
        t_ctxg_free = [[t_ones], [t_ones]]
        t_rcol_free = [[t_ones], [t_ones]]
        t_d4_free = [[t_ones], [t_ones]]
        t_pvgp_free = [t_ones]
        t_pc2_free = [t_ones]
        t_c2_free = [t_weffb]
        t_store = [None] * NSLAB

        ln_calls = [0]
        t_x2b_free = [t_ones]

        def ln_front_start():
            return dict(cast=[], x2=[], mm_s=None, mm_q=None)

        def ln_front_square(st, c):
            nonlocal t_x2b_free
            t_c = st["cast"][c]
            t_q = s.op("dve", lambda e, c=c: e.tensor_tensor(
                out=x2b[:, c * SLAB:(c + 1) * SLAB],
                in0=xhat[:, c * SLAB:(c + 1) * SLAB],
                in1=xhat[:, c * SLAB:(c + 1) * SLAB], op=ALU.mult),
                deps=([t_c] if not isinstance(t_c, list) else t_c)
                     + (list(t_x2b_free) if c == 0 else []))
            st["x2"].append(t_q)

        def ln_front_chunk(st, c, tok, src, pre_cast=None, sq=True):
            nonlocal t_xhat_free
            if pre_cast is not None:
                t_c = pre_cast
            else:
                extra = list(t_xhat_free) if c == 0 else []
                t_c = s.op("dve", lambda e, c=c: e.tensor_copy(
                    out=xhat[:, c * SLAB:(c + 1) * SLAB],
                    in_=src[:, c * SLAB:(c + 1) * SLAB]),
                    deps=[tok] + extra)
            st["cast"].append(t_c)
            if sq:
                ln_front_square(st, c)

        def ln_front_finish(st, which="sq"):
            for c in range(NCH):
                cst = st["cast"][c]
                dep_s = (cst if isinstance(cst, list) else [cst]) + [t_ones]
                if c == 0:
                    dep_s += bank_free["pstat0"]
                if "s" in which:
                    st["mm_s"] = s.op("pe", lambda e, c=c: e.matmul(
                        out=pstat[0][:], lhsT=ones128[:],
                        rhs=xhat[:, c * SLAB:(c + 1) * SLAB],
                        start=(c == 0), stop=(c == NCH - 1), skip_group_check=True),
                        deps=dep_s, inc=(1 if c == NCH - 1 else 0))
            for c in range(NCH):
                if "q" not in which:
                    break
                dep_q = [st["x2"][c]]
                if c == 0:
                    dep_q += bank_free["pstat1"]
                if True:
                    st["mm_q"] = s.op("pe", lambda e, c=c: e.matmul(
                        out=pstat[1][:], lhsT=ones128[:],
                        rhs=x2b[:, c * SLAB:(c + 1) * SLAB],
                        start=(c == 0), stop=(c == NCH - 1), skip_group_check=True),
                        deps=dep_q, inc=(1 if c == NCH - 1 else 0))
            return st

        def ln_front(chunk_toks, src):
            if not isinstance(chunk_toks, list):
                chunk_toks = [chunk_toks] * NCH
            st = ln_front_start()
            for c in range(NCH):
                ln_front_chunk(st, c, chunk_toks[c], src)
            ln_front_finish(st)
            return st

        def ln_back(fr, dst=None, dst_war=None, per_chunk=False):
            nonlocal t_xhat_free, t_rowb_free, t_rows_free, t_x2b_free
            ln_calls[0] += 1
            t_mm_s, t_mm_q = fr["mm_s"], fr["mm_q"]
            t_cast = fr["cast"][-1]
            if isinstance(t_cast, list):
                t_cast = t_cast[-1]
            t_rm = s.op("dve", lambda e: e.tensor_scalar(
                out=rowb[:, 0:512], in0=pstat[0][:], scalar1=1.0 / 1024.0,
                scalar2=None, op0=ALU.mult), deps=[t_mm_s] + t_rowb_free)
            t_msq = s.op("dve", lambda e: e.tensor_tensor(
                out=rows_t, in0=rowb[:, 0:512], in1=rowb[:, 0:512],
                op=ALU.mult), deps=[t_rm] + t_rows_free)
            t_var = s.op("dve", lambda e: e.scalar_tensor_tensor(
                out=rows_v, in0=pstat[1][:], scalar=ivdc[:, 0:1], in1=rows_t,
                op0=ALU.mult, op1=ALU.subtract),
                deps=[t_mm_q, t_msq] + t_rows_free)
            t_ln = s.op("act", lambda e: e.activation(
                out=rows_m, in_=rows_v, func=AF.Ln, bias=epsc[:, 0:1], scale=1.0),
                deps=[t_var, t_eps] + t_rows_free)
            t_rstd = s.op("act", lambda e: e.activation(
                out=rowb[:, 512:1024], in_=rows_m, func=AF.Exp, bias=0.0,
                scale=nhc[:, 0:1]), deps=[t_ln, t_nh] + t_rowb_free)
            xv = xhat[:].rearrange("p (c n) -> p c n", n=SLAB)
            dstb = hbuf if dst is None else dst
            war = list(t_hbuf_free) if dst_war is None else list(dst_war)
            hv = dstb[:].rearrange("p (c n) -> p c n", n=SLAB)
            t_apps = []
            t_subs = []
            if per_chunk:
                for c in range(NCH):
                    cst = fr["cast"][c]
                    cd = cst if isinstance(cst, list) else [cst]
                    t_sub = s.op("dve", lambda e, c=c: e.tensor_tensor(
                        out=dstb[:, c * SLAB:(c + 1) * SLAB],
                        in0=xhat[:, c * SLAB:(c + 1) * SLAB],
                        in1=rowb[:, 0:512], op=ALU.subtract),
                        deps=[t_rm] + cd + (war if c == 0 else []))
                    t_subs.append(t_sub)
                for c in range(NCH):
                    t_app = s.op("dve", lambda e, c=c: e.tensor_tensor(
                        out=dstb[:, c * SLAB:(c + 1) * SLAB],
                        in0=dstb[:, c * SLAB:(c + 1) * SLAB],
                        in1=rowb[:, 512:1024], op=ALU.mult),
                        deps=[t_subs[c], t_rstd])
                    t_apps.append(t_app)
            else:
                for hh in range(2):
                    ho = hh * HALF
                    mb = rowb[:, ho:ho + HALF].unsqueeze(1).broadcast_to(
                        [128, NCH, HALF])
                    t_sub = s.op("dve", lambda e, ho=ho, mb=mb: e.tensor_tensor(
                        out=hv[:, :, ho:ho + HALF], in0=xv[:, :, ho:ho + HALF],
                        in1=mb, op=ALU.subtract),
                        deps=[t_rm, t_cast] + (war if hh == 0 else []))
                    t_subs.append(t_sub)
                for hh in range(2):
                    ho = hh * HALF
                    rb = rowb[:, 512 + ho:512 + ho + HALF].unsqueeze(1).broadcast_to(
                        [128, NCH, HALF])
                    t_app = s.op("dve", lambda e, ho=ho, rb=rb: e.tensor_tensor(
                        out=hv[:, :, ho:ho + HALF], in0=hv[:, :, ho:ho + HALF],
                        in1=rb, op=ALU.mult), deps=[t_subs[hh], t_rstd])
                    t_apps.append(t_app)
            bank_free["pstat0"] = [t_rm, t_msq]
            bank_free["pstat1"] = [t_var]
            t_xhat_free = [t_subs[-1]]
            t_x2b_free = [t_mm_q]
            t_rowb_free = [t_subs[-1], t_apps[-1]]
            t_rows_free = [t_var, t_ln, t_rstd]
            return t_apps

        def projection(t_h, t_w, wbuf, wbase_stride, nout, bias, t_bias, resid_in,
                       t_resid_in, resid_out, t_resid_out_war, operand=None,
                       ln_next_src=None, ln_next_tok=None, pe_mid=None):
            toks = []
            t_mm = None
            fr = ln_front_start() if ln_next_src is not None else None
            nb = len(proj_banks)
            opnd = hbuf if operand is None else operand
            per_kc = isinstance(t_h, list) and len(t_h) == NCH
            for oc in range(nout):
                bname, bk = proj_banks[oc % nb]
                if per_kc:
                    for kc in range(NCH):
                        t_mm = s.op("pe", lambda e, oc=oc, bk=bk, kc=kc: e.matmul(
                            out=bk[:],
                            lhsT=wbuf[:, kc * wbase_stride + oc * 128:
                                      kc * wbase_stride + oc * 128 + 128],
                            rhs=opnd[:, kc * SLAB:(kc + 1) * SLAB],
                            start=(kc == 0), stop=(kc == NCH - 1),
                            skip_group_check=True),
                            deps=[t_h[kc], t_w]
                                 + (bank_free[bname] if kc == 0 else []),
                            inc=(1 if kc == NCH - 1 else 0))
                else:
                    t_mm = s.op("pe", lambda e, oc=oc, bk=bk: [
                        e.matmul(out=bk[:],
                                 lhsT=wbuf[:, kc * wbase_stride + oc * 128:
                                           kc * wbase_stride + oc * 128 + 128],
                                 rhs=opnd[:, kc * SLAB:(kc + 1) * SLAB],
                                 start=(kc == 0), stop=(kc == NCH - 1))
                        for kc in range(NCH)][-1],
                        deps=[t_h, t_w] + bank_free[bname])
                t_pc = None
                if fr is not None and ln_next_tok is None:
                    xw = list(t_xhat_free) if oc == 0 else []
                    t_pc = s.op("dve", lambda e, oc=oc, bk=bk: e.scalar_tensor_tensor(
                        out=xhat[:, oc * SLAB:(oc + 1) * SLAB],
                        in0=bk[:], scalar=bias[:, oc:oc + 1],
                        in1=resid_in[:, oc * SLAB:(oc + 1) * SLAB],
                        op0=ALU.add, op1=ALU.add),
                        deps=[t_mm, t_resid_in, t_bias] + xw)
                t_ev = s.op("dve", lambda e, oc=oc, bk=bk: e.scalar_tensor_tensor(
                    out=resid_out[:, oc * SLAB:(oc + 1) * SLAB],
                    in0=bk[:], scalar=bias[:, oc:oc + 1],
                    in1=resid_in[:, oc * SLAB:(oc + 1) * SLAB],
                    op0=ALU.add, op1=ALU.add),
                    deps=[t_mm, t_resid_in, t_bias] + t_resid_out_war)
                bank_free[bname] = [t_ev]
                toks.append(t_ev)
                if fr is not None:
                    ln_front_chunk(fr, oc,
                                   (ln_next_tok if ln_next_tok is not None else t_ev),
                                   ln_next_src, pre_cast=t_pc)
                if pe_mid is not None and oc == 1:
                    pe_mid(0)
                if pe_mid is not None and oc == 3:
                    pe_mid(1)
            if fr is not None:
                ln_front_finish(fr)
            return toks, t_mm, fr

        def qkv_and_attention(t_h, slab, resid_y, t_y, resid_z, z_war, bo_, t_ln3_src):
            nonlocal t_qkvg_free, t_pvgp_free, t_pc2_free
            nonlocal t_ebuf_free, t_vb1_free, t_ctxg_free, t_rcol_free, t_d4_free
            last_qkv_mm = [None]
            dbg_qkv_toks = []
            t_c2 = [None]
            c2_half = {}
            ev2 = [None] * NCH
            ev2_h1 = [None] * NCH
            pc3 = [[None] * NCH, [None] * NCH]
            fr3 = ln_front_start()
            nb = len(proj_banks)

            def wout_half(half, c2toks, with_ln):
                hoff = half * HALF
                for oc in range(NCH):
                    bname, bk = proj_banks[oc % nb]
                    t_mm = s.op("pe", lambda e, oc=oc, bk=bk, hoff=hoff: [
                        e.matmul(out=bk[:, 0:HALF],
                                 lhsT=woutb[:, j * 1024 + oc * 128:j * 1024 + oc * 128 + 128],
                                 rhs=c2sb[:, j * SLAB + hoff:j * SLAB + hoff + HALF],
                                 start=(j == 0), stop=(j == NCH - 1))
                        for j in range(NCH)][-1],
                        deps=c2toks + [t_woutb] + bank_free[bname])
                    if with_ln or half == 0:
                        xw = list(t_xhat_free) if (half == 0 and oc == 0) else []
                        t_pc = s.op("dve", lambda e, oc=oc, bk=bk, hoff=hoff:
                                    e.scalar_tensor_tensor(
                            out=xhat[:, oc * SLAB + hoff:oc * SLAB + hoff + HALF],
                            in0=bk[:, 0:HALF], scalar=bo_[:, oc:oc + 1],
                            in1=resid_y[:, oc * SLAB + hoff:oc * SLAB + hoff + HALF],
                            op0=ALU.add, op1=ALU.add),
                            deps=[t_mm, t_y, t_ldc] + xw)
                        pc3[half][oc] = t_pc
                    t_ev = s.op("dve", lambda e, oc=oc, bk=bk, hoff=hoff:
                                e.scalar_tensor_tensor(
                        out=resid_z[:, oc * SLAB + hoff:oc * SLAB + hoff + HALF],
                        in0=bk[:, 0:HALF], scalar=bo_[:, oc:oc + 1],
                        in1=resid_y[:, oc * SLAB + hoff:oc * SLAB + hoff + HALF],
                        op0=ALU.add, op1=ALU.add),
                        deps=[t_mm, t_y, t_ldc] + z_war)
                    bank_free[bname] = [t_ev]
                    if half == 0:
                        ev2_h1[oc] = t_ev
                    else:
                        ev2[oc] = t_ev
                        if with_ln:
                            ln_front_chunk(fr3, oc, None, t_ln3_src,
                                           pre_cast=[pc3[0][oc], pc3[1][oc]])
                if half == 1 and with_ln:
                    ln_front_finish(fr3)

            qkv_banks = proj_banks + [("pstat0", pstat[0]), ("pstat1", pstat[1])]
            for half in range(2):
                hoff = half * HALF
                ev_toks_q, ev_toks_kv = [], []
                for oc in range(24):
                    bname, bk = qkv_banks[oc % 6]
                    t_mm = s.op("pe", lambda e, oc=oc, bk=bk, hoff=hoff: [
                        e.matmul(out=bk[:, 0:HALF],
                                 lhsT=wqkvb[:, kc * 3072 + oc * 128:
                                            kc * 3072 + oc * 128 + 128],
                                 rhs=hbuf[:, kc * SLAB + hoff:kc * SLAB + hoff + HALF],
                                 start=(kc == 0), stop=(kc == NCH - 1))
                        for kc in range(NCH)][-1],
                        deps=[t_h[half] if isinstance(t_h, list) else t_h,
                              t_wqkvb] + bank_free[bname])
                    last_qkv_mm[0] = t_mm
                    kind = oc // 8
                    c = oc % 8
                    if kind == 0:
                        dst = qg
                        offs = (8 * c, 64 + 8 * c)
                    elif kind == 1:
                        dst = kg
                        offs = (16 * c, 16 * c + 8)
                    else:
                        dst = vg
                        offs = (16 * c, 16 * c + 8)
                    dstv = dst[:].rearrange("d (g c) -> d g c", c=128)
                    if kind == 0:
                        t_ev = s.op("act", lambda e, bk=bk, oc=oc, dstv=dstv, offs=offs: (
                            e.activation(out=dstv[:, :, offs[0]:offs[0] + 8],
                                         in_=bk[0:64, 0:HALF].rearrange(
                                             "d (g t) -> d g t", t=8),
                                         func=AF.Identity, bias=bq[0:64, oc:oc + 1],
                                         scale=1.0),
                            e.activation(out=dstv[:, :, offs[1]:offs[1] + 8],
                                         in_=bk[64:128, 0:HALF].rearrange(
                                             "d (g t) -> d g t", t=8),
                                         func=AF.Identity, bias=bq[64:128, oc:oc + 1],
                                         scale=1.0))[-1],
                            deps=[t_mm, t_bq] + t_qkvg_free, inc=2)
                        ev_toks_q.append(t_ev)
                    else:
                        sc = ksc[:, 0:1] if kind == 1 else 1.0
                        t_ev = s.op("act", lambda e, bk=bk, oc=oc, dstv=dstv,
                                    offs=offs, sc=sc: (
                            e.activation(out=dstv[:, :, offs[0]:offs[0] + 8],
                                         in_=bk[0:64, 0:HALF].rearrange(
                                             "d (g t) -> d g t", t=8),
                                         func=AF.Identity, bias=bq[0:64, oc:oc + 1],
                                         scale=(sc if isinstance(sc, float)
                                                else ksc[0:64, 0:1])),
                            e.activation(out=dstv[:, :, offs[1]:offs[1] + 8],
                                         in_=bk[64:128, 0:HALF].rearrange(
                                             "d (g t) -> d g t", t=8),
                                         func=AF.Identity, bias=bq[64:128, oc:oc + 1],
                                         scale=(sc if isinstance(sc, float)
                                                else ksc[64:128, 0:1])))[-1],
                            deps=[t_mm, t_bq, t_ksc] + t_qkvg_free, inc=2)
                        ev_toks_kv.append(t_ev)
                    bank_free[bname] = [t_ev]
                t_q_ready = ev_toks_q[-1]
                t_kv_ready = ev_toks_kv[-1]
                if DEBUG and slab == 0 and half == 0:
                    t_dq = dma(dbg_qg, qg[:], "dbg2", deps=[t_q_ready])
                    t_dk = dma(dbg_kg, kg[:], "dbg3", deps=[t_kv_ready])
                    t_dv = dma(dbg_vg, vg[:], "dbg4", deps=[t_kv_ready])
                    dbg_qkv_toks.extend([t_dq, t_dk, t_dv])

                if half == 1:
                    wout_half(0, c2_half[0], with_ln=False)

                sc_banks = [(pscore, "pscore"), (pstat[0], "pstat0")]
                ctx_banks = [(pctx, "pctx"), (pstat[1], "pstat1")]
                t_sc = {}
                t_exp = {}
                t_mask = {}
                t_vt = {}
                t_vev = {}
                t_ctx = {}
                t_rc = {}
                t_d4 = {}
                t_cge = {}
                t_ct = {}
                half_c2 = []
                NB = NBATCH_H
                for b in range(NB + 2):
                    par = b % 2
                    if b < NB:
                        g0 = b * 4
                        sb, sbname = sc_banks[par]
                        war_sc = list(bank_free[sbname])
                        t_sc[b] = s.op("pe", lambda e, g0=g0, sb=sb: [
                            e.matmul(out=sb[:, i * 128:(i + 1) * 128],
                                     lhsT=kg[:, (g0 + i) * 128:(g0 + i + 1) * 128],
                                     rhs=qg[:, (g0 + i) * 128:(g0 + i + 1) * 128],
                                     start=True, stop=True) for i in range(4)][-1],
                            deps=[t_q_ready, t_kv_ready] + war_sc, inc=4)
                        war_vt = [t_vev[b - 1]] if b >= 1 else t_pvgp_free
                        t_vt[b] = s.op("pe", lambda e, g0=g0: [
                            e.transpose(out=pvgp[:, i * 64:(i + 1) * 64],
                                        in_=vg[:, (g0 + i) * 128:(g0 + i + 1) * 128],
                                        identity=idb[0:64, :]) for i in range(4)][-1],
                            deps=[t_kv_ready, t_idb] + war_vt, inc=4)
                        war_e = [t_ctx[b - 2]] if b >= 2 else t_ebuf_free[par]
                        t_exp[b] = s.op("act", lambda e, par=par, sb=sb: e.activation(
                            out=ebuf[par][:], in_=sb[:], func=AF.Exp),
                            deps=[t_sc[b]] + war_e)
                        bank_free[sbname] = [t_exp[b]]
                        t_mask[b] = s.op("dve", lambda e, par=par: e.tensor_tensor(
                            out=ebuf[par][:].rearrange("p (g c) -> p g c", c=128),
                            in0=ebuf[par][:].rearrange("p (g c) -> p g c", c=128),
                            in1=maskb[:].unsqueeze(1).broadcast_to([128, 4, 128]),
                            op=ALU.mult), deps=[t_exp[b], t_maskb])
                        war_v = [t_ctx[b - 2]] if b >= 2 else t_vb1_free[par]
                        t_vev[b] = s.op("dve", lambda e, par=par: e.tensor_copy(
                            out=vb1[par][:].rearrange("p (g c) -> p g c", c=65)[:, :, 0:64],
                            in_=pvgp[:, 0:256].rearrange("p (g c) -> p g c", c=64)),
                            deps=[t_vt[b]] + war_v)
                    bb = b - 1
                    if 0 <= bb < NB:
                        par = bb % 2
                        cb, cbname = ctx_banks[par]
                        war_c = list(bank_free[cbname])
                        t_ctx[bb] = s.op("pe", lambda e, par=par, cb=cb: [
                            e.matmul(out=cb[:, i * 65:(i + 1) * 65],
                                     lhsT=ebuf[par][:, i * 128:(i + 1) * 128],
                                     rhs=vb1[par][:, i * 65:(i + 1) * 65],
                                     start=True, stop=True) for i in range(4)][-1],
                            deps=[t_mask[bb], t_vev[bb]] + war_c, inc=4)
                        t_rc[bb] = s.op("dve", lambda e, par=par, cb=cb: e.reciprocal(
                            out=rcol[par][:], in_=cb[:, 0:260].rearrange(
                                "p (g c) -> p g c", c=65)[:, :, 64]),
                            deps=[t_ctx[bb]] + ([t_d4[bb - 2]] if bb >= 2
                                                else t_rcol_free[par]))
                        t_d4[bb] = s.op("dve", lambda e, par=par: [
                            e.tensor_scalar(out=d4[par][:, i * 64:(i + 1) * 64],
                                            in0=idb[:], scalar1=rcol[par][:, i:i + 1],
                                            scalar2=None, op0=ALU.mult)
                            for i in range(4)][-1],
                            deps=[t_rc[bb], t_idb] + ([t_ct[bb - 2]] if bb >= 2
                                                       else t_d4_free[par]),
                            inc=4)
                        war_g = [t_ct[bb - 2]] if bb >= 2 else t_ctxg_free[par]
                        t_cge[bb] = s.op("act", lambda e, par=par, cb=cb: e.activation(
                            out=ctxg[par][:].rearrange("p (g c) -> p g c", c=64),
                            in_=cb[:, 0:260].rearrange("p (g c) -> p g c", c=65)[:, :, 0:64],
                            func=AF.Identity, bias=0.0, scale=1.0),
                            deps=[t_ctx[bb]] + war_g)
                        bank_free[cbname] = [t_rc[bb], t_cge[bb]]
                    bc = b - 2
                    if 0 <= bc < NB:
                        par = bc % 2
                        war_t = [t_c2[0]] if t_c2[0] is not None else t_pc2_free
                        t_ct[bc] = s.op("pe", lambda e, par=par: [
                            e.matmul(out=pc2[(0 if pp == 0 else 64):(64 if pp == 0 else 128),
                                             i * 64:(i + 1) * 64],
                                     lhsT=ctxg[par][(0 if pp == 0 else 64):(64 if pp == 0 else 128),
                                                    i * 64:(i + 1) * 64],
                                     rhs=d4[par][(0 if pp == 0 else 64):(64 if pp == 0 else 128),
                                                 i * 64:(i + 1) * 64],
                                     start=True, stop=True,
                                     tile_position=(None if pp == 0 else (64, 64)))
                            for i in range(4) for pp in range(2)][-1],
                            deps=[t_cge[bc], t_d4[bc]] + war_t, inc=8)
                        coff = hoff + bc * 32
                        t_c2[0] = s.op("dve", lambda e, coff=coff: e.tensor_copy(
                            out=c2sb[:].rearrange("p (j n) -> p j n", n=SLAB)[
                                :, :, coff:coff + 32].rearrange("p j (g t) -> p j g t", t=8),
                            in_=pc2[:, 0:256].rearrange("p (g j t) -> p j g t", g=4, j=8, t=8)),
                            deps=[t_ct[bc]] + t_c2_free)
                        half_c2.append(t_c2[0])
                t_pvgp_free = [t_vev[NB - 1]]
                t_pc2_free = [t_c2[0]]
                for p in range(2):
                    lastb = NB - 2 + p
                    t_ebuf_free[p] = [t_ctx[lastb]]
                    t_vb1_free[p] = [t_ctx[lastb]]
                    t_ctxg_free[p] = [t_ct[lastb]]
                    t_rcol_free[p] = [t_d4[lastb]]
                    t_d4_free[p] = [t_ct[lastb]]
                t_qkvg_free = [t_ct[NB - 1]] + dbg_qkv_toks
                dbg_qkv_toks = []
                c2_half[half] = [half_c2[-1]]
            wout_half(1, c2_half[1], with_ln=True)
            return ev2, fr3, last_qkv_mm[0], c2_half

        ev3_prev = []
        fr1_next = None
        t_h1_next = None
        for sl in range(NSLAB):
            A = rbuf[sl % 3]
            B = rbuf[(sl + 2) % 3]
            tA = t_x[sl]
            b_war = list(ev3_prev)
            if sl == 0:
                fr1 = ln_front(tA, A)
                t_h1 = ln_back(fr1, dst=c2sb, dst_war=t_c2_free, per_chunk=True)
            else:
                t_h1 = t_h1_next
            ev1, t_mm1, fr2 = projection(t_h1, t_weffb, weffb, 1024, NCH, b1, t_b1,
                                         A, tA, B, b_war, operand=c2sb,
                                         ln_next_src=B)
            t_y = ev1[-1]
            t_c2_free = [t_mm1]
            if DEBUG and sl == 0:
                t_d = dma(dbg_y, B[:], "dbg0", deps=ev1)
                ev1 = ev1 + [t_d]
            t_h2 = ln_back(fr2)
            if DEBUG and sl == 0:
                t_dh = dma(dbg_h2, hbuf[:], "dbg1", deps=[t_h2])
                t_hbuf_free_extra = [t_dh]
            else:
                t_hbuf_free_extra = []
            ev2, fr3, t_mmq, c2_half = qkv_and_attention(
                t_h2, sl, B, t_y, A, ev1, bo, A)
            t_hbuf_free = [t_mmq] + t_hbuf_free_extra
            t_z = ev2[-1]
            t_c2_free = list(ev2)
            if DEBUG and sl == 0:
                t_dc2 = dma(dbg_c2, c2sb[:], "dbg5", deps=c2_half[1])
                t_dz = dma(dbg_z, A[:], "dbg6", deps=ev2)
                t_c2_free = t_c2_free + [t_dc2]
                ev2 = ev2 + [t_dz]
            t_h3 = ln_back(fr3, per_chunk=True)
            if sl + 1 < NSLAB:
                pf_deps = [t_store[sl - 1]] if sl >= 1 else []
                t_x[sl + 1] = dma(rbuf[(sl + 1) % 3][:].rearrange(
                    "p (c n) -> p c n", n=SLAB),
                    xt_v[:, :, (sl + 1) * SLAB:(sl + 2) * SLAB],
                    "ldx%d" % ((sl + 1) % 2), deps=pf_deps)
            pe_mid = None
            if sl + 1 < NSLAB:
                nxt = rbuf[(sl + 1) % 3]
                fr1n = ln_front_start()
                for c in range(NCH):
                    ln_front_chunk(fr1n, c, t_x[sl + 1], nxt, sq=False)
                for c in range(NCH):
                    ln_front_square(fr1n, c)
                h1n_holder = []

                def pe_mid(phase, fr1n=fr1n):
                    if phase == 0:
                        ln_front_finish(fr1n, which="s")
                    else:
                        ln_front_finish(fr1n, which="q")
            ev3, t_mm3, _ = projection(t_h3, t_weffb, weffb, 1024, NCH, b1, t_b1,
                                       A, t_z, B, ev2, pe_mid=pe_mid)
            if sl + 1 < NSLAB:
                t_h1_next = ln_back(fr1n, dst=c2sb, dst_war=t_c2_free,
                                    per_chunk=True)
            t_hbuf_free = [t_mm3]
            ev3_prev = ev3
            t_store[sl] = dma(
                yt_v[:, :, sl * SLAB:(sl + 1) * SLAB],
                B[:].rearrange("p (c n) -> p c n", n=SLAB),
                "st%d" % (sl % 2), deps=ev3)

        fin_deps = [("st0", s.counts["st0"]), ("st1", s.counts["st1"])]
        if DEBUG:
            fin_deps += [(d, s.counts[d]) for d in DMASEMS if d.startswith("dbg")
                         and s.counts[d] > 0]
        s.op("sp", lambda e: None, deps=fin_deps, inc=0)

        @block.sync
        def _(sync):
            s.replay(sync, "sp", sem_handles)

        @block.vector
        def _(vector):
            s.replay(vector, "dve", sem_handles)

        @block.tensor
        def _(tensor):
            s.replay(tensor, "pe", sem_handles)

        @block.scalar
        def _(scalar):
            s.replay(scalar, "act", sem_handles)

        @block.gpsimd
        def _(gpsimd):
            s.replay(gpsimd, "pool", sem_handles)

    return nc


def _host_inputs(L0_attn_x, L0_attn_W_qkv, L0_attn_b_qkv, L0_attn_W_out, L0_attn_b_out,
                 L0_ffn_W, L0_ffn_b, L0_ffn_lora_down, L0_ffn_lora_up,
                 L0_ffn_down_aux, L0_ffn_up_aux):
    x = np.asarray(L0_attn_x, dtype=np.float32).reshape(NCORES * TCORE, D)
    kindv, hv, dv = np.meshgrid(np.arange(3), np.arange(16), np.arange(64),
                                indexing="ij")
    perm = (192 * hv + 64 * kindv + dv).reshape(-1)
    wqkv = np.ascontiguousarray(np.asarray(L0_attn_W_qkv, dtype=np.float32)[:, perm])
    w1t = np.ascontiguousarray(np.asarray(L0_ffn_W, dtype=np.float32).T)
    wout = np.ascontiguousarray(np.asarray(L0_attn_W_out, dtype=np.float32))
    upat = np.ascontiguousarray(np.asarray(L0_ffn_up_aux, dtype=np.float32).T)
    lup = np.ascontiguousarray(np.asarray(L0_ffn_lora_up, dtype=np.float32))
    ldn = np.ascontiguousarray(np.asarray(L0_ffn_lora_down, dtype=np.float32))
    dnaux = np.ascontiguousarray(np.asarray(L0_ffn_down_aux, dtype=np.float32))
    bqd = np.ascontiguousarray(
        np.asarray(L0_attn_b_qkv, dtype=np.float32)[perm].reshape(24, 128).T)
    b1d = np.ascontiguousarray(
        np.asarray(L0_ffn_b, dtype=np.float32).reshape(8, 128).T)
    bod = np.ascontiguousarray(
        np.asarray(L0_attn_b_out, dtype=np.float32).reshape(8, 128).T)
    mask = np.zeros((128, 128), dtype=np.float32)
    for g in range(16):
        for t in range(8):
            for p in range(2):
                for j in range(8):
                    mask[g * 8 + t, p * 64 + j * 8 + t] = 1.0
    ident = np.zeros((128, 64), dtype=np.float32)
    ident[0:64] = np.eye(64, dtype=np.float32)
    ident[64:128] = np.eye(64, dtype=np.float32)
    shared = dict(wqkv=wqkv, w1t=w1t, wout=wout, upat=upat, lup=lup, ldn=ldn,
                  dnaux=dnaux, bqd=bqd, b1d=b1d, bod=bod, maskd=mask, identd=ident)
    in_maps = []
    for c in range(NCORES):
        shard = x[c * TCORE:(c + 1) * TCORE]
        xt = np.ascontiguousarray(shard.T)
        m = dict(shared)
        m["xt"] = xt
        in_maps.append(m)
    return in_maps


def kernel(L0_attn_x, L0_attn_W_qkv, L0_attn_b_qkv, L0_attn_W_out, L0_attn_b_out,
           L0_ffn_W, L0_ffn_b, L0_ffn_lora_down, L0_ffn_lora_up,
           L0_ffn_down_aux, L0_ffn_up_aux):
    in_maps = _host_inputs(L0_attn_x, L0_attn_W_qkv, L0_attn_b_qkv, L0_attn_W_out,
                           L0_attn_b_out, L0_ffn_W, L0_ffn_b, L0_ffn_lora_down,
                           L0_ffn_lora_up, L0_ffn_down_aux, L0_ffn_up_aux)
    nc = build_nc()
    res = run_bass_kernel_spmd(nc, in_maps, core_ids=list(range(NCORES)))
    outs = []
    for c in range(NCORES):
        yt = res.results[c]["yt"]
        outs.append(np.ascontiguousarray(yt.T))
    full = np.concatenate(outs, axis=0)
    return full.reshape(np.asarray(L0_attn_x).shape).astype(np.float32)


# version 42
# speedup vs baseline: 1.0455x; 1.0251x over previous
import numpy as np
from contextlib import ExitStack

import concourse.bass as bass
import concourse.mybir as mybir
from concourse.bass_utils import run_bass_kernel_spmd

F32 = mybir.dt.float32
BF16 = mybir.dt.bfloat16
AF = mybir.ActivationFunctionType
ALU = mybir.AluOpType

NCORES = 8
TCORE = 2048
SLAB = 512
NSLAB = TCORE // SLAB
HALF = 256
NGRP_H = HALF // 8
NBATCH_H = NGRP_H // 4
D = 1024
NCH = 8
EPS = 1e-5

DEBUG = 0

ENGS = ("pe", "dve", "act", "pool", "sp")
DMASEMS = ("ldc", "ldw1", "ldq0", "ldq1", "ldwo", "ldx0", "ldx1", "st0", "st1",
           "dbg0", "dbg1", "dbg2", "dbg3", "dbg4", "dbg5", "dbg6", "dbg7", "dbg8")


class Sched:

    def __init__(self):
        self.q = {e: [] for e in ENGS}
        self.counts = {}
        for e in ENGS:
            self.counts[e] = 0
        for e in DMASEMS:
            self.counts[e] = 0

    def op(self, eng, fn, deps=(), sem=None, inc=1):
        sem = sem or eng
        dl = []
        for d in deps:
            if d is None:
                continue
            if isinstance(d, list):
                dl.extend(x for x in d if x is not None)
            else:
                dl.append(d)
        self.counts[sem] += inc
        tok = (sem, self.counts[sem])
        self.q[eng].append((dl, fn, sem, inc))
        return tok

    def replay(self, engobj, eng, sems):
        marks = {}
        for deps, fn, sem, inc in self.q[eng]:
            for (ps, pc) in deps:
                if marks.get(ps, 0) < pc:
                    engobj.wait_ge(sems[ps], pc)
                    marks[ps] = pc
            inst = fn(engobj)
            if inst is not None and inc:
                inst.then_inc(sems[sem], inc)


def build_nc():
    nc = bass.Bass("TRN2", target_bir_lowering=False)

    xt = nc.dram_tensor("xt", [D, TCORE], F32, kind="ExternalInput").ap()
    wqkv = nc.dram_tensor("wqkv", [D, 3 * D], F32, kind="ExternalInput").ap()
    w1t = nc.dram_tensor("w1t", [D, D], F32, kind="ExternalInput").ap()
    wout = nc.dram_tensor("wout", [D, D], F32, kind="ExternalInput").ap()
    upat = nc.dram_tensor("upat", [50, D], F32, kind="ExternalInput").ap()
    lup = nc.dram_tensor("lup", [50, 4], F32, kind="ExternalInput").ap()
    ldn = nc.dram_tensor("ldn", [4, 100], F32, kind="ExternalInput").ap()
    dnaux = nc.dram_tensor("dnaux", [100, D], F32, kind="ExternalInput").ap()
    bqd = nc.dram_tensor("bqd", [128, 24], F32, kind="ExternalInput").ap()
    b1d = nc.dram_tensor("b1d", [128, 8], F32, kind="ExternalInput").ap()
    bod = nc.dram_tensor("bod", [128, 8], F32, kind="ExternalInput").ap()
    maskd = nc.dram_tensor("maskd", [128, 128], F32, kind="ExternalInput").ap()
    identd = nc.dram_tensor("identd", [128, 64], F32, kind="ExternalInput").ap()
    yt = nc.dram_tensor("yt", [D, TCORE], F32, kind="ExternalOutput").ap()
    if DEBUG:
        dbg_y = nc.dram_tensor("dbg_y", [128, NCH * SLAB], F32, kind="ExternalOutput").ap()
        dbg_h2 = nc.dram_tensor("dbg_h2", [128, NCH * SLAB], BF16, kind="ExternalOutput").ap()
        dbg_qg = nc.dram_tensor("dbg_qg", [64, NGRP_H * 128], BF16, kind="ExternalOutput").ap()
        dbg_kg = nc.dram_tensor("dbg_kg", [64, NGRP_H * 128], BF16, kind="ExternalOutput").ap()
        dbg_vg = nc.dram_tensor("dbg_vg", [64, NGRP_H * 128], BF16, kind="ExternalOutput").ap()
        dbg_c2 = nc.dram_tensor("dbg_c2", [128, 8 * SLAB], BF16, kind="ExternalOutput").ap()
        dbg_z = nc.dram_tensor("dbg_z", [128, NCH * SLAB], F32, kind="ExternalOutput").ap()
        dbg_xh = nc.dram_tensor("dbg_xh", [128, NCH * SLAB], BF16, kind="ExternalOutput").ap()
        dbg_rows = nc.dram_tensor("dbg_rows", [128, 2048], F32, kind="ExternalOutput").ap()

    s = Sched()
    with ExitStack() as ctx:
        al = lambda *a, **k: ctx.enter_context(nc.sbuf_tensor(*a, **k))
        alp = lambda *a, **k: ctx.enter_context(nc.psum_tensor(*a, **k))
        stg = al("stg", [128, 3072], F32)
        rowb = al("rowb", [128, 1024], BF16)
        wqkvb = al("wqkvb", [128, 8 * 3072], BF16)
        weffb = al("weffb", [128, 8 * 1024], BF16)
        woutb = al("woutb", [128, 8 * 1024], BF16)
        rbuf = [al(f"rbuf{i}", [128, NCH * SLAB], F32) for i in range(3)]
        xhat = al("xhat", [128, NCH * SLAB], BF16)
        x2b = al("x2b", [128, NCH * SLAB], BF16)
        hbuf = al("hbuf", [128, NCH * SLAB], BF16)
        qg = al("qg", [64, NGRP_H * 128], BF16)
        kg = al("kg", [64, NGRP_H * 128], BF16)
        vg = al("vg", [64, NGRP_H * 128], BF16)
        ebuf = [al(f"ebuf{i}", [128, 512], BF16) for i in range(2)]
        vb1 = [al(f"vb1_{i}", [128, 4 * 65], BF16) for i in range(2)]
        ctxg = [al(f"ctxg{i}", [128, 256], BF16) for i in range(2)]
        c2sb = al("c2sb", [128, 8 * SLAB], BF16)
        bq = al("bq", [128, 24], F32)
        b1 = al("b1", [128, 8], F32)
        bo = al("bo", [128, 8], F32)
        maskf = al("maskf", [128, 128], F32)
        maskb = al("maskb", [128, 128], BF16)
        identf = al("identf", [128, 64], F32)
        idb = al("idb", [128, 64], BF16)
        ones128 = al("ones128", [128, 128], BF16)
        epsc = al("epsc", [128, 1], F32)
        ksc = al("ksc", [128, 1], F32)
        nhc = al("nhc", [128, 1], F32)
        ivdc = al("ivdc", [128, 1], F32)
        rcol = [al(f"rcol{i}", [128, 4], F32) for i in range(2)]
        d4 = [al(f"d4_{i}", [128, 256], BF16) for i in range(2)]
        lora_up = al("lora_up", [50, 4], F32)
        lora_dn = al("lora_dn", [4, 100], F32)
        pbank = [alp(f"pbank{i}", [128, 512], F32) for i in range(2)]
        pstat = [alp(f"pstat{i}", [128, 512], F32) for i in range(2)]
        pscore = alp("pscore", [128, 512], F32)
        pctx = alp("pctx", [128, 512], F32)
        pvgp = alp("pvgp", [128, 1024], BF16)
        pc2 = alp("pc2", [128, 512], F32)

        sem_handles = {}
        for name in ENGS + DMASEMS:
            sem_handles[name] = ctx.enter_context(nc.semaphore(name))
        block = ctx.enter_context(nc.Block())

        c2f32 = c2sb[:].bitcast(F32)
        dna = c2f32[0:100, 0:1024]
        upa = c2f32[0:64, 1024:2048]
        w1t_sl2 = xhat[:].bitcast(F32)[:, 1024:2048]
        rows_m = stg[:, 0:512]
        rows_rstd = stg[:, 512:1024]
        rows_t = stg[:, 1024:1536]
        rows_v = stg[:, 1536:2048]

        def dma(dst, src, sem, deps=()):
            return s.op("sp", lambda e, dst=dst, src=src: e.dma_start(out=dst, in_=src),
                        deps=deps, sem=sem, inc=16)

        t_ones = s.op("dve", lambda e: e.memset(ones128[:], 1.0))
        t_eps = s.op("dve", lambda e: e.memset(epsc[:], EPS))
        t_ksc = s.op("dve", lambda e: e.memset(ksc[:], 0.125))
        t_nh = s.op("dve", lambda e: e.memset(nhc[:], -0.5))
        t_ivd = s.op("dve", lambda e: e.memset(ivdc[:], 1.0 / 1024.0))
        t_v1 = [s.op("dve", lambda e, i=i: e.memset(
            vb1[i][:].rearrange("p (g c) -> p g c", c=65)[:, :, 64:65], 1.0)) for i in range(2)]
        d_lora = [dma(dst, srcd, fam) for dst, srcd, fam in
                  ((lora_up[:], lup, "dbg0"), (lora_dn[:], ldn, "dbg1"),
                   (dna[0:100, :], dnaux, "dbg2"), (upa[0:50, :], upat, "dbg3"))]
        for dst, srcd in ((bq[:], bqd), (b1[:], b1d), (bo[:], bod),
                          (maskf[:], maskd), (identf[:], identd)):
            t_ldc = dma(dst, srcd, "ldc")
        d_b = [t_ldc] * 3
        t_maskb = s.op("dve", lambda e: e.tensor_copy(out=maskb[:], in_=maskf[:]),
                       deps=[t_ldc])
        t_idb = s.op("dve", lambda e: e.tensor_copy(out=idb[:], in_=identf[:]),
                     deps=[t_ldc])
        t_bq = s.op("dve", lambda e: e.tensor_scalar(
            out=bq[:, 8:16], in0=bq[:, 8:16], scalar1=0.125, scalar2=None,
            op0=ALU.mult), deps=[t_ldc])
        t_b1 = s.op("dve", lambda e: e.tensor_scalar(
            out=b1[:], in0=b1[:], scalar1=0.5, scalar2=None, op0=ALU.mult),
            deps=[t_ldc])

        xt_v = xt.rearrange("(c p) n -> p c n", p=128)
        yt_v = yt.rearrange("(c p) n -> p c n", p=128)
        t_x = [None] * NSLAB
        t_x[0] = dma(rbuf[0][:].rearrange("p (c n) -> p c n", n=SLAB),
                     xt_v[:, :, 0:SLAB], "ldx0")

        A_sl = stg[0:4, 0:1024]
        B_sl = stg[0:100, 1024:2048]
        w1t_sl = stg[:, 2048:3072]

        t_A_mm = s.op("pe", lambda e: (
            e.matmul(out=pbank[0][0:4, :], lhsT=lora_up[:], rhs=upa[0:50, 0:512],
                     start=True, stop=True),
            e.matmul(out=pbank[1][0:4, :], lhsT=lora_up[:], rhs=upa[0:50, 512:1024],
                     start=True, stop=True))[-1], deps=d_lora[:2] + [d_lora[3]])
        t_A = s.op("dve", lambda e: (
            e.tensor_copy(out=A_sl[:, 0:512], in_=pbank[0][0:4, :]),
            e.tensor_copy(out=A_sl[:, 512:1024], in_=pbank[1][0:4, :]))[-1],
            deps=[t_A_mm])
        t_B_mm = s.op("pe", lambda e: (
            e.matmul(out=pbank[0][0:100, :], lhsT=lora_dn[:], rhs=A_sl[:, 0:512],
                     start=True, stop=True),
            e.matmul(out=pbank[1][0:100, :], lhsT=lora_dn[:], rhs=A_sl[:, 512:1024],
                     start=True, stop=True))[-1], deps=[t_A, d_lora[1]])
        t_B = s.op("dve", lambda e: (
            e.tensor_copy(out=B_sl[:, 0:512], in_=pbank[0][0:100, :]),
            e.tensor_copy(out=B_sl[:, 512:1024], in_=pbank[1][0:100, :]))[-1],
            deps=[t_B_mm])
        t_weff_parts = []
        for kc in range(NCH):
            slw = w1t_sl if kc % 2 == 0 else w1t_sl2
            war = [t_weff_parts[kc - 2]] if kc >= 2 else []
            d_w1 = dma(slw, w1t.rearrange("(c p) m -> c p m", p=128)[kc],
                       "ldw1" if kc % 2 == 0 else "ldq1", deps=war)
            t_w1h = s.op("dve", lambda e, slw=slw: e.tensor_scalar(
                out=slw, in0=slw, scalar1=0.5, scalar2=None, op0=ALU.mult),
                deps=[d_w1])
            cbk0 = [pbank[0], pbank[1], pscore, pctx][(2 * kc) % 4]
            cbk1 = [pbank[0], pbank[1], pscore, pctx][(2 * kc + 1) % 4]
            t_c = s.op("pe", lambda e, kc=kc, cbk0=cbk0, cbk1=cbk1: (
                e.matmul(out=cbk0[:], lhsT=dna[0:100, kc * 128:(kc + 1) * 128],
                         rhs=B_sl[:, 0:512], start=True, stop=True),
                e.matmul(out=cbk1[:], lhsT=dna[0:100, kc * 128:(kc + 1) * 128],
                         rhs=B_sl[:, 512:1024], start=True, stop=True))[-1],
                deps=[t_B, d_lora[2]] + ([t_weff_parts[kc - 2]] if kc >= 2 else []))
            t_we = s.op("dve", lambda e, kc=kc, slw=slw, cbk0=cbk0, cbk1=cbk1: (
                e.scalar_tensor_tensor(
                    out=weffb[:, kc * 1024:kc * 1024 + 512], in0=cbk0[:],
                    scalar=0.5, in1=slw[:, 0:512], op0=ALU.mult, op1=ALU.add),
                e.scalar_tensor_tensor(
                    out=weffb[:, kc * 1024 + 512:(kc + 1) * 1024], in0=cbk1[:],
                    scalar=0.5, in1=slw[:, 512:1024], op0=ALU.mult, op1=ALU.add))[-1],
                deps=[t_c, t_w1h])
            t_weff_parts.append(t_we)
        t_weffb = t_weff_parts[-1]

        wqkv_v = wqkv.rearrange("(c p) m -> c p m", p=128)
        wout_v = wout.rearrange("(c p) m -> c p m", p=128)
        for kc in range(NCH):
            t_wqkvb = s.op("pool", lambda e, kc=kc: e.dma_start(
                out=wqkvb[:, kc * 3072:(kc + 1) * 3072], in_=wqkv_v[kc]),
                deps=[t_ldc] if kc == 0 else [], sem="ldq0", inc=16)
        for kc in range(NCH):
            t_woutb = s.op("pool", lambda e, kc=kc: e.dma_start(
                out=woutb[:, kc * 1024:(kc + 1) * 1024], in_=wout_v[kc]),
                deps=[], sem="ldwo", inc=16)

        bank_free = {"pscore": [t_woutb], "pstat0": [t_B_mm], "pstat1": [t_B_mm],
                     "pctx": [t_woutb], "pbank0": [t_woutb], "pbank1": [t_woutb]}
        proj_banks = [("pbank0", pbank[0]), ("pbank1", pbank[1]),
                      ("pscore", pscore), ("pctx", pctx)]
        t_xhat_free = [t_ones]
        t_hbuf_free = [t_ones]
        t_rowb_free = [t_ones]
        t_rows_free = [t_woutb]
        t_qkvg_free = [t_ones]
        t_ebuf_free = [[t_ones], [t_ones]]
        t_vb1_free = [[t_v1[0]], [t_v1[1]]]
        t_ctxg_free = [[t_ones], [t_ones]]
        t_rcol_free = [[t_ones], [t_ones]]
        t_d4_free = [[t_ones], [t_ones]]
        t_pvgp_free = [t_ones]
        t_pc2_free = [t_ones]
        t_c2_free = [t_weffb]
        t_store = [None] * NSLAB

        ln_calls = [0]
        t_x2b_free = [t_ones]

        def ln_front_start():
            return dict(cast=[], x2=[], mm_s=None, mm_q=None)

        def ln_front_square(st, c):
            nonlocal t_x2b_free
            t_c = st["cast"][c]
            t_q = s.op("dve", lambda e, c=c: e.tensor_tensor(
                out=x2b[:, c * SLAB:(c + 1) * SLAB],
                in0=xhat[:, c * SLAB:(c + 1) * SLAB],
                in1=xhat[:, c * SLAB:(c + 1) * SLAB], op=ALU.mult),
                deps=([t_c] if not isinstance(t_c, list) else t_c)
                     + (list(t_x2b_free) if c == 0 else []))
            st["x2"].append(t_q)

        def ln_front_chunk(st, c, tok, src, pre_cast=None, sq=True):
            nonlocal t_xhat_free
            if pre_cast is not None:
                t_c = pre_cast
            else:
                extra = list(t_xhat_free) if c == 0 else []
                t_c = s.op("dve", lambda e, c=c: e.tensor_copy(
                    out=xhat[:, c * SLAB:(c + 1) * SLAB],
                    in_=src[:, c * SLAB:(c + 1) * SLAB]),
                    deps=[tok] + extra)
            st["cast"].append(t_c)
            if sq:
                ln_front_square(st, c)

        def ln_front_finish(st, which="sq"):
            for c in range(NCH):
                cst = st["cast"][c]
                dep_s = (cst if isinstance(cst, list) else [cst]) + [t_ones]
                if c == 0:
                    dep_s += bank_free["pstat0"]
                if "s" in which:
                    st["mm_s"] = s.op("pe", lambda e, c=c: e.matmul(
                        out=pstat[0][:], lhsT=ones128[:],
                        rhs=xhat[:, c * SLAB:(c + 1) * SLAB],
                        start=(c == 0), stop=(c == NCH - 1), skip_group_check=True),
                        deps=dep_s, inc=(1 if c == NCH - 1 else 0))
            for c in range(NCH):
                if "q" not in which:
                    break
                dep_q = [st["x2"][c]]
                if c == 0:
                    dep_q += bank_free["pstat1"]
                if True:
                    st["mm_q"] = s.op("pe", lambda e, c=c: e.matmul(
                        out=pstat[1][:], lhsT=ones128[:],
                        rhs=x2b[:, c * SLAB:(c + 1) * SLAB],
                        start=(c == 0), stop=(c == NCH - 1), skip_group_check=True),
                        deps=dep_q, inc=(1 if c == NCH - 1 else 0))
            return st

        def ln_front(chunk_toks, src):
            if not isinstance(chunk_toks, list):
                chunk_toks = [chunk_toks] * NCH
            st = ln_front_start()
            for c in range(NCH):
                ln_front_chunk(st, c, chunk_toks[c], src)
            ln_front_finish(st)
            return st

        def ln_back(fr, dst=None, dst_war=None, per_chunk=False):
            nonlocal t_xhat_free, t_rowb_free, t_rows_free, t_x2b_free
            ln_calls[0] += 1
            t_mm_s, t_mm_q = fr["mm_s"], fr["mm_q"]
            t_cast = fr["cast"][-1]
            if isinstance(t_cast, list):
                t_cast = t_cast[-1]
            t_rm = s.op("dve", lambda e: e.tensor_scalar(
                out=rowb[:, 0:512], in0=pstat[0][:], scalar1=1.0 / 1024.0,
                scalar2=None, op0=ALU.mult), deps=[t_mm_s] + t_rowb_free)
            t_msq = s.op("dve", lambda e: e.tensor_tensor(
                out=rows_t, in0=rowb[:, 0:512], in1=rowb[:, 0:512],
                op=ALU.mult), deps=[t_rm] + t_rows_free)
            t_var = s.op("dve", lambda e: e.scalar_tensor_tensor(
                out=rows_v, in0=pstat[1][:], scalar=ivdc[:, 0:1], in1=rows_t,
                op0=ALU.mult, op1=ALU.subtract),
                deps=[t_mm_q, t_msq] + t_rows_free)
            t_ln = s.op("act", lambda e: e.activation(
                out=rows_m, in_=rows_v, func=AF.Ln, bias=epsc[:, 0:1], scale=1.0),
                deps=[t_var, t_eps] + t_rows_free)
            t_rstd = s.op("act", lambda e: e.activation(
                out=rowb[:, 512:1024], in_=rows_m, func=AF.Exp, bias=0.0,
                scale=nhc[:, 0:1]), deps=[t_ln, t_nh] + t_rowb_free)
            xv = xhat[:].rearrange("p (c n) -> p c n", n=SLAB)
            dstb = hbuf if dst is None else dst
            war = list(t_hbuf_free) if dst_war is None else list(dst_war)
            hv = dstb[:].rearrange("p (c n) -> p c n", n=SLAB)
            t_apps = []
            t_subs = []
            if per_chunk:
                for c in range(NCH):
                    cst = fr["cast"][c]
                    cd = cst if isinstance(cst, list) else [cst]
                    t_sub = s.op("dve", lambda e, c=c: e.tensor_tensor(
                        out=dstb[:, c * SLAB:(c + 1) * SLAB],
                        in0=xhat[:, c * SLAB:(c + 1) * SLAB],
                        in1=rowb[:, 0:512], op=ALU.subtract),
                        deps=[t_rm] + cd + (war if c == 0 else []))
                    t_subs.append(t_sub)
                for c in range(NCH):
                    t_app = s.op("dve", lambda e, c=c: e.tensor_tensor(
                        out=dstb[:, c * SLAB:(c + 1) * SLAB],
                        in0=dstb[:, c * SLAB:(c + 1) * SLAB],
                        in1=rowb[:, 512:1024], op=ALU.mult),
                        deps=[t_subs[c], t_rstd])
                    t_apps.append(t_app)
            else:
                for hh in range(2):
                    ho = hh * HALF
                    mb = rowb[:, ho:ho + HALF].unsqueeze(1).broadcast_to(
                        [128, NCH, HALF])
                    t_sub = s.op("dve", lambda e, ho=ho, mb=mb: e.tensor_tensor(
                        out=hv[:, :, ho:ho + HALF], in0=xv[:, :, ho:ho + HALF],
                        in1=mb, op=ALU.subtract),
                        deps=[t_rm, t_cast] + (war if hh == 0 else []))
                    t_subs.append(t_sub)
                for hh in range(2):
                    ho = hh * HALF
                    rb = rowb[:, 512 + ho:512 + ho + HALF].unsqueeze(1).broadcast_to(
                        [128, NCH, HALF])
                    t_app = s.op("dve", lambda e, ho=ho, rb=rb: e.tensor_tensor(
                        out=hv[:, :, ho:ho + HALF], in0=hv[:, :, ho:ho + HALF],
                        in1=rb, op=ALU.mult), deps=[t_subs[hh], t_rstd])
                    t_apps.append(t_app)
            bank_free["pstat0"] = [t_rm, t_msq]
            bank_free["pstat1"] = [t_var]
            t_xhat_free = [t_subs[-1]]
            t_x2b_free = [t_mm_q]
            t_rowb_free = [t_subs[-1], t_apps[-1]]
            t_rows_free = [t_var, t_ln, t_rstd]
            return t_apps

        def projection(t_h, t_w, wbuf, wbase_stride, nout, bias, t_bias, resid_in,
                       t_resid_in, resid_out, t_resid_out_war, operand=None,
                       ln_next_src=None, ln_next_tok=None, pe_mid=None):
            toks = []
            t_mm = None
            fr = ln_front_start() if ln_next_src is not None else None
            nb = len(proj_banks)
            opnd = hbuf if operand is None else operand
            per_kc = isinstance(t_h, list) and len(t_h) == NCH
            for oc in range(nout):
                bname, bk = proj_banks[oc % nb]
                if per_kc:
                    for kc in range(NCH):
                        t_mm = s.op("pe", lambda e, oc=oc, bk=bk, kc=kc: e.matmul(
                            out=bk[:],
                            lhsT=wbuf[:, kc * wbase_stride + oc * 128:
                                      kc * wbase_stride + oc * 128 + 128],
                            rhs=opnd[:, kc * SLAB:(kc + 1) * SLAB],
                            start=(kc == 0), stop=(kc == NCH - 1),
                            skip_group_check=True),
                            deps=[t_h[kc], t_w]
                                 + (bank_free[bname] if kc == 0 else []),
                            inc=(1 if kc == NCH - 1 else 0))
                else:
                    t_mm = s.op("pe", lambda e, oc=oc, bk=bk: [
                        e.matmul(out=bk[:],
                                 lhsT=wbuf[:, kc * wbase_stride + oc * 128:
                                           kc * wbase_stride + oc * 128 + 128],
                                 rhs=opnd[:, kc * SLAB:(kc + 1) * SLAB],
                                 start=(kc == 0), stop=(kc == NCH - 1))
                        for kc in range(NCH)][-1],
                        deps=[t_h, t_w] + bank_free[bname])
                t_pc = None
                if fr is not None and ln_next_tok is None:
                    xw = list(t_xhat_free) if oc == 0 else []
                    t_pc = s.op("dve", lambda e, oc=oc, bk=bk: e.scalar_tensor_tensor(
                        out=xhat[:, oc * SLAB:(oc + 1) * SLAB],
                        in0=bk[:], scalar=bias[:, oc:oc + 1],
                        in1=resid_in[:, oc * SLAB:(oc + 1) * SLAB],
                        op0=ALU.add, op1=ALU.add),
                        deps=[t_mm, t_resid_in, t_bias] + xw)
                t_ev = s.op("dve", lambda e, oc=oc, bk=bk: e.scalar_tensor_tensor(
                    out=resid_out[:, oc * SLAB:(oc + 1) * SLAB],
                    in0=bk[:], scalar=bias[:, oc:oc + 1],
                    in1=resid_in[:, oc * SLAB:(oc + 1) * SLAB],
                    op0=ALU.add, op1=ALU.add),
                    deps=[t_mm, t_resid_in, t_bias] + t_resid_out_war)
                bank_free[bname] = [t_ev]
                toks.append(t_ev)
                if fr is not None:
                    ln_front_chunk(fr, oc,
                                   (ln_next_tok if ln_next_tok is not None else t_ev),
                                   ln_next_src, pre_cast=t_pc)
                if pe_mid is not None and oc == 1:
                    pe_mid(0)
                if pe_mid is not None and oc == 3:
                    pe_mid(1)
            if fr is not None:
                ln_front_finish(fr)
            return toks, t_mm, fr

        def qkv_and_attention(t_h, slab, resid_y, t_y, resid_z, z_war, bo_, t_ln3_src):
            nonlocal t_qkvg_free, t_pvgp_free, t_pc2_free
            nonlocal t_ebuf_free, t_vb1_free, t_ctxg_free, t_rcol_free, t_d4_free
            last_qkv_mm = [None]
            dbg_qkv_toks = []
            t_c2 = [None]
            c2_half = {}
            ev2 = [None] * NCH
            ev2_h1 = [None] * NCH
            pc3 = [[None] * NCH, [None] * NCH]
            fr3 = ln_front_start()
            nb = len(proj_banks)

            def wout_half(half, c2toks, with_ln):
                hoff = half * HALF
                for oc in range(NCH):
                    bname, bk = proj_banks[oc % nb]
                    t_mm = s.op("pe", lambda e, oc=oc, bk=bk, hoff=hoff: [
                        e.matmul(out=bk[:, 0:HALF],
                                 lhsT=woutb[:, j * 1024 + oc * 128:j * 1024 + oc * 128 + 128],
                                 rhs=c2sb[:, j * SLAB + hoff:j * SLAB + hoff + HALF],
                                 start=(j == 0), stop=(j == NCH - 1))
                        for j in range(NCH)][-1],
                        deps=c2toks + [t_woutb] + bank_free[bname])
                    if with_ln or half == 0:
                        xw = list(t_xhat_free) if (half == 0 and oc == 0) else []
                        t_pc = s.op("dve", lambda e, oc=oc, bk=bk, hoff=hoff:
                                    e.scalar_tensor_tensor(
                            out=xhat[:, oc * SLAB + hoff:oc * SLAB + hoff + HALF],
                            in0=bk[:, 0:HALF], scalar=bo_[:, oc:oc + 1],
                            in1=resid_y[:, oc * SLAB + hoff:oc * SLAB + hoff + HALF],
                            op0=ALU.add, op1=ALU.add),
                            deps=[t_mm, t_y, t_ldc] + xw)
                        pc3[half][oc] = t_pc
                    t_ev = s.op("dve", lambda e, oc=oc, bk=bk, hoff=hoff:
                                e.scalar_tensor_tensor(
                        out=resid_z[:, oc * SLAB + hoff:oc * SLAB + hoff + HALF],
                        in0=bk[:, 0:HALF], scalar=bo_[:, oc:oc + 1],
                        in1=resid_y[:, oc * SLAB + hoff:oc * SLAB + hoff + HALF],
                        op0=ALU.add, op1=ALU.add),
                        deps=[t_mm, t_y, t_ldc] + z_war)
                    bank_free[bname] = [t_ev]
                    if half == 0:
                        ev2_h1[oc] = t_ev
                    else:
                        ev2[oc] = t_ev
                        if with_ln:
                            ln_front_chunk(fr3, oc, None, t_ln3_src,
                                           pre_cast=[pc3[0][oc], pc3[1][oc]])
                if half == 1 and with_ln:
                    ln_front_finish(fr3)

            qkv_banks = proj_banks + [("pstat0", pstat[0]), ("pstat1", pstat[1])]
            for half in range(2):
                hoff = half * HALF
                ev_toks_q, ev_toks_kv = [], []
                for oc in range(24):
                    bname, bk = qkv_banks[oc % 6]
                    t_mm = s.op("pe", lambda e, oc=oc, bk=bk, hoff=hoff: [
                        e.matmul(out=bk[:, 0:HALF],
                                 lhsT=wqkvb[:, kc * 3072 + oc * 128:
                                            kc * 3072 + oc * 128 + 128],
                                 rhs=hbuf[:, kc * SLAB + hoff:kc * SLAB + hoff + HALF],
                                 start=(kc == 0), stop=(kc == NCH - 1))
                        for kc in range(NCH)][-1],
                        deps=[t_h[half] if isinstance(t_h, list) else t_h,
                              t_wqkvb] + bank_free[bname])
                    last_qkv_mm[0] = t_mm
                    kind = oc // 8
                    c = oc % 8
                    if kind == 0:
                        dst = qg
                        offs = (8 * c, 64 + 8 * c)
                    elif kind == 1:
                        dst = kg
                        offs = (16 * c, 16 * c + 8)
                    else:
                        dst = vg
                        offs = (16 * c, 16 * c + 8)
                    dstv = dst[:].rearrange("d (g c) -> d g c", c=128)
                    if kind == 0:
                        t_ev = s.op("act", lambda e, bk=bk, oc=oc, dstv=dstv, offs=offs: (
                            e.activation(out=dstv[:, :, offs[0]:offs[0] + 8],
                                         in_=bk[0:64, 0:HALF].rearrange(
                                             "d (g t) -> d g t", t=8),
                                         func=AF.Identity, bias=bq[0:64, oc:oc + 1],
                                         scale=1.0),
                            e.activation(out=dstv[:, :, offs[1]:offs[1] + 8],
                                         in_=bk[64:128, 0:HALF].rearrange(
                                             "d (g t) -> d g t", t=8),
                                         func=AF.Identity, bias=bq[64:128, oc:oc + 1],
                                         scale=1.0))[-1],
                            deps=[t_mm, t_bq] + t_qkvg_free, inc=2)
                        ev_toks_q.append(t_ev)
                    else:
                        sc = ksc[:, 0:1] if kind == 1 else 1.0
                        t_ev = s.op("act", lambda e, bk=bk, oc=oc, dstv=dstv,
                                    offs=offs, sc=sc: (
                            e.activation(out=dstv[:, :, offs[0]:offs[0] + 8],
                                         in_=bk[0:64, 0:HALF].rearrange(
                                             "d (g t) -> d g t", t=8),
                                         func=AF.Identity, bias=bq[0:64, oc:oc + 1],
                                         scale=(sc if isinstance(sc, float)
                                                else ksc[0:64, 0:1])),
                            e.activation(out=dstv[:, :, offs[1]:offs[1] + 8],
                                         in_=bk[64:128, 0:HALF].rearrange(
                                             "d (g t) -> d g t", t=8),
                                         func=AF.Identity, bias=bq[64:128, oc:oc + 1],
                                         scale=(sc if isinstance(sc, float)
                                                else ksc[64:128, 0:1])))[-1],
                            deps=[t_mm, t_bq, t_ksc] + t_qkvg_free, inc=2)
                        ev_toks_kv.append(t_ev)
                    bank_free[bname] = [t_ev]
                t_q_ready = ev_toks_q[-1]
                t_kv_ready = ev_toks_kv[-1]
                if DEBUG and slab == 0 and half == 0:
                    t_dq = dma(dbg_qg, qg[:], "dbg2", deps=[t_q_ready])
                    t_dk = dma(dbg_kg, kg[:], "dbg3", deps=[t_kv_ready])
                    t_dv = dma(dbg_vg, vg[:], "dbg4", deps=[t_kv_ready])
                    dbg_qkv_toks.extend([t_dq, t_dk, t_dv])

                if half == 1:
                    wout_half(0, c2_half[0], with_ln=False)

                sc_banks = [(pscore, "pscore"), (pstat[0], "pstat0")]
                ctx_banks = [(pctx, "pctx"), (pstat[1], "pstat1")]
                t_sc = {}
                t_exp = {}
                t_mask = {}
                t_vt = {}
                t_vev = {}
                t_ctx = {}
                t_rc = {}
                t_d4 = {}
                t_cge = {}
                t_ct = {}
                half_c2 = []
                NB = NBATCH_H
                for b in range(NB + 2):
                    par = b % 2
                    if b < NB:
                        g0 = b * 4
                        sb, sbname = sc_banks[par]
                        war_sc = list(bank_free[sbname])
                        t_sc[b] = s.op("pe", lambda e, g0=g0, sb=sb: [
                            e.matmul(out=sb[:, i * 128:(i + 1) * 128],
                                     lhsT=kg[:, (g0 + i) * 128:(g0 + i + 1) * 128],
                                     rhs=qg[:, (g0 + i) * 128:(g0 + i + 1) * 128],
                                     start=True, stop=True) for i in range(4)][-1],
                            deps=[t_q_ready, t_kv_ready] + war_sc, inc=4)
                        war_vt = [t_vev[b - 1]] if b >= 1 else t_pvgp_free
                        t_vt[b] = s.op("pe", lambda e, g0=g0: [
                            e.transpose(out=pvgp[:, i * 64:(i + 1) * 64],
                                        in_=vg[:, (g0 + i) * 128:(g0 + i + 1) * 128],
                                        identity=idb[0:64, :]) for i in range(4)][-1],
                            deps=[t_kv_ready, t_idb] + war_vt, inc=4)
                        war_e = [t_ctx[b - 2]] if b >= 2 else t_ebuf_free[par]
                        t_exp[b] = s.op("act", lambda e, par=par, sb=sb: e.activation(
                            out=ebuf[par][:], in_=sb[:], func=AF.Exp),
                            deps=[t_sc[b]] + war_e)
                        bank_free[sbname] = [t_exp[b]]
                        t_mask[b] = s.op("dve", lambda e, par=par: e.tensor_tensor(
                            out=ebuf[par][:].rearrange("p (g c) -> p g c", c=128),
                            in0=ebuf[par][:].rearrange("p (g c) -> p g c", c=128),
                            in1=maskb[:].unsqueeze(1).broadcast_to([128, 4, 128]),
                            op=ALU.mult), deps=[t_exp[b], t_maskb])
                        war_v = [t_ctx[b - 2]] if b >= 2 else t_vb1_free[par]
                        t_vev[b] = s.op("dve", lambda e, par=par: e.tensor_copy(
                            out=vb1[par][:].rearrange("p (g c) -> p g c", c=65)[:, :, 0:64],
                            in_=pvgp[:, 0:256].rearrange("p (g c) -> p g c", c=64)),
                            deps=[t_vt[b]] + war_v)
                    bb = b - 1
                    if 0 <= bb < NB:
                        par = bb % 2
                        cb, cbname = ctx_banks[par]
                        war_c = list(bank_free[cbname])
                        t_ctx[bb] = s.op("pe", lambda e, par=par, cb=cb: [
                            e.matmul(out=cb[:, i * 65:(i + 1) * 65],
                                     lhsT=ebuf[par][:, i * 128:(i + 1) * 128],
                                     rhs=vb1[par][:, i * 65:(i + 1) * 65],
                                     start=True, stop=True) for i in range(4)][-1],
                            deps=[t_mask[bb], t_vev[bb]] + war_c, inc=4)
                        t_rc[bb] = s.op("dve", lambda e, par=par, cb=cb: e.reciprocal(
                            out=rcol[par][:], in_=cb[:, 0:260].rearrange(
                                "p (g c) -> p g c", c=65)[:, :, 64]),
                            deps=[t_ctx[bb]] + ([t_d4[bb - 2]] if bb >= 2
                                                else t_rcol_free[par]))
                        t_d4[bb] = s.op("dve", lambda e, par=par: [
                            e.tensor_scalar(out=d4[par][:, i * 64:(i + 1) * 64],
                                            in0=idb[:], scalar1=rcol[par][:, i:i + 1],
                                            scalar2=None, op0=ALU.mult)
                            for i in range(4)][-1],
                            deps=[t_rc[bb], t_idb] + ([t_ct[bb - 2]] if bb >= 2
                                                       else t_d4_free[par]),
                            inc=4)
                        war_g = [t_ct[bb - 2]] if bb >= 2 else t_ctxg_free[par]
                        t_cge[bb] = s.op("act", lambda e, par=par, cb=cb: e.activation(
                            out=ctxg[par][:].rearrange("p (g c) -> p g c", c=64),
                            in_=cb[:, 0:260].rearrange("p (g c) -> p g c", c=65)[:, :, 0:64],
                            func=AF.Identity, bias=0.0, scale=1.0),
                            deps=[t_ctx[bb]] + war_g)
                        bank_free[cbname] = [t_rc[bb], t_cge[bb]]
                    bc = b - 2
                    if 0 <= bc < NB:
                        par = bc % 2
                        war_t = [t_c2[0]] if t_c2[0] is not None else t_pc2_free
                        t_ct[bc] = s.op("pe", lambda e, par=par: [
                            e.matmul(out=pc2[(0 if pp == 0 else 64):(64 if pp == 0 else 128),
                                             i * 64:(i + 1) * 64],
                                     lhsT=ctxg[par][(0 if pp == 0 else 64):(64 if pp == 0 else 128),
                                                    i * 64:(i + 1) * 64],
                                     rhs=d4[par][(0 if pp == 0 else 64):(64 if pp == 0 else 128),
                                                 i * 64:(i + 1) * 64],
                                     start=True, stop=True,
                                     tile_position=(None if pp == 0 else (64, 64)))
                            for i in range(4) for pp in range(2)][-1],
                            deps=[t_cge[bc], t_d4[bc]] + war_t, inc=8)
                        coff = hoff + bc * 32
                        t_c2[0] = s.op("dve", lambda e, coff=coff: e.tensor_copy(
                            out=c2sb[:].rearrange("p (j n) -> p j n", n=SLAB)[
                                :, :, coff:coff + 32].rearrange("p j (g t) -> p j g t", t=8),
                            in_=pc2[:, 0:256].rearrange("p (g j t) -> p j g t", g=4, j=8, t=8)),
                            deps=[t_ct[bc]] + t_c2_free)
                        half_c2.append(t_c2[0])
                t_pvgp_free = [t_vev[NB - 1]]
                t_pc2_free = [t_c2[0]]
                for p in range(2):
                    lastb = NB - 2 + p
                    t_ebuf_free[p] = [t_ctx[lastb]]
                    t_vb1_free[p] = [t_ctx[lastb]]
                    t_ctxg_free[p] = [t_ct[lastb]]
                    t_rcol_free[p] = [t_d4[lastb]]
                    t_d4_free[p] = [t_ct[lastb]]
                t_qkvg_free = [t_ct[NB - 1]] + dbg_qkv_toks
                dbg_qkv_toks = []
                c2_half[half] = [half_c2[-1]]
            wout_half(1, c2_half[1], with_ln=True)
            return ev2, fr3, last_qkv_mm[0], c2_half

        ev3_prev = []
        fr1_next = None
        t_h1_next = None
        for sl in range(NSLAB):
            A = rbuf[sl % 3]
            B = rbuf[(sl + 2) % 3]
            tA = t_x[sl]
            b_war = list(ev3_prev)
            if sl == 0:
                fr1 = ln_front(tA, A)
                t_h1 = ln_back(fr1, dst=c2sb, dst_war=t_c2_free, per_chunk=True)
            else:
                t_h1 = t_h1_next
            ev1, t_mm1, fr2 = projection(t_h1, t_weffb, weffb, 1024, NCH, b1, t_b1,
                                         A, tA, B, b_war, operand=c2sb,
                                         ln_next_src=B)
            t_y = ev1[-1]
            t_c2_free = [t_mm1]
            if DEBUG and sl == 0:
                t_d = dma(dbg_y, B[:], "dbg0", deps=ev1)
                ev1 = ev1 + [t_d]
            t_h2 = ln_back(fr2)
            if DEBUG and sl == 0:
                t_dh = dma(dbg_h2, hbuf[:], "dbg1", deps=[t_h2])
                t_hbuf_free_extra = [t_dh]
            else:
                t_hbuf_free_extra = []
            ev2, fr3, t_mmq, c2_half = qkv_and_attention(
                t_h2, sl, B, t_y, A, ev1, bo, A)
            t_hbuf_free = [t_mmq] + t_hbuf_free_extra
            t_z = ev2[-1]
            t_c2_free = list(ev2)
            if DEBUG and sl == 0:
                t_dc2 = dma(dbg_c2, c2sb[:], "dbg5", deps=c2_half[1])
                t_dz = dma(dbg_z, A[:], "dbg6", deps=ev2)
                t_c2_free = t_c2_free + [t_dc2]
                ev2 = ev2 + [t_dz]
            t_h3 = ln_back(fr3, per_chunk=True)
            if sl + 1 < NSLAB:
                pf_deps = [t_store[sl - 1]] if sl >= 1 else []
                t_x[sl + 1] = dma(rbuf[(sl + 1) % 3][:].rearrange(
                    "p (c n) -> p c n", n=SLAB),
                    xt_v[:, :, (sl + 1) * SLAB:(sl + 2) * SLAB],
                    "ldx%d" % ((sl + 1) % 2), deps=pf_deps)
            pe_mid = None
            if sl + 1 < NSLAB:
                nxt = rbuf[(sl + 1) % 3]
                fr1n = ln_front_start()
                for c in range(NCH):
                    ln_front_chunk(fr1n, c, t_x[sl + 1], nxt, sq=False)
                for c in range(NCH):
                    ln_front_square(fr1n, c)
                h1n_holder = []

                h1n_holder = []

                def pe_mid(phase, fr1n=fr1n, h1n_holder=h1n_holder,
                           c2w=list(t_c2_free)):
                    if phase == 0:
                        ln_front_finish(fr1n, which="s")
                    else:
                        ln_front_finish(fr1n, which="q")
                        h1n_holder.append(ln_back(fr1n, dst=c2sb, dst_war=c2w,
                                                  per_chunk=True))
            ev3, t_mm3, _ = projection(t_h3, t_weffb, weffb, 1024, NCH, b1, t_b1,
                                       A, t_z, B, ev2, pe_mid=pe_mid)
            if sl + 1 < NSLAB:
                t_h1_next = h1n_holder[0]
            t_hbuf_free = [t_mm3]
            ev3_prev = ev3
            t_store[sl] = dma(
                yt_v[:, :, sl * SLAB:(sl + 1) * SLAB],
                B[:].rearrange("p (c n) -> p c n", n=SLAB),
                "st%d" % (sl % 2), deps=ev3)

        fin_deps = [("st0", s.counts["st0"]), ("st1", s.counts["st1"])]
        if DEBUG:
            fin_deps += [(d, s.counts[d]) for d in DMASEMS if d.startswith("dbg")
                         and s.counts[d] > 0]
        s.op("sp", lambda e: None, deps=fin_deps, inc=0)

        @block.sync
        def _(sync):
            s.replay(sync, "sp", sem_handles)

        @block.vector
        def _(vector):
            s.replay(vector, "dve", sem_handles)

        @block.tensor
        def _(tensor):
            s.replay(tensor, "pe", sem_handles)

        @block.scalar
        def _(scalar):
            s.replay(scalar, "act", sem_handles)

        @block.gpsimd
        def _(gpsimd):
            s.replay(gpsimd, "pool", sem_handles)

    return nc


def _host_inputs(L0_attn_x, L0_attn_W_qkv, L0_attn_b_qkv, L0_attn_W_out, L0_attn_b_out,
                 L0_ffn_W, L0_ffn_b, L0_ffn_lora_down, L0_ffn_lora_up,
                 L0_ffn_down_aux, L0_ffn_up_aux):
    x = np.asarray(L0_attn_x, dtype=np.float32).reshape(NCORES * TCORE, D)
    kindv, hv, dv = np.meshgrid(np.arange(3), np.arange(16), np.arange(64),
                                indexing="ij")
    perm = (192 * hv + 64 * kindv + dv).reshape(-1)
    wqkv = np.ascontiguousarray(np.asarray(L0_attn_W_qkv, dtype=np.float32)[:, perm])
    w1t = np.ascontiguousarray(np.asarray(L0_ffn_W, dtype=np.float32).T)
    wout = np.ascontiguousarray(np.asarray(L0_attn_W_out, dtype=np.float32))
    upat = np.ascontiguousarray(np.asarray(L0_ffn_up_aux, dtype=np.float32).T)
    lup = np.ascontiguousarray(np.asarray(L0_ffn_lora_up, dtype=np.float32))
    ldn = np.ascontiguousarray(np.asarray(L0_ffn_lora_down, dtype=np.float32))
    dnaux = np.ascontiguousarray(np.asarray(L0_ffn_down_aux, dtype=np.float32))
    bqd = np.ascontiguousarray(
        np.asarray(L0_attn_b_qkv, dtype=np.float32)[perm].reshape(24, 128).T)
    b1d = np.ascontiguousarray(
        np.asarray(L0_ffn_b, dtype=np.float32).reshape(8, 128).T)
    bod = np.ascontiguousarray(
        np.asarray(L0_attn_b_out, dtype=np.float32).reshape(8, 128).T)
    mask = np.zeros((128, 128), dtype=np.float32)
    for g in range(16):
        for t in range(8):
            for p in range(2):
                for j in range(8):
                    mask[g * 8 + t, p * 64 + j * 8 + t] = 1.0
    ident = np.zeros((128, 64), dtype=np.float32)
    ident[0:64] = np.eye(64, dtype=np.float32)
    ident[64:128] = np.eye(64, dtype=np.float32)
    shared = dict(wqkv=wqkv, w1t=w1t, wout=wout, upat=upat, lup=lup, ldn=ldn,
                  dnaux=dnaux, bqd=bqd, b1d=b1d, bod=bod, maskd=mask, identd=ident)
    in_maps = []
    for c in range(NCORES):
        shard = x[c * TCORE:(c + 1) * TCORE]
        xt = np.ascontiguousarray(shard.T)
        m = dict(shared)
        m["xt"] = xt
        in_maps.append(m)
    return in_maps


def kernel(L0_attn_x, L0_attn_W_qkv, L0_attn_b_qkv, L0_attn_W_out, L0_attn_b_out,
           L0_ffn_W, L0_ffn_b, L0_ffn_lora_down, L0_ffn_lora_up,
           L0_ffn_down_aux, L0_ffn_up_aux):
    in_maps = _host_inputs(L0_attn_x, L0_attn_W_qkv, L0_attn_b_qkv, L0_attn_W_out,
                           L0_attn_b_out, L0_ffn_W, L0_ffn_b, L0_ffn_lora_down,
                           L0_ffn_lora_up, L0_ffn_down_aux, L0_ffn_up_aux)
    nc = build_nc()
    res = run_bass_kernel_spmd(nc, in_maps, core_ids=list(range(NCORES)))
    outs = []
    for c in range(NCORES):
        yt = res.results[c]["yt"]
        outs.append(np.ascontiguousarray(yt.T))
    full = np.concatenate(outs, axis=0)
    return full.reshape(np.asarray(L0_attn_x).shape).astype(np.float32)


# version 43 (final)
# speedup vs baseline: 1.0539x; 1.0080x over previous
import numpy as np
from contextlib import ExitStack

import concourse.bass as bass
import concourse.mybir as mybir
from concourse.bass_utils import run_bass_kernel_spmd

F32 = mybir.dt.float32
BF16 = mybir.dt.bfloat16
AF = mybir.ActivationFunctionType
ALU = mybir.AluOpType

NCORES = 8
TCORE = 2048
SLAB = 512
NSLAB = TCORE // SLAB
HALF = 256
NGRP_H = HALF // 8
NBATCH_H = NGRP_H // 4
D = 1024
NCH = 8
EPS = 1e-5

DEBUG = 0

ENGS = ("pe", "dve", "act", "pool", "sp")
DMASEMS = ("ldc", "ldw1", "ldq0", "ldq1", "ldwo", "ldx0", "ldx1", "st0", "st1",
           "dbg0", "dbg1", "dbg2", "dbg3", "dbg4", "dbg5", "dbg6", "dbg7", "dbg8")


class Sched:

    def __init__(self):
        self.q = {e: [] for e in ENGS}
        self.counts = {}
        for e in ENGS:
            self.counts[e] = 0
        for e in DMASEMS:
            self.counts[e] = 0

    def op(self, eng, fn, deps=(), sem=None, inc=1):
        sem = sem or eng
        dl = []
        for d in deps:
            if d is None:
                continue
            if isinstance(d, list):
                dl.extend(x for x in d if x is not None)
            else:
                dl.append(d)
        self.counts[sem] += inc
        tok = (sem, self.counts[sem])
        self.q[eng].append((dl, fn, sem, inc))
        return tok

    def replay(self, engobj, eng, sems):
        marks = {}
        for deps, fn, sem, inc in self.q[eng]:
            for (ps, pc) in deps:
                if marks.get(ps, 0) < pc:
                    engobj.wait_ge(sems[ps], pc)
                    marks[ps] = pc
            inst = fn(engobj)
            if inst is not None and inc:
                inst.then_inc(sems[sem], inc)


def build_nc():
    nc = bass.Bass("TRN2", target_bir_lowering=False)

    xt = nc.dram_tensor("xt", [D, TCORE], F32, kind="ExternalInput").ap()
    wqkv = nc.dram_tensor("wqkv", [D, 3 * D], F32, kind="ExternalInput").ap()
    w1t = nc.dram_tensor("w1t", [D, D], F32, kind="ExternalInput").ap()
    wout = nc.dram_tensor("wout", [D, D], F32, kind="ExternalInput").ap()
    upat = nc.dram_tensor("upat", [50, D], F32, kind="ExternalInput").ap()
    lup = nc.dram_tensor("lup", [50, 4], F32, kind="ExternalInput").ap()
    ldn = nc.dram_tensor("ldn", [4, 100], F32, kind="ExternalInput").ap()
    dnaux = nc.dram_tensor("dnaux", [100, D], F32, kind="ExternalInput").ap()
    bqd = nc.dram_tensor("bqd", [128, 24], F32, kind="ExternalInput").ap()
    b1d = nc.dram_tensor("b1d", [128, 8], F32, kind="ExternalInput").ap()
    bod = nc.dram_tensor("bod", [128, 8], F32, kind="ExternalInput").ap()
    maskd = nc.dram_tensor("maskd", [128, 128], F32, kind="ExternalInput").ap()
    identd = nc.dram_tensor("identd", [128, 64], F32, kind="ExternalInput").ap()
    yt = nc.dram_tensor("yt", [D, TCORE], F32, kind="ExternalOutput").ap()
    if DEBUG:
        dbg_y = nc.dram_tensor("dbg_y", [128, NCH * SLAB], F32, kind="ExternalOutput").ap()
        dbg_h2 = nc.dram_tensor("dbg_h2", [128, NCH * SLAB], BF16, kind="ExternalOutput").ap()
        dbg_qg = nc.dram_tensor("dbg_qg", [64, NGRP_H * 128], BF16, kind="ExternalOutput").ap()
        dbg_kg = nc.dram_tensor("dbg_kg", [64, NGRP_H * 128], BF16, kind="ExternalOutput").ap()
        dbg_vg = nc.dram_tensor("dbg_vg", [64, NGRP_H * 128], BF16, kind="ExternalOutput").ap()
        dbg_c2 = nc.dram_tensor("dbg_c2", [128, 8 * SLAB], BF16, kind="ExternalOutput").ap()
        dbg_z = nc.dram_tensor("dbg_z", [128, NCH * SLAB], F32, kind="ExternalOutput").ap()
        dbg_xh = nc.dram_tensor("dbg_xh", [128, NCH * SLAB], BF16, kind="ExternalOutput").ap()
        dbg_rows = nc.dram_tensor("dbg_rows", [128, 2048], F32, kind="ExternalOutput").ap()

    s = Sched()
    with ExitStack() as ctx:
        al = lambda *a, **k: ctx.enter_context(nc.sbuf_tensor(*a, **k))
        alp = lambda *a, **k: ctx.enter_context(nc.psum_tensor(*a, **k))
        stg = al("stg", [128, 3072], F32)
        rowb = al("rowb", [128, 1024], BF16)
        wqkvb = al("wqkvb", [128, 8 * 3072], BF16)
        weffb = al("weffb", [128, 8 * 1024], BF16)
        woutb = al("woutb", [128, 8 * 1024], BF16)
        rbuf = [al(f"rbuf{i}", [128, NCH * SLAB], F32) for i in range(3)]
        xhat = al("xhat", [128, NCH * SLAB], BF16)
        x2b = al("x2b", [128, NCH * SLAB], BF16)
        hbuf = al("hbuf", [128, NCH * SLAB], BF16)
        qg = al("qg", [64, NGRP_H * 128], BF16)
        kg = al("kg", [64, NGRP_H * 128], BF16)
        vg = al("vg", [64, NGRP_H * 128], BF16)
        ebuf = [al(f"ebuf{i}", [128, 512], BF16) for i in range(2)]
        vb1 = [al(f"vb1_{i}", [128, 4 * 65], BF16) for i in range(2)]
        ctxg = [al(f"ctxg{i}", [128, 256], BF16) for i in range(2)]
        c2sb = al("c2sb", [128, 8 * SLAB], BF16)
        bq = al("bq", [128, 24], F32)
        b1 = al("b1", [128, 8], F32)
        bo = al("bo", [128, 8], F32)
        maskf = al("maskf", [128, 128], F32)
        maskb = al("maskb", [128, 128], BF16)
        identf = al("identf", [128, 64], F32)
        idb = al("idb", [128, 64], BF16)
        ones128 = al("ones128", [128, 128], BF16)
        epsc = al("epsc", [128, 1], F32)
        ksc = al("ksc", [128, 1], F32)
        nhc = al("nhc", [128, 1], F32)
        ivdc = al("ivdc", [128, 1], F32)
        rcol = [al(f"rcol{i}", [128, 4], F32) for i in range(2)]
        d4 = [al(f"d4_{i}", [128, 256], BF16) for i in range(2)]
        lora_up = al("lora_up", [50, 4], F32)
        lora_dn = al("lora_dn", [4, 100], F32)
        pbank = [alp(f"pbank{i}", [128, 512], F32) for i in range(2)]
        pstat = [alp(f"pstat{i}", [128, 512], F32) for i in range(2)]
        pscore = alp("pscore", [128, 512], F32)
        pctx = alp("pctx", [128, 512], F32)
        pvgp = alp("pvgp", [128, 1024], BF16)
        pc2 = alp("pc2", [128, 512], F32)

        sem_handles = {}
        for name in ENGS + DMASEMS:
            sem_handles[name] = ctx.enter_context(nc.semaphore(name))
        block = ctx.enter_context(nc.Block())

        c2f32 = c2sb[:].bitcast(F32)
        dna = c2f32[0:100, 0:1024]
        upa = c2f32[0:64, 1024:2048]
        w1t_sl2 = xhat[:].bitcast(F32)[:, 1024:2048]
        rows_m = stg[:, 0:512]
        rows_rstd = stg[:, 512:1024]
        rows_t = stg[:, 1024:1536]
        rows_v = stg[:, 1536:2048]

        def dma(dst, src, sem, deps=()):
            return s.op("sp", lambda e, dst=dst, src=src: e.dma_start(out=dst, in_=src),
                        deps=deps, sem=sem, inc=16)

        t_ones = s.op("dve", lambda e: e.memset(ones128[:], 1.0))
        t_eps = s.op("dve", lambda e: e.memset(epsc[:], EPS))
        t_ksc = s.op("dve", lambda e: e.memset(ksc[:], 0.125))
        t_nh = s.op("dve", lambda e: e.memset(nhc[:], -0.5))
        t_ivd = s.op("dve", lambda e: e.memset(ivdc[:], 1.0 / 1024.0))
        t_v1 = [s.op("dve", lambda e, i=i: e.memset(
            vb1[i][:].rearrange("p (g c) -> p g c", c=65)[:, :, 64:65], 1.0)) for i in range(2)]
        d_lora = [dma(dst, srcd, fam) for dst, srcd, fam in
                  ((lora_up[:], lup, "dbg0"), (lora_dn[:], ldn, "dbg1"),
                   (dna[0:100, :], dnaux, "dbg2"), (upa[0:50, :], upat, "dbg3"))]
        for dst, srcd in ((bq[:], bqd), (b1[:], b1d), (bo[:], bod),
                          (maskf[:], maskd), (identf[:], identd)):
            t_ldc = dma(dst, srcd, "ldc")
        d_b = [t_ldc] * 3
        t_maskb = s.op("dve", lambda e: e.tensor_copy(out=maskb[:], in_=maskf[:]),
                       deps=[t_ldc])
        t_idb = s.op("dve", lambda e: e.tensor_copy(out=idb[:], in_=identf[:]),
                     deps=[t_ldc])
        t_bq = s.op("dve", lambda e: e.tensor_scalar(
            out=bq[:, 8:16], in0=bq[:, 8:16], scalar1=0.125, scalar2=None,
            op0=ALU.mult), deps=[t_ldc])
        t_b1 = s.op("dve", lambda e: e.tensor_scalar(
            out=b1[:], in0=b1[:], scalar1=0.5, scalar2=None, op0=ALU.mult),
            deps=[t_ldc])

        xt_v = xt.rearrange("(c p) n -> p c n", p=128)
        yt_v = yt.rearrange("(c p) n -> p c n", p=128)
        t_x = [None] * NSLAB
        t_x[0] = dma(rbuf[0][:].rearrange("p (c n) -> p c n", n=SLAB),
                     xt_v[:, :, 0:SLAB], "ldx0")

        A_sl = stg[0:4, 0:1024]
        B_sl = stg[0:100, 1024:2048]
        w1t_sl = stg[:, 2048:3072]

        t_A_mm = s.op("pe", lambda e: (
            e.matmul(out=pbank[0][0:4, :], lhsT=lora_up[:], rhs=upa[0:50, 0:512],
                     start=True, stop=True),
            e.matmul(out=pbank[1][0:4, :], lhsT=lora_up[:], rhs=upa[0:50, 512:1024],
                     start=True, stop=True))[-1], deps=d_lora[:2] + [d_lora[3]])
        t_A = s.op("dve", lambda e: (
            e.tensor_copy(out=A_sl[:, 0:512], in_=pbank[0][0:4, :]),
            e.tensor_copy(out=A_sl[:, 512:1024], in_=pbank[1][0:4, :]))[-1],
            deps=[t_A_mm])
        t_B_mm = s.op("pe", lambda e: (
            e.matmul(out=pbank[0][0:100, :], lhsT=lora_dn[:], rhs=A_sl[:, 0:512],
                     start=True, stop=True),
            e.matmul(out=pbank[1][0:100, :], lhsT=lora_dn[:], rhs=A_sl[:, 512:1024],
                     start=True, stop=True))[-1], deps=[t_A, d_lora[1]])
        t_B = s.op("dve", lambda e: (
            e.tensor_copy(out=B_sl[:, 0:512], in_=pbank[0][0:100, :]),
            e.tensor_copy(out=B_sl[:, 512:1024], in_=pbank[1][0:100, :]))[-1],
            deps=[t_B_mm])
        t_weff_parts = []
        for kc in range(NCH):
            slw = w1t_sl if kc % 2 == 0 else w1t_sl2
            war = [t_weff_parts[kc - 2]] if kc >= 2 else []
            d_w1 = dma(slw, w1t.rearrange("(c p) m -> c p m", p=128)[kc],
                       "ldw1" if kc % 2 == 0 else "ldq1", deps=war)
            t_w1h = s.op("dve", lambda e, slw=slw: e.tensor_scalar(
                out=slw, in0=slw, scalar1=0.5, scalar2=None, op0=ALU.mult),
                deps=[d_w1])
            cbk0 = [pbank[0], pbank[1], pscore, pctx][(2 * kc) % 4]
            cbk1 = [pbank[0], pbank[1], pscore, pctx][(2 * kc + 1) % 4]
            t_c = s.op("pe", lambda e, kc=kc, cbk0=cbk0, cbk1=cbk1: (
                e.matmul(out=cbk0[:], lhsT=dna[0:100, kc * 128:(kc + 1) * 128],
                         rhs=B_sl[:, 0:512], start=True, stop=True),
                e.matmul(out=cbk1[:], lhsT=dna[0:100, kc * 128:(kc + 1) * 128],
                         rhs=B_sl[:, 512:1024], start=True, stop=True))[-1],
                deps=[t_B, d_lora[2]] + ([t_weff_parts[kc - 2]] if kc >= 2 else []))
            t_we = s.op("dve", lambda e, kc=kc, slw=slw, cbk0=cbk0, cbk1=cbk1: (
                e.scalar_tensor_tensor(
                    out=weffb[:, kc * 1024:kc * 1024 + 512], in0=cbk0[:],
                    scalar=0.5, in1=slw[:, 0:512], op0=ALU.mult, op1=ALU.add),
                e.scalar_tensor_tensor(
                    out=weffb[:, kc * 1024 + 512:(kc + 1) * 1024], in0=cbk1[:],
                    scalar=0.5, in1=slw[:, 512:1024], op0=ALU.mult, op1=ALU.add))[-1],
                deps=[t_c, t_w1h])
            t_weff_parts.append(t_we)
        t_weffb = t_weff_parts[-1]

        wqkv_v = wqkv.rearrange("(c p) m -> c p m", p=128)
        wout_v = wout.rearrange("(c p) m -> c p m", p=128)
        for kc in range(NCH):
            t_wqkvb = s.op("pool", lambda e, kc=kc: e.dma_start(
                out=wqkvb[:, kc * 3072:(kc + 1) * 3072], in_=wqkv_v[kc]),
                deps=[t_ldc] if kc == 0 else [], sem="ldq0", inc=16)
        for kc in range(NCH):
            t_woutb = s.op("pool", lambda e, kc=kc: e.dma_start(
                out=woutb[:, kc * 1024:(kc + 1) * 1024], in_=wout_v[kc]),
                deps=[], sem="ldwo", inc=16)

        bank_free = {"pscore": [t_woutb], "pstat0": [t_B_mm], "pstat1": [t_B_mm],
                     "pctx": [t_woutb], "pbank0": [t_woutb], "pbank1": [t_woutb]}
        proj_banks = [("pbank0", pbank[0]), ("pbank1", pbank[1]),
                      ("pscore", pscore), ("pctx", pctx)]
        t_xhat_free = [t_ones]
        t_hbuf_free = [t_ones]
        t_rowb_free = [t_ones]
        t_rows_free = [t_woutb]
        t_qkvg_free = [t_ones]
        t_ebuf_free = [[t_ones], [t_ones]]
        t_vb1_free = [[t_v1[0]], [t_v1[1]]]
        t_ctxg_free = [[t_ones], [t_ones]]
        t_rcol_free = [[t_ones], [t_ones]]
        t_d4_free = [[t_ones], [t_ones]]
        t_pvgp_free = [t_ones]
        t_pc2_free = [t_ones]
        t_c2_free = [t_weffb]
        t_store = [None] * NSLAB

        ln_calls = [0]
        t_x2b_free = [t_ones]

        def ln_front_start():
            return dict(cast=[], x2=[], mm_s=None, mm_q=None)

        def ln_front_square(st, c):
            nonlocal t_x2b_free
            t_c = st["cast"][c]
            t_q = s.op("dve", lambda e, c=c: e.tensor_tensor(
                out=x2b[:, c * SLAB:(c + 1) * SLAB],
                in0=xhat[:, c * SLAB:(c + 1) * SLAB],
                in1=xhat[:, c * SLAB:(c + 1) * SLAB], op=ALU.mult),
                deps=([t_c] if not isinstance(t_c, list) else t_c)
                     + (list(t_x2b_free) if c == 0 else []))
            st["x2"].append(t_q)

        def ln_front_chunk(st, c, tok, src, pre_cast=None, sq=True):
            nonlocal t_xhat_free
            if pre_cast is not None:
                t_c = pre_cast
            else:
                extra = list(t_xhat_free) if c == 0 else []
                t_c = s.op("dve", lambda e, c=c: e.tensor_copy(
                    out=xhat[:, c * SLAB:(c + 1) * SLAB],
                    in_=src[:, c * SLAB:(c + 1) * SLAB]),
                    deps=[tok] + extra)
            st["cast"].append(t_c)
            if sq:
                ln_front_square(st, c)

        def ln_front_finish(st, which="sq"):
            for c in range(NCH):
                cst = st["cast"][c]
                dep_s = (cst if isinstance(cst, list) else [cst]) + [t_ones]
                if c == 0:
                    dep_s += bank_free["pstat0"]
                if "s" in which:
                    st["mm_s"] = s.op("pe", lambda e, c=c: e.matmul(
                        out=pstat[0][:], lhsT=ones128[:],
                        rhs=xhat[:, c * SLAB:(c + 1) * SLAB],
                        start=(c == 0), stop=(c == NCH - 1), skip_group_check=True),
                        deps=dep_s, inc=(1 if c == NCH - 1 else 0))
            for c in range(NCH):
                if "q" not in which:
                    break
                dep_q = [st["x2"][c]]
                if c == 0:
                    dep_q += bank_free["pstat1"]
                if True:
                    st["mm_q"] = s.op("pe", lambda e, c=c: e.matmul(
                        out=pstat[1][:], lhsT=ones128[:],
                        rhs=x2b[:, c * SLAB:(c + 1) * SLAB],
                        start=(c == 0), stop=(c == NCH - 1), skip_group_check=True),
                        deps=dep_q, inc=(1 if c == NCH - 1 else 0))
            return st

        def ln_front(chunk_toks, src):
            if not isinstance(chunk_toks, list):
                chunk_toks = [chunk_toks] * NCH
            st = ln_front_start()
            for c in range(NCH):
                ln_front_chunk(st, c, chunk_toks[c], src)
            ln_front_finish(st)
            return st

        def ln_back(fr, dst=None, dst_war=None, per_chunk=False):
            nonlocal t_xhat_free, t_rowb_free, t_rows_free, t_x2b_free
            ln_calls[0] += 1
            t_mm_s, t_mm_q = fr["mm_s"], fr["mm_q"]
            t_cast = fr["cast"][-1]
            if isinstance(t_cast, list):
                t_cast = t_cast[-1]
            t_rm = s.op("dve", lambda e: e.tensor_scalar(
                out=rowb[:, 0:512], in0=pstat[0][:], scalar1=1.0 / 1024.0,
                scalar2=None, op0=ALU.mult), deps=[t_mm_s] + t_rowb_free)
            t_msq = s.op("dve", lambda e: e.tensor_tensor(
                out=rows_t, in0=rowb[:, 0:512], in1=rowb[:, 0:512],
                op=ALU.mult), deps=[t_rm] + t_rows_free)
            t_var = s.op("dve", lambda e: e.scalar_tensor_tensor(
                out=rows_v, in0=pstat[1][:], scalar=ivdc[:, 0:1], in1=rows_t,
                op0=ALU.mult, op1=ALU.subtract),
                deps=[t_mm_q, t_msq] + t_rows_free)
            t_ln = s.op("act", lambda e: e.activation(
                out=rows_m, in_=rows_v, func=AF.Ln, bias=epsc[:, 0:1], scale=1.0),
                deps=[t_var, t_eps] + t_rows_free)
            t_rstd = s.op("act", lambda e: e.activation(
                out=rowb[:, 512:1024], in_=rows_m, func=AF.Exp, bias=0.0,
                scale=nhc[:, 0:1]), deps=[t_ln, t_nh] + t_rowb_free)
            xv = xhat[:].rearrange("p (c n) -> p c n", n=SLAB)
            dstb = hbuf if dst is None else dst
            war = list(t_hbuf_free) if dst_war is None else list(dst_war)
            hv = dstb[:].rearrange("p (c n) -> p c n", n=SLAB)
            t_apps = []
            t_subs = []
            if per_chunk:
                for c in range(NCH):
                    cst = fr["cast"][c]
                    cd = cst if isinstance(cst, list) else [cst]
                    t_sub = s.op("dve", lambda e, c=c: e.tensor_tensor(
                        out=dstb[:, c * SLAB:(c + 1) * SLAB],
                        in0=xhat[:, c * SLAB:(c + 1) * SLAB],
                        in1=rowb[:, 0:512], op=ALU.subtract),
                        deps=[t_rm] + cd + (war if c == 0 else []))
                    t_subs.append(t_sub)
                for c in range(NCH):
                    t_app = s.op("dve", lambda e, c=c: e.tensor_tensor(
                        out=dstb[:, c * SLAB:(c + 1) * SLAB],
                        in0=dstb[:, c * SLAB:(c + 1) * SLAB],
                        in1=rowb[:, 512:1024], op=ALU.mult),
                        deps=[t_subs[c], t_rstd])
                    t_apps.append(t_app)
            else:
                for hh in range(2):
                    ho = hh * HALF
                    mb = rowb[:, ho:ho + HALF].unsqueeze(1).broadcast_to(
                        [128, NCH, HALF])
                    t_sub = s.op("dve", lambda e, ho=ho, mb=mb: e.tensor_tensor(
                        out=hv[:, :, ho:ho + HALF], in0=xv[:, :, ho:ho + HALF],
                        in1=mb, op=ALU.subtract),
                        deps=[t_rm, t_cast] + (war if hh == 0 else []))
                    t_subs.append(t_sub)
                for hh in range(2):
                    ho = hh * HALF
                    rb = rowb[:, 512 + ho:512 + ho + HALF].unsqueeze(1).broadcast_to(
                        [128, NCH, HALF])
                    t_app = s.op("dve", lambda e, ho=ho, rb=rb: e.tensor_tensor(
                        out=hv[:, :, ho:ho + HALF], in0=hv[:, :, ho:ho + HALF],
                        in1=rb, op=ALU.mult), deps=[t_subs[hh], t_rstd])
                    t_apps.append(t_app)
            bank_free["pstat0"] = [t_rm, t_msq]
            bank_free["pstat1"] = [t_var]
            t_xhat_free = [t_subs[-1]]
            t_x2b_free = [t_mm_q]
            t_rowb_free = [t_subs[-1], t_apps[-1]]
            t_rows_free = [t_var, t_ln, t_rstd]
            return t_apps

        def projection(t_h, t_w, wbuf, wbase_stride, nout, bias, t_bias, resid_in,
                       t_resid_in, resid_out, t_resid_out_war, operand=None,
                       ln_next_src=None, ln_next_tok=None, pe_mid=None):
            toks = []
            t_mm = None
            fr = ln_front_start() if ln_next_src is not None else None
            nb = len(proj_banks)
            opnd = hbuf if operand is None else operand
            per_kc = isinstance(t_h, list) and len(t_h) == NCH
            for oc in range(nout):
                bname, bk = proj_banks[oc % nb]
                if per_kc:
                    for kc in range(NCH):
                        t_mm = s.op("pe", lambda e, oc=oc, bk=bk, kc=kc: e.matmul(
                            out=bk[:],
                            lhsT=wbuf[:, kc * wbase_stride + oc * 128:
                                      kc * wbase_stride + oc * 128 + 128],
                            rhs=opnd[:, kc * SLAB:(kc + 1) * SLAB],
                            start=(kc == 0), stop=(kc == NCH - 1),
                            skip_group_check=True),
                            deps=[t_h[kc], t_w]
                                 + (bank_free[bname] if kc == 0 else []),
                            inc=(1 if kc == NCH - 1 else 0))
                else:
                    t_mm = s.op("pe", lambda e, oc=oc, bk=bk: [
                        e.matmul(out=bk[:],
                                 lhsT=wbuf[:, kc * wbase_stride + oc * 128:
                                           kc * wbase_stride + oc * 128 + 128],
                                 rhs=opnd[:, kc * SLAB:(kc + 1) * SLAB],
                                 start=(kc == 0), stop=(kc == NCH - 1))
                        for kc in range(NCH)][-1],
                        deps=[t_h, t_w] + bank_free[bname])
                t_pc = None
                if fr is not None and ln_next_tok is None:
                    xw = list(t_xhat_free) if oc == 0 else []
                    t_pc = s.op("dve", lambda e, oc=oc, bk=bk: e.scalar_tensor_tensor(
                        out=xhat[:, oc * SLAB:(oc + 1) * SLAB],
                        in0=bk[:], scalar=bias[:, oc:oc + 1],
                        in1=resid_in[:, oc * SLAB:(oc + 1) * SLAB],
                        op0=ALU.add, op1=ALU.add),
                        deps=[t_mm, t_resid_in, t_bias] + xw)
                t_ev = s.op("dve", lambda e, oc=oc, bk=bk: e.scalar_tensor_tensor(
                    out=resid_out[:, oc * SLAB:(oc + 1) * SLAB],
                    in0=bk[:], scalar=bias[:, oc:oc + 1],
                    in1=resid_in[:, oc * SLAB:(oc + 1) * SLAB],
                    op0=ALU.add, op1=ALU.add),
                    deps=[t_mm, t_resid_in, t_bias] + t_resid_out_war)
                bank_free[bname] = [t_ev]
                toks.append(t_ev)
                if fr is not None:
                    ln_front_chunk(fr, oc,
                                   (ln_next_tok if ln_next_tok is not None else t_ev),
                                   ln_next_src, pre_cast=t_pc)
                if pe_mid is not None and oc == 1:
                    pe_mid(0)
                if pe_mid is not None and oc == 3:
                    pe_mid(1)
            if fr is not None:
                ln_front_finish(fr)
            return toks, t_mm, fr

        def qkv_and_attention(t_h, slab, resid_y, t_y, resid_z, z_war, bo_, t_ln3_src):
            nonlocal t_qkvg_free, t_pvgp_free, t_pc2_free
            nonlocal t_ebuf_free, t_vb1_free, t_ctxg_free, t_rcol_free, t_d4_free
            last_qkv_mm = [None]
            dbg_qkv_toks = []
            t_c2 = [None]
            c2_half = {}
            ev2 = [None] * NCH
            ev2_h1 = [None] * NCH
            pc3 = [[None] * NCH, [None] * NCH]
            fr3 = ln_front_start()
            nb = len(proj_banks)

            def wout_half(half, c2toks, with_ln):
                hoff = half * HALF
                for oc in range(NCH):
                    bname, bk = proj_banks[oc % nb]
                    t_mm = s.op("pe", lambda e, oc=oc, bk=bk, hoff=hoff: [
                        e.matmul(out=bk[:, 0:HALF],
                                 lhsT=woutb[:, j * 1024 + oc * 128:j * 1024 + oc * 128 + 128],
                                 rhs=c2sb[:, j * SLAB + hoff:j * SLAB + hoff + HALF],
                                 start=(j == 0), stop=(j == NCH - 1))
                        for j in range(NCH)][-1],
                        deps=c2toks + [t_woutb] + bank_free[bname])
                    if with_ln or half == 0:
                        xw = list(t_xhat_free) if (half == 0 and oc == 0) else []
                        t_pc = s.op("dve", lambda e, oc=oc, bk=bk, hoff=hoff:
                                    e.scalar_tensor_tensor(
                            out=xhat[:, oc * SLAB + hoff:oc * SLAB + hoff + HALF],
                            in0=bk[:, 0:HALF], scalar=bo_[:, oc:oc + 1],
                            in1=resid_y[:, oc * SLAB + hoff:oc * SLAB + hoff + HALF],
                            op0=ALU.add, op1=ALU.add),
                            deps=[t_mm, t_y, t_ldc] + xw)
                        pc3[half][oc] = t_pc
                    t_ev = s.op("dve", lambda e, oc=oc, bk=bk, hoff=hoff:
                                e.scalar_tensor_tensor(
                        out=resid_z[:, oc * SLAB + hoff:oc * SLAB + hoff + HALF],
                        in0=bk[:, 0:HALF], scalar=bo_[:, oc:oc + 1],
                        in1=resid_y[:, oc * SLAB + hoff:oc * SLAB + hoff + HALF],
                        op0=ALU.add, op1=ALU.add),
                        deps=[t_mm, t_y, t_ldc] + z_war)
                    bank_free[bname] = [t_ev]
                    if half == 0:
                        ev2_h1[oc] = t_ev
                    else:
                        ev2[oc] = t_ev
                        if with_ln:
                            ln_front_chunk(fr3, oc, None, t_ln3_src,
                                           pre_cast=[pc3[0][oc], pc3[1][oc]])
                if half == 1 and with_ln:
                    ln_front_finish(fr3)

            qkv_banks = proj_banks + [("pstat0", pstat[0]), ("pstat1", pstat[1])]
            for half in range(2):
                hoff = half * HALF
                ev_toks_q, ev_toks_kv = [], []
                for oc in range(24):
                    bname, bk = qkv_banks[oc % 6]
                    t_mm = s.op("pe", lambda e, oc=oc, bk=bk, hoff=hoff: [
                        e.matmul(out=bk[:, 0:HALF],
                                 lhsT=wqkvb[:, kc * 3072 + oc * 128:
                                            kc * 3072 + oc * 128 + 128],
                                 rhs=hbuf[:, kc * SLAB + hoff:kc * SLAB + hoff + HALF],
                                 start=(kc == 0), stop=(kc == NCH - 1))
                        for kc in range(NCH)][-1],
                        deps=[t_h[half] if isinstance(t_h, list) else t_h,
                              t_wqkvb] + bank_free[bname])
                    last_qkv_mm[0] = t_mm
                    kind = oc // 8
                    c = oc % 8
                    if kind == 0:
                        dst = qg
                        offs = (8 * c, 64 + 8 * c)
                    elif kind == 1:
                        dst = kg
                        offs = (16 * c, 16 * c + 8)
                    else:
                        dst = vg
                        offs = (16 * c, 16 * c + 8)
                    dstv = dst[:].rearrange("d (g c) -> d g c", c=128)
                    if kind == 0:
                        t_ev = s.op("act", lambda e, bk=bk, oc=oc, dstv=dstv, offs=offs: (
                            e.activation(out=dstv[:, :, offs[0]:offs[0] + 8],
                                         in_=bk[0:64, 0:HALF].rearrange(
                                             "d (g t) -> d g t", t=8),
                                         func=AF.Identity, bias=bq[0:64, oc:oc + 1],
                                         scale=1.0),
                            e.activation(out=dstv[:, :, offs[1]:offs[1] + 8],
                                         in_=bk[64:128, 0:HALF].rearrange(
                                             "d (g t) -> d g t", t=8),
                                         func=AF.Identity, bias=bq[64:128, oc:oc + 1],
                                         scale=1.0))[-1],
                            deps=[t_mm, t_bq] + t_qkvg_free, inc=2)
                        ev_toks_q.append(t_ev)
                    else:
                        sc = ksc[:, 0:1] if kind == 1 else 1.0
                        t_ev = s.op("act", lambda e, bk=bk, oc=oc, dstv=dstv,
                                    offs=offs, sc=sc: (
                            e.activation(out=dstv[:, :, offs[0]:offs[0] + 8],
                                         in_=bk[0:64, 0:HALF].rearrange(
                                             "d (g t) -> d g t", t=8),
                                         func=AF.Identity, bias=bq[0:64, oc:oc + 1],
                                         scale=(sc if isinstance(sc, float)
                                                else ksc[0:64, 0:1])),
                            e.activation(out=dstv[:, :, offs[1]:offs[1] + 8],
                                         in_=bk[64:128, 0:HALF].rearrange(
                                             "d (g t) -> d g t", t=8),
                                         func=AF.Identity, bias=bq[64:128, oc:oc + 1],
                                         scale=(sc if isinstance(sc, float)
                                                else ksc[64:128, 0:1])))[-1],
                            deps=[t_mm, t_bq, t_ksc] + t_qkvg_free, inc=2)
                        ev_toks_kv.append(t_ev)
                    bank_free[bname] = [t_ev]
                t_q_ready = ev_toks_q[-1]
                t_kv_ready = ev_toks_kv[-1]
                if DEBUG and slab == 0 and half == 0:
                    t_dq = dma(dbg_qg, qg[:], "dbg2", deps=[t_q_ready])
                    t_dk = dma(dbg_kg, kg[:], "dbg3", deps=[t_kv_ready])
                    t_dv = dma(dbg_vg, vg[:], "dbg4", deps=[t_kv_ready])
                    dbg_qkv_toks.extend([t_dq, t_dk, t_dv])

                if half == 1:
                    wout_half(0, c2_half[0], with_ln=False)

                sc_banks = [(pscore, "pscore"), (pstat[0], "pstat0")]
                ctx_banks = [(pctx, "pctx"), (pstat[1], "pstat1")]
                t_sc = {}
                t_exp = {}
                t_mask = {}
                t_vt = {}
                t_vev = {}
                t_ctx = {}
                t_rc = {}
                t_d4 = {}
                t_d4i = {}
                t_cge = {}
                t_ct = {}
                half_c2 = []
                NB = NBATCH_H
                for b in range(NB + 2):
                    par = b % 2
                    if b < NB:
                        g0 = b * 4
                        sb, sbname = sc_banks[par]
                        war_sc = list(bank_free[sbname])
                        t_sc[b] = s.op("pe", lambda e, g0=g0, sb=sb: [
                            e.matmul(out=sb[:, i * 128:(i + 1) * 128],
                                     lhsT=kg[:, (g0 + i) * 128:(g0 + i + 1) * 128],
                                     rhs=qg[:, (g0 + i) * 128:(g0 + i + 1) * 128],
                                     start=True, stop=True) for i in range(4)][-1],
                            deps=[t_q_ready, t_kv_ready] + war_sc, inc=4)
                        war_vt = [t_vev[b - 1]] if b >= 1 else t_pvgp_free
                        t_vt[b] = s.op("pe", lambda e, g0=g0: [
                            e.transpose(out=pvgp[:, i * 64:(i + 1) * 64],
                                        in_=vg[:, (g0 + i) * 128:(g0 + i + 1) * 128],
                                        identity=idb[0:64, :]) for i in range(4)][-1],
                            deps=[t_kv_ready, t_idb] + war_vt, inc=4)
                        war_e = [t_ctx[b - 2]] if b >= 2 else t_ebuf_free[par]
                        t_exp[b] = s.op("act", lambda e, par=par, sb=sb: e.activation(
                            out=ebuf[par][:], in_=sb[:], func=AF.Exp),
                            deps=[t_sc[b]] + war_e)
                        bank_free[sbname] = [t_exp[b]]
                        t_mask[b] = s.op("dve", lambda e, par=par: e.tensor_tensor(
                            out=ebuf[par][:].rearrange("p (g c) -> p g c", c=128),
                            in0=ebuf[par][:].rearrange("p (g c) -> p g c", c=128),
                            in1=maskb[:].unsqueeze(1).broadcast_to([128, 4, 128]),
                            op=ALU.mult), deps=[t_exp[b], t_maskb])
                        war_v = [t_ctx[b - 2]] if b >= 2 else t_vb1_free[par]
                        t_vev[b] = s.op("dve", lambda e, par=par: e.tensor_copy(
                            out=vb1[par][:].rearrange("p (g c) -> p g c", c=65)[:, :, 0:64],
                            in_=pvgp[:, 0:256].rearrange("p (g c) -> p g c", c=64)),
                            deps=[t_vt[b]] + war_v)
                    bb = b - 1
                    if 0 <= bb < NB:
                        par = bb % 2
                        cb, cbname = ctx_banks[par]
                        war_c = list(bank_free[cbname])
                        t_ctx[bb] = s.op("pe", lambda e, par=par, cb=cb: [
                            e.matmul(out=cb[:, i * 65:(i + 1) * 65],
                                     lhsT=ebuf[par][:, i * 128:(i + 1) * 128],
                                     rhs=vb1[par][:, i * 65:(i + 1) * 65],
                                     start=True, stop=True) for i in range(4)][-1],
                            deps=[t_mask[bb], t_vev[bb]] + war_c, inc=4)
                        t_rc[bb] = s.op("dve", lambda e, par=par, cb=cb: e.reciprocal(
                            out=rcol[par][:], in_=cb[:, 0:260].rearrange(
                                "p (g c) -> p g c", c=65)[:, :, 64]),
                            deps=[t_ctx[bb]] + ([t_d4[bb - 2]] if bb >= 2
                                                else t_rcol_free[par]))
                        d4i = []
                        for i in range(4):
                            war_d = ([] if i else ([t_ct[bb - 2]] if bb >= 2
                                                   else t_d4_free[par]))
                            d4i.append(s.op("dve", lambda e, par=par, i=i:
                                            e.tensor_scalar(
                                out=d4[par][:, i * 64:(i + 1) * 64],
                                in0=idb[:], scalar1=rcol[par][:, i:i + 1],
                                scalar2=None, op0=ALU.mult),
                                deps=[t_rc[bb], t_idb] + war_d))
                        t_d4[bb] = d4i[-1]
                        t_d4i[bb] = d4i
                        war_g = [t_ct[bb - 2]] if bb >= 2 else t_ctxg_free[par]
                        t_cge[bb] = s.op("act", lambda e, par=par, cb=cb: e.activation(
                            out=ctxg[par][:].rearrange("p (g c) -> p g c", c=64),
                            in_=cb[:, 0:260].rearrange("p (g c) -> p g c", c=65)[:, :, 0:64],
                            func=AF.Identity, bias=0.0, scale=1.0),
                            deps=[t_ctx[bb]] + war_g)
                        bank_free[cbname] = [t_rc[bb], t_cge[bb]]
                    bc = b - 2
                    if 0 <= bc < NB:
                        par = bc % 2
                        war_t = [t_c2[0]] if t_c2[0] is not None else t_pc2_free
                        for i in range(4):
                            t_ct[bc] = s.op("pe", lambda e, par=par, i=i: [
                                e.matmul(out=pc2[(0 if pp == 0 else 64):
                                                 (64 if pp == 0 else 128),
                                                 i * 64:(i + 1) * 64],
                                         lhsT=ctxg[par][(0 if pp == 0 else 64):
                                                        (64 if pp == 0 else 128),
                                                        i * 64:(i + 1) * 64],
                                         rhs=d4[par][(0 if pp == 0 else 64):
                                                     (64 if pp == 0 else 128),
                                                     i * 64:(i + 1) * 64],
                                         start=True, stop=True,
                                         tile_position=(None if pp == 0
                                                        else (64, 64)))
                                for pp in range(2)][-1],
                                deps=[t_cge[bc], t_d4i[bc][i]]
                                     + (war_t if i == 0 else []), inc=2)
                        coff = hoff + bc * 32
                        t_c2[0] = s.op("dve", lambda e, coff=coff: e.tensor_copy(
                            out=c2sb[:].rearrange("p (j n) -> p j n", n=SLAB)[
                                :, :, coff:coff + 32].rearrange("p j (g t) -> p j g t", t=8),
                            in_=pc2[:, 0:256].rearrange("p (g j t) -> p j g t", g=4, j=8, t=8)),
                            deps=[t_ct[bc]] + t_c2_free)
                        half_c2.append(t_c2[0])
                t_pvgp_free = [t_vev[NB - 1]]
                t_pc2_free = [t_c2[0]]
                for p in range(2):
                    lastb = NB - 2 + p
                    t_ebuf_free[p] = [t_ctx[lastb]]
                    t_vb1_free[p] = [t_ctx[lastb]]
                    t_ctxg_free[p] = [t_ct[lastb]]
                    t_rcol_free[p] = [t_d4[lastb]]
                    t_d4_free[p] = [t_ct[lastb]]
                t_qkvg_free = [t_ct[NB - 1]] + dbg_qkv_toks
                dbg_qkv_toks = []
                c2_half[half] = [half_c2[-1]]
            wout_half(1, c2_half[1], with_ln=True)
            return ev2, fr3, last_qkv_mm[0], c2_half

        ev3_prev = []
        fr1_next = None
        t_h1_next = None
        for sl in range(NSLAB):
            A = rbuf[sl % 3]
            B = rbuf[(sl + 2) % 3]
            tA = t_x[sl]
            b_war = list(ev3_prev)
            if sl == 0:
                fr1 = ln_front(tA, A)
                t_h1 = ln_back(fr1, dst=c2sb, dst_war=t_c2_free, per_chunk=True)
            else:
                t_h1 = t_h1_next
            ev1, t_mm1, fr2 = projection(t_h1, t_weffb, weffb, 1024, NCH, b1, t_b1,
                                         A, tA, B, b_war, operand=c2sb,
                                         ln_next_src=B)
            t_y = ev1[-1]
            t_c2_free = [t_mm1]
            if DEBUG and sl == 0:
                t_d = dma(dbg_y, B[:], "dbg0", deps=ev1)
                ev1 = ev1 + [t_d]
            t_h2 = ln_back(fr2)
            if DEBUG and sl == 0:
                t_dh = dma(dbg_h2, hbuf[:], "dbg1", deps=[t_h2])
                t_hbuf_free_extra = [t_dh]
            else:
                t_hbuf_free_extra = []
            ev2, fr3, t_mmq, c2_half = qkv_and_attention(
                t_h2, sl, B, t_y, A, ev1, bo, A)
            t_hbuf_free = [t_mmq] + t_hbuf_free_extra
            t_z = ev2[-1]
            t_c2_free = list(ev2)
            if DEBUG and sl == 0:
                t_dc2 = dma(dbg_c2, c2sb[:], "dbg5", deps=c2_half[1])
                t_dz = dma(dbg_z, A[:], "dbg6", deps=ev2)
                t_c2_free = t_c2_free + [t_dc2]
                ev2 = ev2 + [t_dz]
            t_h3 = ln_back(fr3, per_chunk=True)
            if sl + 1 < NSLAB:
                pf_deps = [t_store[sl - 1]] if sl >= 1 else []
                t_x[sl + 1] = dma(rbuf[(sl + 1) % 3][:].rearrange(
                    "p (c n) -> p c n", n=SLAB),
                    xt_v[:, :, (sl + 1) * SLAB:(sl + 2) * SLAB],
                    "ldx%d" % ((sl + 1) % 2), deps=pf_deps)
            pe_mid = None
            if sl + 1 < NSLAB:
                nxt = rbuf[(sl + 1) % 3]
                fr1n = ln_front_start()
                for c in range(NCH):
                    ln_front_chunk(fr1n, c, t_x[sl + 1], nxt, sq=False)
                for c in range(NCH):
                    ln_front_square(fr1n, c)
                h1n_holder = []

                h1n_holder = []

                def pe_mid(phase, fr1n=fr1n, h1n_holder=h1n_holder,
                           c2w=list(t_c2_free)):
                    if phase == 0:
                        ln_front_finish(fr1n, which="s")
                    else:
                        ln_front_finish(fr1n, which="q")
                        h1n_holder.append(ln_back(fr1n, dst=c2sb, dst_war=c2w,
                                                  per_chunk=True))
            ev3, t_mm3, _ = projection(t_h3, t_weffb, weffb, 1024, NCH, b1, t_b1,
                                       A, t_z, B, ev2, pe_mid=pe_mid)
            if sl + 1 < NSLAB:
                t_h1_next = h1n_holder[0]
            t_hbuf_free = [t_mm3]
            ev3_prev = ev3
            t_store[sl] = dma(
                yt_v[:, :, sl * SLAB:(sl + 1) * SLAB],
                B[:].rearrange("p (c n) -> p c n", n=SLAB),
                "st%d" % (sl % 2), deps=ev3)

        fin_deps = [("st0", s.counts["st0"]), ("st1", s.counts["st1"])]
        if DEBUG:
            fin_deps += [(d, s.counts[d]) for d in DMASEMS if d.startswith("dbg")
                         and s.counts[d] > 0]
        s.op("sp", lambda e: None, deps=fin_deps, inc=0)

        @block.sync
        def _(sync):
            s.replay(sync, "sp", sem_handles)

        @block.vector
        def _(vector):
            s.replay(vector, "dve", sem_handles)

        @block.tensor
        def _(tensor):
            s.replay(tensor, "pe", sem_handles)

        @block.scalar
        def _(scalar):
            s.replay(scalar, "act", sem_handles)

        @block.gpsimd
        def _(gpsimd):
            s.replay(gpsimd, "pool", sem_handles)

    return nc


def _host_inputs(L0_attn_x, L0_attn_W_qkv, L0_attn_b_qkv, L0_attn_W_out, L0_attn_b_out,
                 L0_ffn_W, L0_ffn_b, L0_ffn_lora_down, L0_ffn_lora_up,
                 L0_ffn_down_aux, L0_ffn_up_aux):
    x = np.asarray(L0_attn_x, dtype=np.float32).reshape(NCORES * TCORE, D)
    kindv, hv, dv = np.meshgrid(np.arange(3), np.arange(16), np.arange(64),
                                indexing="ij")
    perm = (192 * hv + 64 * kindv + dv).reshape(-1)
    wqkv = np.ascontiguousarray(np.asarray(L0_attn_W_qkv, dtype=np.float32)[:, perm])
    w1t = np.ascontiguousarray(np.asarray(L0_ffn_W, dtype=np.float32).T)
    wout = np.ascontiguousarray(np.asarray(L0_attn_W_out, dtype=np.float32))
    upat = np.ascontiguousarray(np.asarray(L0_ffn_up_aux, dtype=np.float32).T)
    lup = np.ascontiguousarray(np.asarray(L0_ffn_lora_up, dtype=np.float32))
    ldn = np.ascontiguousarray(np.asarray(L0_ffn_lora_down, dtype=np.float32))
    dnaux = np.ascontiguousarray(np.asarray(L0_ffn_down_aux, dtype=np.float32))
    bqd = np.ascontiguousarray(
        np.asarray(L0_attn_b_qkv, dtype=np.float32)[perm].reshape(24, 128).T)
    b1d = np.ascontiguousarray(
        np.asarray(L0_ffn_b, dtype=np.float32).reshape(8, 128).T)
    bod = np.ascontiguousarray(
        np.asarray(L0_attn_b_out, dtype=np.float32).reshape(8, 128).T)
    mask = np.zeros((128, 128), dtype=np.float32)
    for g in range(16):
        for t in range(8):
            for p in range(2):
                for j in range(8):
                    mask[g * 8 + t, p * 64 + j * 8 + t] = 1.0
    ident = np.zeros((128, 64), dtype=np.float32)
    ident[0:64] = np.eye(64, dtype=np.float32)
    ident[64:128] = np.eye(64, dtype=np.float32)
    shared = dict(wqkv=wqkv, w1t=w1t, wout=wout, upat=upat, lup=lup, ldn=ldn,
                  dnaux=dnaux, bqd=bqd, b1d=b1d, bod=bod, maskd=mask, identd=ident)
    in_maps = []
    for c in range(NCORES):
        shard = x[c * TCORE:(c + 1) * TCORE]
        xt = np.ascontiguousarray(shard.T)
        m = dict(shared)
        m["xt"] = xt
        in_maps.append(m)
    return in_maps


def kernel(L0_attn_x, L0_attn_W_qkv, L0_attn_b_qkv, L0_attn_W_out, L0_attn_b_out,
           L0_ffn_W, L0_ffn_b, L0_ffn_lora_down, L0_ffn_lora_up,
           L0_ffn_down_aux, L0_ffn_up_aux):
    in_maps = _host_inputs(L0_attn_x, L0_attn_W_qkv, L0_attn_b_qkv, L0_attn_W_out,
                           L0_attn_b_out, L0_ffn_W, L0_ffn_b, L0_ffn_lora_down,
                           L0_ffn_lora_up, L0_ffn_down_aux, L0_ffn_up_aux)
    nc = build_nc()
    res = run_bass_kernel_spmd(nc, in_maps, core_ids=list(range(NCORES)))
    outs = []
    for c in range(NCORES):
        yt = res.results[c]["yt"]
        outs.append(np.ascontiguousarray(yt.T))
    full = np.concatenate(outs, axis=0)
    return full.reshape(np.asarray(L0_attn_x).shape).astype(np.float32)


# version 44
# speedup vs baseline: 1.0647x; 1.0103x over previous
import numpy as np
from contextlib import ExitStack

import concourse.bass as bass
import concourse.mybir as mybir
from concourse.bass_utils import run_bass_kernel_spmd

F32 = mybir.dt.float32
BF16 = mybir.dt.bfloat16
AF = mybir.ActivationFunctionType
ALU = mybir.AluOpType

NCORES = 8
TCORE = 2048
SLAB = 512
NSLAB = TCORE // SLAB
HALF = 256
NGRP_H = HALF // 8
NBATCH_H = NGRP_H // 4
D = 1024
NCH = 8
EPS = 1e-5

DEBUG = 0

ENGS = ("pe", "dve", "act", "pool", "sp")
DMASEMS = ("ldc", "ldw1", "ldq0", "ldq1", "ldwo", "ldx0", "ldx1", "st0", "st1",
           "dbg0", "dbg1", "dbg2", "dbg3", "dbg4", "dbg5", "dbg6", "dbg7", "dbg8")


class Sched:

    def __init__(self):
        self.q = {e: [] for e in ENGS}
        self.counts = {}
        for e in ENGS:
            self.counts[e] = 0
        for e in DMASEMS:
            self.counts[e] = 0

    def op(self, eng, fn, deps=(), sem=None, inc=1):
        sem = sem or eng
        dl = []
        for d in deps:
            if d is None:
                continue
            if isinstance(d, list):
                dl.extend(x for x in d if x is not None)
            else:
                dl.append(d)
        self.counts[sem] += inc
        tok = (sem, self.counts[sem])
        self.q[eng].append((dl, fn, sem, inc))
        return tok

    def replay(self, engobj, eng, sems):
        marks = {}
        for deps, fn, sem, inc in self.q[eng]:
            for (ps, pc) in deps:
                if marks.get(ps, 0) < pc:
                    engobj.wait_ge(sems[ps], pc)
                    marks[ps] = pc
            inst = fn(engobj)
            if inst is not None and inc:
                inst.then_inc(sems[sem], inc)


def build_nc():
    nc = bass.Bass("TRN2", target_bir_lowering=False)

    xt = nc.dram_tensor("xt", [D, TCORE], F32, kind="ExternalInput").ap()
    wqkv = nc.dram_tensor("wqkv", [D, 3 * D], F32, kind="ExternalInput").ap()
    w1t = nc.dram_tensor("w1t", [D, D], F32, kind="ExternalInput").ap()
    wout = nc.dram_tensor("wout", [D, D], F32, kind="ExternalInput").ap()
    upat = nc.dram_tensor("upat", [50, D], F32, kind="ExternalInput").ap()
    lup = nc.dram_tensor("lup", [50, 4], F32, kind="ExternalInput").ap()
    ldn = nc.dram_tensor("ldn", [4, 100], F32, kind="ExternalInput").ap()
    dnaux = nc.dram_tensor("dnaux", [100, D], F32, kind="ExternalInput").ap()
    bqd = nc.dram_tensor("bqd", [128, 24], F32, kind="ExternalInput").ap()
    b1d = nc.dram_tensor("b1d", [128, 8], F32, kind="ExternalInput").ap()
    bod = nc.dram_tensor("bod", [128, 8], F32, kind="ExternalInput").ap()
    maskd = nc.dram_tensor("maskd", [128, 128], F32, kind="ExternalInput").ap()
    identd = nc.dram_tensor("identd", [128, 64], F32, kind="ExternalInput").ap()
    yt = nc.dram_tensor("yt", [D, TCORE], F32, kind="ExternalOutput").ap()
    if DEBUG:
        dbg_y = nc.dram_tensor("dbg_y", [128, NCH * SLAB], F32, kind="ExternalOutput").ap()
        dbg_h2 = nc.dram_tensor("dbg_h2", [128, NCH * SLAB], BF16, kind="ExternalOutput").ap()
        dbg_qg = nc.dram_tensor("dbg_qg", [64, NGRP_H * 128], BF16, kind="ExternalOutput").ap()
        dbg_kg = nc.dram_tensor("dbg_kg", [64, NGRP_H * 128], BF16, kind="ExternalOutput").ap()
        dbg_vg = nc.dram_tensor("dbg_vg", [64, NGRP_H * 128], BF16, kind="ExternalOutput").ap()
        dbg_c2 = nc.dram_tensor("dbg_c2", [128, 8 * SLAB], BF16, kind="ExternalOutput").ap()
        dbg_z = nc.dram_tensor("dbg_z", [128, NCH * SLAB], F32, kind="ExternalOutput").ap()
        dbg_xh = nc.dram_tensor("dbg_xh", [128, NCH * SLAB], BF16, kind="ExternalOutput").ap()
        dbg_rows = nc.dram_tensor("dbg_rows", [128, 2048], F32, kind="ExternalOutput").ap()

    s = Sched()
    with ExitStack() as ctx:
        al = lambda *a, **k: ctx.enter_context(nc.sbuf_tensor(*a, **k))
        alp = lambda *a, **k: ctx.enter_context(nc.psum_tensor(*a, **k))
        stg = al("stg", [128, 3072], F32)
        rowb = al("rowb", [128, 1024], BF16)
        wqkvb = al("wqkvb", [128, 8 * 3072], BF16)
        weffb = al("weffb", [128, 8 * 1024], BF16)
        woutb = al("woutb", [128, 8 * 1024], BF16)
        rbuf = [al(f"rbuf{i}", [128, NCH * SLAB], F32) for i in range(3)]
        xhat = al("xhat", [128, NCH * SLAB], BF16)
        x2b = al("x2b", [128, NCH * SLAB], BF16)
        hbuf = al("hbuf", [128, NCH * SLAB], BF16)
        qg = al("qg", [64, NGRP_H * 128], BF16)
        kg = al("kg", [64, NGRP_H * 128], BF16)
        vg = al("vg", [64, NGRP_H * 128], BF16)
        ebuf = [al(f"ebuf{i}", [128, 512], BF16) for i in range(2)]
        vb1 = [al(f"vb1_{i}", [128, 4 * 65], BF16) for i in range(2)]
        ctxg = [al(f"ctxg{i}", [128, 256], BF16) for i in range(2)]
        c2sb = al("c2sb", [128, 8 * SLAB], BF16)
        bq = al("bq", [128, 24], F32)
        b1 = al("b1", [128, 8], F32)
        bo = al("bo", [128, 8], F32)
        maskf = al("maskf", [128, 128], F32)
        maskb = al("maskb", [128, 128], BF16)
        identf = al("identf", [128, 64], F32)
        idb = al("idb", [128, 64], BF16)
        ones128 = al("ones128", [128, 128], BF16)
        epsc = al("epsc", [128, 1], F32)
        ksc = al("ksc", [128, 1], F32)
        nhc = al("nhc", [128, 1], F32)
        ivdc = al("ivdc", [128, 1], F32)
        rcol = [al(f"rcol{i}", [128, 4], F32) for i in range(2)]
        d4 = [al(f"d4_{i}", [128, 256], BF16) for i in range(2)]
        lora_up = al("lora_up", [50, 4], F32)
        lora_dn = al("lora_dn", [4, 100], F32)
        pbank = [alp(f"pbank{i}", [128, 512], F32) for i in range(2)]
        pstat = [alp(f"pstat{i}", [128, 512], F32) for i in range(2)]
        pscore = alp("pscore", [128, 512], F32)
        pctx = alp("pctx", [128, 512], F32)
        pvgp = alp("pvgp", [128, 1024], BF16)
        pc2 = alp("pc2", [128, 512], F32)

        sem_handles = {}
        for name in ENGS + DMASEMS:
            sem_handles[name] = ctx.enter_context(nc.semaphore(name))
        block = ctx.enter_context(nc.Block())

        c2f32 = c2sb[:].bitcast(F32)
        dna = c2f32[0:100, 0:1024]
        upa = c2f32[0:64, 1024:2048]
        w1t_sl2 = xhat[:].bitcast(F32)[:, 1024:2048]
        rows_m = stg[:, 0:512]
        rows_rstd = stg[:, 512:1024]
        rows_t = stg[:, 1024:1536]
        rows_v = stg[:, 1536:2048]

        def dma(dst, src, sem, deps=()):
            return s.op("sp", lambda e, dst=dst, src=src: e.dma_start(out=dst, in_=src),
                        deps=deps, sem=sem, inc=16)

        t_ones = s.op("dve", lambda e: e.memset(ones128[:], 1.0))
        t_eps = s.op("dve", lambda e: e.memset(epsc[:], EPS))
        t_ksc = s.op("dve", lambda e: e.memset(ksc[:], 0.125))
        t_nh = s.op("dve", lambda e: e.memset(nhc[:], -0.5))
        t_ivd = s.op("dve", lambda e: e.memset(ivdc[:], 1.0 / 1024.0))
        t_v1 = [s.op("dve", lambda e, i=i: e.memset(
            vb1[i][:].rearrange("p (g c) -> p g c", c=65)[:, :, 64:65], 1.0)) for i in range(2)]
        d_lora = [dma(dst, srcd, fam) for dst, srcd, fam in
                  ((lora_up[:], lup, "dbg0"), (lora_dn[:], ldn, "dbg1"),
                   (dna[0:100, :], dnaux, "dbg2"), (upa[0:50, :], upat, "dbg3"))]
        for dst, srcd in ((bq[:], bqd), (b1[:], b1d), (bo[:], bod),
                          (maskf[:], maskd), (identf[:], identd)):
            t_ldc = dma(dst, srcd, "ldc")
        d_b = [t_ldc] * 3
        t_maskb = s.op("dve", lambda e: e.tensor_copy(out=maskb[:], in_=maskf[:]),
                       deps=[t_ldc])
        t_idb = s.op("dve", lambda e: e.tensor_copy(out=idb[:], in_=identf[:]),
                     deps=[t_ldc])
        t_bq = s.op("dve", lambda e: e.tensor_scalar(
            out=bq[:, 8:16], in0=bq[:, 8:16], scalar1=0.125, scalar2=None,
            op0=ALU.mult), deps=[t_ldc])
        t_b1 = s.op("dve", lambda e: e.tensor_scalar(
            out=b1[:], in0=b1[:], scalar1=0.5, scalar2=None, op0=ALU.mult),
            deps=[t_ldc])

        xt_v = xt.rearrange("(c p) n -> p c n", p=128)
        yt_v = yt.rearrange("(c p) n -> p c n", p=128)
        t_x = [None] * NSLAB
        t_x[0] = dma(rbuf[0][:].rearrange("p (c n) -> p c n", n=SLAB),
                     xt_v[:, :, 0:SLAB], "ldx0")

        A_sl = stg[0:4, 0:1024]
        B_sl = stg[0:100, 1024:2048]
        w1t_sl = stg[:, 2048:3072]

        t_A_mm = s.op("pe", lambda e: (
            e.matmul(out=pbank[0][0:4, :], lhsT=lora_up[:], rhs=upa[0:50, 0:512],
                     start=True, stop=True),
            e.matmul(out=pbank[1][0:4, :], lhsT=lora_up[:], rhs=upa[0:50, 512:1024],
                     start=True, stop=True))[-1], deps=d_lora[:2] + [d_lora[3]])
        t_A = s.op("dve", lambda e: (
            e.tensor_copy(out=A_sl[:, 0:512], in_=pbank[0][0:4, :]),
            e.tensor_copy(out=A_sl[:, 512:1024], in_=pbank[1][0:4, :]))[-1],
            deps=[t_A_mm])
        t_B_mm = s.op("pe", lambda e: (
            e.matmul(out=pbank[0][0:100, :], lhsT=lora_dn[:], rhs=A_sl[:, 0:512],
                     start=True, stop=True),
            e.matmul(out=pbank[1][0:100, :], lhsT=lora_dn[:], rhs=A_sl[:, 512:1024],
                     start=True, stop=True))[-1], deps=[t_A, d_lora[1]])
        t_B = s.op("dve", lambda e: (
            e.tensor_copy(out=B_sl[:, 0:512], in_=pbank[0][0:100, :]),
            e.tensor_copy(out=B_sl[:, 512:1024], in_=pbank[1][0:100, :]))[-1],
            deps=[t_B_mm])
        t_weff_parts = []
        for kc in range(NCH):
            slw = w1t_sl if kc % 2 == 0 else w1t_sl2
            war = [t_weff_parts[kc - 2]] if kc >= 2 else []
            d_w1 = dma(slw, w1t.rearrange("(c p) m -> c p m", p=128)[kc],
                       "ldw1" if kc % 2 == 0 else "ldq1", deps=war)
            t_w1h = s.op("dve", lambda e, slw=slw: e.tensor_scalar(
                out=slw, in0=slw, scalar1=0.5, scalar2=None, op0=ALU.mult),
                deps=[d_w1])
            cbk0 = [pbank[0], pbank[1], pscore, pctx][(2 * kc) % 4]
            cbk1 = [pbank[0], pbank[1], pscore, pctx][(2 * kc + 1) % 4]
            t_c = s.op("pe", lambda e, kc=kc, cbk0=cbk0, cbk1=cbk1: (
                e.matmul(out=cbk0[:], lhsT=dna[0:100, kc * 128:(kc + 1) * 128],
                         rhs=B_sl[:, 0:512], start=True, stop=True),
                e.matmul(out=cbk1[:], lhsT=dna[0:100, kc * 128:(kc + 1) * 128],
                         rhs=B_sl[:, 512:1024], start=True, stop=True))[-1],
                deps=[t_B, d_lora[2]] + ([t_weff_parts[kc - 2]] if kc >= 2 else []))
            t_we = s.op("dve", lambda e, kc=kc, slw=slw, cbk0=cbk0, cbk1=cbk1: (
                e.scalar_tensor_tensor(
                    out=weffb[:, kc * 1024:kc * 1024 + 512], in0=cbk0[:],
                    scalar=0.5, in1=slw[:, 0:512], op0=ALU.mult, op1=ALU.add),
                e.scalar_tensor_tensor(
                    out=weffb[:, kc * 1024 + 512:(kc + 1) * 1024], in0=cbk1[:],
                    scalar=0.5, in1=slw[:, 512:1024], op0=ALU.mult, op1=ALU.add))[-1],
                deps=[t_c, t_w1h])
            t_weff_parts.append(t_we)
        t_weffb = t_weff_parts[-1]

        wqkv_v = wqkv.rearrange("(c p) m -> c p m", p=128)
        wout_v = wout.rearrange("(c p) m -> c p m", p=128)
        for kc in range(NCH):
            t_wqkvb = s.op("pool", lambda e, kc=kc: e.dma_start(
                out=wqkvb[:, kc * 3072:(kc + 1) * 3072], in_=wqkv_v[kc]),
                deps=[t_ldc] if kc == 0 else [], sem="ldq0", inc=16)
        for kc in range(NCH):
            t_woutb = s.op("pool", lambda e, kc=kc: e.dma_start(
                out=woutb[:, kc * 1024:(kc + 1) * 1024], in_=wout_v[kc]),
                deps=[], sem="ldwo", inc=16)

        bank_free = {"pscore": [t_woutb], "pstat0": [t_B_mm], "pstat1": [t_B_mm],
                     "pctx": [t_woutb], "pbank0": [t_woutb], "pbank1": [t_woutb]}
        proj_banks = [("pbank0", pbank[0]), ("pbank1", pbank[1]),
                      ("pscore", pscore), ("pctx", pctx)]
        t_xhat_free = [t_ones]
        t_hbuf_free = [t_ones]
        t_rowb_free = [t_ones]
        t_rows_free = [t_woutb]
        t_qkvg_free = [t_ones]
        t_ebuf_free = [[t_ones], [t_ones]]
        t_vb1_free = [[t_v1[0]], [t_v1[1]]]
        t_ctxg_free = [[t_ones], [t_ones]]
        t_rcol_free = [[t_ones], [t_ones]]
        t_d4_free = [[t_ones], [t_ones]]
        t_pvgp_free = [t_ones]
        t_pc2_free = [t_ones]
        t_c2_free = [t_weffb]
        t_store = [None] * NSLAB

        ln_calls = [0]
        t_x2b_free = [t_ones]

        def ln_front_start():
            return dict(cast=[], x2=[], mm_s=None, mm_q=None)

        def ln_front_square(st, c):
            nonlocal t_x2b_free
            t_c = st["cast"][c]
            t_q = s.op("dve", lambda e, c=c: e.tensor_tensor(
                out=x2b[:, c * SLAB:(c + 1) * SLAB],
                in0=xhat[:, c * SLAB:(c + 1) * SLAB],
                in1=xhat[:, c * SLAB:(c + 1) * SLAB], op=ALU.mult),
                deps=([t_c] if not isinstance(t_c, list) else t_c)
                     + (list(t_x2b_free) if c == 0 else []))
            st["x2"].append(t_q)

        def ln_front_chunk(st, c, tok, src, pre_cast=None, sq=True):
            nonlocal t_xhat_free
            if pre_cast is not None:
                t_c = pre_cast
            else:
                extra = list(t_xhat_free) if c == 0 else []
                t_c = s.op("dve", lambda e, c=c: e.tensor_copy(
                    out=xhat[:, c * SLAB:(c + 1) * SLAB],
                    in_=src[:, c * SLAB:(c + 1) * SLAB]),
                    deps=[tok] + extra)
            st["cast"].append(t_c)
            if sq:
                ln_front_square(st, c)

        def ln_front_finish(st, which="sq"):
            for c in range(NCH):
                cst = st["cast"][c]
                dep_s = (cst if isinstance(cst, list) else [cst]) + [t_ones]
                if c == 0:
                    dep_s += bank_free["pstat0"]
                if "s" in which:
                    st["mm_s"] = s.op("pe", lambda e, c=c: e.matmul(
                        out=pstat[0][:], lhsT=ones128[:],
                        rhs=xhat[:, c * SLAB:(c + 1) * SLAB],
                        start=(c == 0), stop=(c == NCH - 1), skip_group_check=True),
                        deps=dep_s, inc=(1 if c == NCH - 1 else 0))
            for c in range(NCH):
                if "q" not in which:
                    break
                dep_q = [st["x2"][c]]
                if c == 0:
                    dep_q += bank_free["pstat1"]
                if True:
                    st["mm_q"] = s.op("pe", lambda e, c=c: e.matmul(
                        out=pstat[1][:], lhsT=ones128[:],
                        rhs=x2b[:, c * SLAB:(c + 1) * SLAB],
                        start=(c == 0), stop=(c == NCH - 1), skip_group_check=True),
                        deps=dep_q, inc=(1 if c == NCH - 1 else 0))
            return st

        def ln_front(chunk_toks, src):
            if not isinstance(chunk_toks, list):
                chunk_toks = [chunk_toks] * NCH
            st = ln_front_start()
            for c in range(NCH):
                ln_front_chunk(st, c, chunk_toks[c], src)
            ln_front_finish(st)
            return st

        def ln_back(fr, dst=None, dst_war=None, per_chunk=False):
            nonlocal t_xhat_free, t_rowb_free, t_rows_free, t_x2b_free
            ln_calls[0] += 1
            t_mm_s, t_mm_q = fr["mm_s"], fr["mm_q"]
            t_cast = fr["cast"][-1]
            if isinstance(t_cast, list):
                t_cast = t_cast[-1]
            t_rm = s.op("dve", lambda e: e.tensor_scalar(
                out=rowb[:, 0:512], in0=pstat[0][:], scalar1=1.0 / 1024.0,
                scalar2=None, op0=ALU.mult), deps=[t_mm_s] + t_rowb_free)
            t_msq = s.op("dve", lambda e: e.tensor_tensor(
                out=rows_t, in0=rowb[:, 0:512], in1=rowb[:, 0:512],
                op=ALU.mult), deps=[t_rm] + t_rows_free)
            t_var = s.op("dve", lambda e: e.scalar_tensor_tensor(
                out=rows_v, in0=pstat[1][:], scalar=ivdc[:, 0:1], in1=rows_t,
                op0=ALU.mult, op1=ALU.subtract),
                deps=[t_mm_q, t_msq] + t_rows_free)
            t_ln = s.op("act", lambda e: e.activation(
                out=rows_m, in_=rows_v, func=AF.Ln, bias=epsc[:, 0:1], scale=1.0),
                deps=[t_var, t_eps] + t_rows_free)
            t_rstd = s.op("act", lambda e: e.activation(
                out=rowb[:, 512:1024], in_=rows_m, func=AF.Exp, bias=0.0,
                scale=nhc[:, 0:1]), deps=[t_ln, t_nh] + t_rowb_free)
            xv = xhat[:].rearrange("p (c n) -> p c n", n=SLAB)
            dstb = hbuf if dst is None else dst
            war = list(t_hbuf_free) if dst_war is None else list(dst_war)
            hv = dstb[:].rearrange("p (c n) -> p c n", n=SLAB)
            t_apps = []
            t_subs = []
            if per_chunk:
                for c in range(NCH):
                    cst = fr["cast"][c]
                    cd = cst if isinstance(cst, list) else [cst]
                    t_sub = s.op("dve", lambda e, c=c: e.tensor_tensor(
                        out=dstb[:, c * SLAB:(c + 1) * SLAB],
                        in0=xhat[:, c * SLAB:(c + 1) * SLAB],
                        in1=rowb[:, 0:512], op=ALU.subtract),
                        deps=[t_rm] + cd + (war if c == 0 else []))
                    t_subs.append(t_sub)
                for c in range(NCH):
                    t_app = s.op("dve", lambda e, c=c: e.tensor_tensor(
                        out=dstb[:, c * SLAB:(c + 1) * SLAB],
                        in0=dstb[:, c * SLAB:(c + 1) * SLAB],
                        in1=rowb[:, 512:1024], op=ALU.mult),
                        deps=[t_subs[c], t_rstd])
                    t_apps.append(t_app)
            else:
                for hh in range(2):
                    ho = hh * HALF
                    mb = rowb[:, ho:ho + HALF].unsqueeze(1).broadcast_to(
                        [128, NCH, HALF])
                    t_sub = s.op("dve", lambda e, ho=ho, mb=mb: e.tensor_tensor(
                        out=hv[:, :, ho:ho + HALF], in0=xv[:, :, ho:ho + HALF],
                        in1=mb, op=ALU.subtract),
                        deps=[t_rm, t_cast] + (war if hh == 0 else []))
                    t_subs.append(t_sub)
                for hh in range(2):
                    ho = hh * HALF
                    rb = rowb[:, 512 + ho:512 + ho + HALF].unsqueeze(1).broadcast_to(
                        [128, NCH, HALF])
                    t_app = s.op("dve", lambda e, ho=ho, rb=rb: e.tensor_tensor(
                        out=hv[:, :, ho:ho + HALF], in0=hv[:, :, ho:ho + HALF],
                        in1=rb, op=ALU.mult), deps=[t_subs[hh], t_rstd])
                    t_apps.append(t_app)
            bank_free["pstat0"] = [t_rm, t_msq]
            bank_free["pstat1"] = [t_var]
            t_xhat_free = [t_subs[-1]]
            t_x2b_free = [t_mm_q]
            t_rowb_free = [t_subs[-1], t_apps[-1]]
            t_rows_free = [t_var, t_ln, t_rstd]
            return t_apps

        def projection(t_h, t_w, wbuf, wbase_stride, nout, bias, t_bias, resid_in,
                       t_resid_in, resid_out, t_resid_out_war, operand=None,
                       ln_next_src=None, ln_next_tok=None, pe_mid=None):
            toks = []
            t_mm = None
            fr = ln_front_start() if ln_next_src is not None else None
            nb = len(proj_banks)
            opnd = hbuf if operand is None else operand
            per_kc = isinstance(t_h, list) and len(t_h) == NCH
            for oc in range(nout):
                bname, bk = proj_banks[oc % nb]
                if per_kc:
                    for kc in range(NCH):
                        t_mm = s.op("pe", lambda e, oc=oc, bk=bk, kc=kc: e.matmul(
                            out=bk[:],
                            lhsT=wbuf[:, kc * wbase_stride + oc * 128:
                                      kc * wbase_stride + oc * 128 + 128],
                            rhs=opnd[:, kc * SLAB:(kc + 1) * SLAB],
                            start=(kc == 0), stop=(kc == NCH - 1),
                            skip_group_check=True),
                            deps=[t_h[kc], t_w]
                                 + (bank_free[bname] if kc == 0 else []),
                            inc=(1 if kc == NCH - 1 else 0))
                else:
                    t_mm = s.op("pe", lambda e, oc=oc, bk=bk: [
                        e.matmul(out=bk[:],
                                 lhsT=wbuf[:, kc * wbase_stride + oc * 128:
                                           kc * wbase_stride + oc * 128 + 128],
                                 rhs=opnd[:, kc * SLAB:(kc + 1) * SLAB],
                                 start=(kc == 0), stop=(kc == NCH - 1))
                        for kc in range(NCH)][-1],
                        deps=[t_h, t_w] + bank_free[bname])
                t_pc = None
                if fr is not None and ln_next_tok is None:
                    xw = list(t_xhat_free) if oc == 0 else []
                    t_pc = s.op("dve", lambda e, oc=oc, bk=bk: e.scalar_tensor_tensor(
                        out=xhat[:, oc * SLAB:(oc + 1) * SLAB],
                        in0=bk[:], scalar=bias[:, oc:oc + 1],
                        in1=resid_in[:, oc * SLAB:(oc + 1) * SLAB],
                        op0=ALU.add, op1=ALU.add),
                        deps=[t_mm, t_resid_in, t_bias] + xw)
                t_ev = s.op("dve", lambda e, oc=oc, bk=bk: e.scalar_tensor_tensor(
                    out=resid_out[:, oc * SLAB:(oc + 1) * SLAB],
                    in0=bk[:], scalar=bias[:, oc:oc + 1],
                    in1=resid_in[:, oc * SLAB:(oc + 1) * SLAB],
                    op0=ALU.add, op1=ALU.add),
                    deps=[t_mm, t_resid_in, t_bias] + t_resid_out_war)
                bank_free[bname] = [t_ev]
                toks.append(t_ev)
                if fr is not None:
                    ln_front_chunk(fr, oc,
                                   (ln_next_tok if ln_next_tok is not None else t_ev),
                                   ln_next_src, pre_cast=t_pc)
                if pe_mid is not None and oc == 1:
                    pe_mid(0)
                if pe_mid is not None and oc == 3:
                    pe_mid(1)
            if fr is not None:
                ln_front_finish(fr)
            return toks, t_mm, fr

        def qkv_and_attention(t_h, slab, resid_y, t_y, resid_z, z_war, bo_, t_ln3_src):
            nonlocal t_qkvg_free, t_pvgp_free, t_pc2_free
            nonlocal t_ebuf_free, t_vb1_free, t_ctxg_free, t_rcol_free, t_d4_free
            last_qkv_mm = [None]
            dbg_qkv_toks = []
            t_c2 = [None]
            c2_half = {}
            ev2 = [None] * NCH
            ev2_h1 = [None] * NCH
            pc3 = [[None] * NCH, [None] * NCH]
            fr3 = ln_front_start()
            nb = len(proj_banks)

            def wout_half(half, c2toks, with_ln):
                hoff = half * HALF
                for oc in range(NCH):
                    bname, bk = proj_banks[oc % nb]
                    t_mm = s.op("pe", lambda e, oc=oc, bk=bk, hoff=hoff: [
                        e.matmul(out=bk[:, 0:HALF],
                                 lhsT=woutb[:, j * 1024 + oc * 128:j * 1024 + oc * 128 + 128],
                                 rhs=c2sb[:, j * SLAB + hoff:j * SLAB + hoff + HALF],
                                 start=(j == 0), stop=(j == NCH - 1))
                        for j in range(NCH)][-1],
                        deps=c2toks + [t_woutb] + bank_free[bname])
                    if with_ln or half == 0:
                        xw = list(t_xhat_free) if (half == 0 and oc == 0) else []
                        t_pc = s.op("dve", lambda e, oc=oc, bk=bk, hoff=hoff:
                                    e.scalar_tensor_tensor(
                            out=xhat[:, oc * SLAB + hoff:oc * SLAB + hoff + HALF],
                            in0=bk[:, 0:HALF], scalar=bo_[:, oc:oc + 1],
                            in1=resid_y[:, oc * SLAB + hoff:oc * SLAB + hoff + HALF],
                            op0=ALU.add, op1=ALU.add),
                            deps=[t_mm, t_y, t_ldc] + xw)
                        pc3[half][oc] = t_pc
                    t_ev = s.op("dve", lambda e, oc=oc, bk=bk, hoff=hoff:
                                e.scalar_tensor_tensor(
                        out=resid_z[:, oc * SLAB + hoff:oc * SLAB + hoff + HALF],
                        in0=bk[:, 0:HALF], scalar=bo_[:, oc:oc + 1],
                        in1=resid_y[:, oc * SLAB + hoff:oc * SLAB + hoff + HALF],
                        op0=ALU.add, op1=ALU.add),
                        deps=[t_mm, t_y, t_ldc] + z_war)
                    bank_free[bname] = [t_ev]
                    if half == 0:
                        ev2_h1[oc] = t_ev
                    else:
                        ev2[oc] = t_ev
                        if with_ln:
                            ln_front_chunk(fr3, oc, None, t_ln3_src,
                                           pre_cast=[pc3[0][oc], pc3[1][oc]])
                if half == 1 and with_ln:
                    ln_front_finish(fr3)

            qkv_banks = proj_banks + [("pstat0", pstat[0]), ("pstat1", pstat[1])]
            for half in range(2):
                hoff = half * HALF
                ev_toks_q, ev_toks_kv = [], []
                for oc in range(24):
                    bname, bk = qkv_banks[oc % 6]
                    t_mm = s.op("pe", lambda e, oc=oc, bk=bk, hoff=hoff: [
                        e.matmul(out=bk[:, 0:HALF],
                                 lhsT=wqkvb[:, kc * 3072 + oc * 128:
                                            kc * 3072 + oc * 128 + 128],
                                 rhs=hbuf[:, kc * SLAB + hoff:kc * SLAB + hoff + HALF],
                                 start=(kc == 0), stop=(kc == NCH - 1))
                        for kc in range(NCH)][-1],
                        deps=[t_h[half] if isinstance(t_h, list) else t_h,
                              t_wqkvb] + bank_free[bname])
                    last_qkv_mm[0] = t_mm
                    kind = oc // 8
                    c = oc % 8
                    if kind == 0:
                        dst = qg
                        offs = (8 * c, 64 + 8 * c)
                    elif kind == 1:
                        dst = kg
                        offs = (16 * c, 16 * c + 8)
                    else:
                        dst = vg
                        offs = (16 * c, 16 * c + 8)
                    dstv = dst[:].rearrange("d (g c) -> d g c", c=128)
                    if kind == 0:
                        t_ev = s.op("act", lambda e, bk=bk, oc=oc, dstv=dstv, offs=offs: (
                            e.activation(out=dstv[:, :, offs[0]:offs[0] + 8],
                                         in_=bk[0:64, 0:HALF].rearrange(
                                             "d (g t) -> d g t", t=8),
                                         func=AF.Identity, bias=bq[0:64, oc:oc + 1],
                                         scale=1.0),
                            e.activation(out=dstv[:, :, offs[1]:offs[1] + 8],
                                         in_=bk[64:128, 0:HALF].rearrange(
                                             "d (g t) -> d g t", t=8),
                                         func=AF.Identity, bias=bq[64:128, oc:oc + 1],
                                         scale=1.0))[-1],
                            deps=[t_mm, t_bq] + t_qkvg_free, inc=2)
                        ev_toks_q.append(t_ev)
                    else:
                        sc = ksc[:, 0:1] if kind == 1 else 1.0
                        t_ev = s.op("act", lambda e, bk=bk, oc=oc, dstv=dstv,
                                    offs=offs, sc=sc: (
                            e.activation(out=dstv[:, :, offs[0]:offs[0] + 8],
                                         in_=bk[0:64, 0:HALF].rearrange(
                                             "d (g t) -> d g t", t=8),
                                         func=AF.Identity, bias=bq[0:64, oc:oc + 1],
                                         scale=(sc if isinstance(sc, float)
                                                else ksc[0:64, 0:1])),
                            e.activation(out=dstv[:, :, offs[1]:offs[1] + 8],
                                         in_=bk[64:128, 0:HALF].rearrange(
                                             "d (g t) -> d g t", t=8),
                                         func=AF.Identity, bias=bq[64:128, oc:oc + 1],
                                         scale=(sc if isinstance(sc, float)
                                                else ksc[64:128, 0:1])))[-1],
                            deps=[t_mm, t_bq, t_ksc] + t_qkvg_free, inc=2)
                        ev_toks_kv.append(t_ev)
                    bank_free[bname] = [t_ev]
                t_q_ready = ev_toks_q[-1]
                t_kv_ready = ev_toks_kv[-1]
                if DEBUG and slab == 0 and half == 0:
                    t_dq = dma(dbg_qg, qg[:], "dbg2", deps=[t_q_ready])
                    t_dk = dma(dbg_kg, kg[:], "dbg3", deps=[t_kv_ready])
                    t_dv = dma(dbg_vg, vg[:], "dbg4", deps=[t_kv_ready])
                    dbg_qkv_toks.extend([t_dq, t_dk, t_dv])

                if half == 1:
                    wout_half(0, c2_half[0], with_ln=False)

                sc_banks = [(pscore, "pscore"), (pstat[0], "pstat0")]
                ctx_banks = [(pctx, "pctx"), (pstat[1], "pstat1")]
                t_sc = {}
                t_exp = {}
                t_mask = {}
                t_vt = {}
                t_vev = {}
                t_ctx = {}
                t_rc = {}
                t_d4 = {}
                t_d4i = {}
                t_cge = {}
                t_ct = {}
                half_c2 = []
                NB = NBATCH_H
                for b in range(NB + 2):
                    par = b % 2
                    if b < NB:
                        g0 = b * 4
                        sb, sbname = sc_banks[par]
                        war_sc = list(bank_free[sbname])
                        t_sc[b] = s.op("pe", lambda e, g0=g0, sb=sb: [
                            e.matmul(out=sb[:, i * 128:(i + 1) * 128],
                                     lhsT=kg[:, (g0 + i) * 128:(g0 + i + 1) * 128],
                                     rhs=qg[:, (g0 + i) * 128:(g0 + i + 1) * 128],
                                     start=True, stop=True) for i in range(4)][-1],
                            deps=[t_q_ready, t_kv_ready] + war_sc, inc=4)
                        war_vt = [t_vev[b - 1]] if b >= 1 else t_pvgp_free
                        t_vt[b] = s.op("pe", lambda e, g0=g0: [
                            e.transpose(out=pvgp[:, i * 64:(i + 1) * 64],
                                        in_=vg[:, (g0 + i) * 128:(g0 + i + 1) * 128],
                                        identity=idb[0:64, :]) for i in range(4)][-1],
                            deps=[t_kv_ready, t_idb] + war_vt, inc=4)
                        war_e = [t_ctx[b - 2]] if b >= 2 else t_ebuf_free[par]
                        t_exp[b] = s.op("act", lambda e, par=par, sb=sb: e.activation(
                            out=ebuf[par][:], in_=sb[:], func=AF.Exp),
                            deps=[t_sc[b]] + war_e)
                        bank_free[sbname] = [t_exp[b]]
                        t_mask[b] = s.op("dve", lambda e, par=par: e.tensor_tensor(
                            out=ebuf[par][:].rearrange("p (g c) -> p g c", c=128),
                            in0=ebuf[par][:].rearrange("p (g c) -> p g c", c=128),
                            in1=maskb[:].unsqueeze(1).broadcast_to([128, 4, 128]),
                            op=ALU.mult), deps=[t_exp[b], t_maskb])
                        war_v = [t_ctx[b - 2]] if b >= 2 else t_vb1_free[par]
                        t_vev[b] = s.op("dve", lambda e, par=par: e.tensor_copy(
                            out=vb1[par][:].rearrange("p (g c) -> p g c", c=65)[:, :, 0:64],
                            in_=pvgp[:, 0:256].rearrange("p (g c) -> p g c", c=64)),
                            deps=[t_vt[b]] + war_v)
                    bb = b - 1
                    if 0 <= bb < NB:
                        par = bb % 2
                        cb, cbname = ctx_banks[par]
                        war_c = list(bank_free[cbname])
                        t_ctx[bb] = s.op("pe", lambda e, par=par, cb=cb: [
                            e.matmul(out=cb[:, i * 65:(i + 1) * 65],
                                     lhsT=ebuf[par][:, i * 128:(i + 1) * 128],
                                     rhs=vb1[par][:, i * 65:(i + 1) * 65],
                                     start=True, stop=True) for i in range(4)][-1],
                            deps=[t_mask[bb], t_vev[bb]] + war_c, inc=4)
                        t_rc[bb] = s.op("dve", lambda e, par=par, cb=cb: e.reciprocal(
                            out=rcol[par][:], in_=cb[:, 0:260].rearrange(
                                "p (g c) -> p g c", c=65)[:, :, 64]),
                            deps=[t_ctx[bb]] + ([t_d4[bb - 2]] if bb >= 2
                                                else t_rcol_free[par]))
                        d4i = []
                        for i in range(4):
                            war_d = ([] if i else ([t_ct[bb - 2]] if bb >= 2
                                                   else t_d4_free[par]))
                            d4i.append(s.op("dve", lambda e, par=par, i=i:
                                            e.tensor_scalar(
                                out=d4[par][:, i * 64:(i + 1) * 64],
                                in0=idb[:], scalar1=rcol[par][:, i:i + 1],
                                scalar2=None, op0=ALU.mult),
                                deps=[t_rc[bb], t_idb] + war_d))
                        t_d4[bb] = d4i[-1]
                        t_d4i[bb] = d4i
                        war_g = [t_ct[bb - 2]] if bb >= 2 else t_ctxg_free[par]
                        t_cge[bb] = s.op("act", lambda e, par=par, cb=cb: e.activation(
                            out=ctxg[par][:].rearrange("p (g c) -> p g c", c=64),
                            in_=cb[:, 0:260].rearrange("p (g c) -> p g c", c=65)[:, :, 0:64],
                            func=AF.Identity, bias=0.0, scale=1.0),
                            deps=[t_ctx[bb]] + war_g)
                        bank_free[cbname] = [t_rc[bb], t_cge[bb]]
                    bc = b - 2
                    if 0 <= bc < NB:
                        par = bc % 2
                        war_t = [t_c2[0]] if t_c2[0] is not None else t_pc2_free
                        for i in range(4):
                            t_ct[bc] = s.op("pe", lambda e, par=par, i=i: [
                                e.matmul(out=pc2[(0 if pp == 0 else 64):
                                                 (64 if pp == 0 else 128),
                                                 i * 64:(i + 1) * 64],
                                         lhsT=ctxg[par][(0 if pp == 0 else 64):
                                                        (64 if pp == 0 else 128),
                                                        i * 64:(i + 1) * 64],
                                         rhs=d4[par][(0 if pp == 0 else 64):
                                                     (64 if pp == 0 else 128),
                                                     i * 64:(i + 1) * 64],
                                         start=True, stop=True,
                                         tile_position=(None if pp == 0
                                                        else (64, 64)))
                                for pp in range(2)][-1],
                                deps=[t_cge[bc], t_d4i[bc][i]]
                                     + (war_t if i == 0 else []), inc=2)
                        coff = hoff + bc * 32
                        t_c2[0] = s.op("dve", lambda e, coff=coff: e.tensor_copy(
                            out=c2sb[:].rearrange("p (j n) -> p j n", n=SLAB)[
                                :, :, coff:coff + 32].rearrange("p j (g t) -> p j g t", t=8),
                            in_=pc2[:, 0:256].rearrange("p (g j t) -> p j g t", g=4, j=8, t=8)),
                            deps=[t_ct[bc]] + t_c2_free)
                        half_c2.append(t_c2[0])
                t_pvgp_free = [t_vev[NB - 1]]
                t_pc2_free = [t_c2[0]]
                for p in range(2):
                    lastb = NB - 2 + p
                    t_ebuf_free[p] = [t_ctx[lastb]]
                    t_vb1_free[p] = [t_ctx[lastb]]
                    t_ctxg_free[p] = [t_ct[lastb]]
                    t_rcol_free[p] = [t_d4[lastb]]
                    t_d4_free[p] = [t_ct[lastb]]
                t_qkvg_free = [t_ct[NB - 1]] + dbg_qkv_toks
                dbg_qkv_toks = []
                c2_half[half] = [half_c2[-1]]
            wout_half(1, c2_half[1], with_ln=True)
            return ev2, fr3, last_qkv_mm[0], c2_half

        ev3_prev = []
        fr1_next = None
        t_h1_next = None
        for sl in range(NSLAB):
            A = rbuf[sl % 3]
            B = rbuf[(sl + 2) % 3]
            tA = t_x[sl]
            b_war = list(ev3_prev)
            if sl == 0:
                fr1 = ln_front(tA, A)
                t_h1 = ln_back(fr1, dst=c2sb, dst_war=t_c2_free, per_chunk=True)
            else:
                t_h1 = t_h1_next
            ev1, t_mm1, fr2 = projection(t_h1, t_weffb, weffb, 1024, NCH, b1, t_b1,
                                         A, tA, B, b_war, operand=c2sb,
                                         ln_next_src=B)
            t_y = ev1[-1]
            t_c2_free = [t_mm1]
            if DEBUG and sl == 0:
                t_d = dma(dbg_y, B[:], "dbg0", deps=ev1)
                ev1 = ev1 + [t_d]
            t_h2 = ln_back(fr2)
            if DEBUG and sl == 0:
                t_dh = dma(dbg_h2, hbuf[:], "dbg1", deps=[t_h2])
                t_hbuf_free_extra = [t_dh]
            else:
                t_hbuf_free_extra = []
            ev2, fr3, t_mmq, c2_half = qkv_and_attention(
                t_h2, sl, B, t_y, A, ev1, bo, A)
            t_hbuf_free = [t_mmq] + t_hbuf_free_extra
            t_z = ev2[-1]
            t_c2_free = list(ev2)
            if DEBUG and sl == 0:
                t_dc2 = dma(dbg_c2, c2sb[:], "dbg5", deps=c2_half[1])
                t_dz = dma(dbg_z, A[:], "dbg6", deps=ev2)
                t_c2_free = t_c2_free + [t_dc2]
                ev2 = ev2 + [t_dz]
            t_h3 = ln_back(fr3, per_chunk=True)
            if sl + 1 < NSLAB:
                pf_deps = [t_store[sl - 1]] if sl >= 1 else []
                t_x[sl + 1] = dma(rbuf[(sl + 1) % 3][:].rearrange(
                    "p (c n) -> p c n", n=SLAB),
                    xt_v[:, :, (sl + 1) * SLAB:(sl + 2) * SLAB],
                    "ldx%d" % ((sl + 1) % 2), deps=pf_deps)
            pe_mid = None
            if sl + 1 < NSLAB:
                nxt = rbuf[(sl + 1) % 3]
                fr1n = ln_front_start()
                for c in range(NCH):
                    ln_front_chunk(fr1n, c, t_x[sl + 1], nxt, sq=False)
                for c in range(NCH):
                    ln_front_square(fr1n, c)
                h1n_holder = []

                h1n_holder = []

                def pe_mid(phase, fr1n=fr1n, h1n_holder=h1n_holder,
                           c2w=list(t_c2_free)):
                    if phase == 0:
                        ln_front_finish(fr1n, which="s")
                    else:
                        ln_front_finish(fr1n, which="q")
                        h1n_holder.append(ln_back(fr1n, dst=c2sb, dst_war=c2w,
                                                  per_chunk=True))
            ev3, t_mm3, _ = projection(t_h3, t_weffb, weffb, 1024, NCH, b1, t_b1,
                                       A, t_z, B, ev2, pe_mid=pe_mid)
            if sl + 1 < NSLAB:
                t_h1_next = h1n_holder[0]
            t_hbuf_free = [t_mm3]
            ev3_prev = ev3
            if sl == NSLAB - 1:
                dma(yt_v[:, 0:4, sl * SLAB:(sl + 1) * SLAB],
                    B[:, 0:4 * SLAB].rearrange("p (c n) -> p c n", n=SLAB),
                    "st0", deps=ev3[0:4])
                t_store[sl] = dma(
                    yt_v[:, 4:8, sl * SLAB:(sl + 1) * SLAB],
                    B[:, 4 * SLAB:8 * SLAB].rearrange("p (c n) -> p c n", n=SLAB),
                    "st1", deps=ev3[4:8])
            else:
                t_store[sl] = dma(
                    yt_v[:, :, sl * SLAB:(sl + 1) * SLAB],
                    B[:].rearrange("p (c n) -> p c n", n=SLAB),
                    "st%d" % (sl % 2), deps=ev3)

        fin_deps = [("st0", s.counts["st0"]), ("st1", s.counts["st1"])]
        if DEBUG:
            fin_deps += [(d, s.counts[d]) for d in DMASEMS if d.startswith("dbg")
                         and s.counts[d] > 0]
        s.op("sp", lambda e: None, deps=fin_deps, inc=0)

        @block.sync
        def _(sync):
            s.replay(sync, "sp", sem_handles)

        @block.vector
        def _(vector):
            s.replay(vector, "dve", sem_handles)

        @block.tensor
        def _(tensor):
            s.replay(tensor, "pe", sem_handles)

        @block.scalar
        def _(scalar):
            s.replay(scalar, "act", sem_handles)

        @block.gpsimd
        def _(gpsimd):
            s.replay(gpsimd, "pool", sem_handles)

    return nc


def _host_inputs(L0_attn_x, L0_attn_W_qkv, L0_attn_b_qkv, L0_attn_W_out, L0_attn_b_out,
                 L0_ffn_W, L0_ffn_b, L0_ffn_lora_down, L0_ffn_lora_up,
                 L0_ffn_down_aux, L0_ffn_up_aux):
    x = np.asarray(L0_attn_x, dtype=np.float32).reshape(NCORES * TCORE, D)
    kindv, hv, dv = np.meshgrid(np.arange(3), np.arange(16), np.arange(64),
                                indexing="ij")
    perm = (192 * hv + 64 * kindv + dv).reshape(-1)
    wqkv = np.ascontiguousarray(np.asarray(L0_attn_W_qkv, dtype=np.float32)[:, perm])
    w1t = np.ascontiguousarray(np.asarray(L0_ffn_W, dtype=np.float32).T)
    wout = np.ascontiguousarray(np.asarray(L0_attn_W_out, dtype=np.float32))
    upat = np.ascontiguousarray(np.asarray(L0_ffn_up_aux, dtype=np.float32).T)
    lup = np.ascontiguousarray(np.asarray(L0_ffn_lora_up, dtype=np.float32))
    ldn = np.ascontiguousarray(np.asarray(L0_ffn_lora_down, dtype=np.float32))
    dnaux = np.ascontiguousarray(np.asarray(L0_ffn_down_aux, dtype=np.float32))
    bqd = np.ascontiguousarray(
        np.asarray(L0_attn_b_qkv, dtype=np.float32)[perm].reshape(24, 128).T)
    b1d = np.ascontiguousarray(
        np.asarray(L0_ffn_b, dtype=np.float32).reshape(8, 128).T)
    bod = np.ascontiguousarray(
        np.asarray(L0_attn_b_out, dtype=np.float32).reshape(8, 128).T)
    mask = np.zeros((128, 128), dtype=np.float32)
    for g in range(16):
        for t in range(8):
            for p in range(2):
                for j in range(8):
                    mask[g * 8 + t, p * 64 + j * 8 + t] = 1.0
    ident = np.zeros((128, 64), dtype=np.float32)
    ident[0:64] = np.eye(64, dtype=np.float32)
    ident[64:128] = np.eye(64, dtype=np.float32)
    shared = dict(wqkv=wqkv, w1t=w1t, wout=wout, upat=upat, lup=lup, ldn=ldn,
                  dnaux=dnaux, bqd=bqd, b1d=b1d, bod=bod, maskd=mask, identd=ident)
    in_maps = []
    for c in range(NCORES):
        shard = x[c * TCORE:(c + 1) * TCORE]
        xt = np.ascontiguousarray(shard.T)
        m = dict(shared)
        m["xt"] = xt
        in_maps.append(m)
    return in_maps


def kernel(L0_attn_x, L0_attn_W_qkv, L0_attn_b_qkv, L0_attn_W_out, L0_attn_b_out,
           L0_ffn_W, L0_ffn_b, L0_ffn_lora_down, L0_ffn_lora_up,
           L0_ffn_down_aux, L0_ffn_up_aux):
    in_maps = _host_inputs(L0_attn_x, L0_attn_W_qkv, L0_attn_b_qkv, L0_attn_W_out,
                           L0_attn_b_out, L0_ffn_W, L0_ffn_b, L0_ffn_lora_down,
                           L0_ffn_lora_up, L0_ffn_down_aux, L0_ffn_up_aux)
    nc = build_nc()
    res = run_bass_kernel_spmd(nc, in_maps, core_ids=list(range(NCORES)))
    outs = []
    for c in range(NCORES):
        yt = res.results[c]["yt"]
        outs.append(np.ascontiguousarray(yt.T))
    full = np.concatenate(outs, axis=0)
    return full.reshape(np.asarray(L0_attn_x).shape).astype(np.float32)


# version 45
# speedup vs baseline: 1.0663x; 1.0016x over previous
import numpy as np
from contextlib import ExitStack

import concourse.bass as bass
import concourse.mybir as mybir
from concourse.bass_utils import run_bass_kernel_spmd

F32 = mybir.dt.float32
BF16 = mybir.dt.bfloat16
AF = mybir.ActivationFunctionType
ALU = mybir.AluOpType

NCORES = 8
TCORE = 2048
SLAB = 512
NSLAB = TCORE // SLAB
HALF = 256
NGRP_H = HALF // 8
NBATCH_H = NGRP_H // 4
D = 1024
NCH = 8
EPS = 1e-5

DEBUG = 0

ENGS = ("pe", "dve", "act", "pool", "sp")
DMASEMS = ("ldc", "ldw1", "ldq0", "ldq1", "ldwo", "ldx0", "ldx1", "st0", "st1",
           "dbg0", "dbg1", "dbg2", "dbg3", "dbg4", "dbg5", "dbg6", "dbg7", "dbg8")


class Sched:

    def __init__(self):
        self.q = {e: [] for e in ENGS}
        self.counts = {}
        for e in ENGS:
            self.counts[e] = 0
        for e in DMASEMS:
            self.counts[e] = 0

    def op(self, eng, fn, deps=(), sem=None, inc=1):
        sem = sem or eng
        dl = []
        for d in deps:
            if d is None:
                continue
            if isinstance(d, list):
                dl.extend(x for x in d if x is not None)
            else:
                dl.append(d)
        self.counts[sem] += inc
        tok = (sem, self.counts[sem])
        self.q[eng].append((dl, fn, sem, inc))
        return tok

    def replay(self, engobj, eng, sems):
        marks = {}
        for deps, fn, sem, inc in self.q[eng]:
            for (ps, pc) in deps:
                if marks.get(ps, 0) < pc:
                    engobj.wait_ge(sems[ps], pc)
                    marks[ps] = pc
            inst = fn(engobj)
            if inst is not None and inc:
                inst.then_inc(sems[sem], inc)


def build_nc():
    nc = bass.Bass("TRN2", target_bir_lowering=False)

    xt = nc.dram_tensor("xt", [D, TCORE], F32, kind="ExternalInput").ap()
    wqkv = nc.dram_tensor("wqkv", [D, 3 * D], F32, kind="ExternalInput").ap()
    w1t = nc.dram_tensor("w1t", [D, D], F32, kind="ExternalInput").ap()
    wout = nc.dram_tensor("wout", [D, D], F32, kind="ExternalInput").ap()
    upat = nc.dram_tensor("upat", [50, D], F32, kind="ExternalInput").ap()
    lup = nc.dram_tensor("lup", [50, 4], F32, kind="ExternalInput").ap()
    ldn = nc.dram_tensor("ldn", [4, 100], F32, kind="ExternalInput").ap()
    dnaux = nc.dram_tensor("dnaux", [100, D], F32, kind="ExternalInput").ap()
    bqd = nc.dram_tensor("bqd", [128, 24], F32, kind="ExternalInput").ap()
    b1d = nc.dram_tensor("b1d", [128, 8], F32, kind="ExternalInput").ap()
    bod = nc.dram_tensor("bod", [128, 8], F32, kind="ExternalInput").ap()
    maskd = nc.dram_tensor("maskd", [128, 128], F32, kind="ExternalInput").ap()
    identd = nc.dram_tensor("identd", [128, 64], F32, kind="ExternalInput").ap()
    yt = nc.dram_tensor("yt", [D, TCORE], F32, kind="ExternalOutput").ap()
    if DEBUG:
        dbg_y = nc.dram_tensor("dbg_y", [128, NCH * SLAB], F32, kind="ExternalOutput").ap()
        dbg_h2 = nc.dram_tensor("dbg_h2", [128, NCH * SLAB], BF16, kind="ExternalOutput").ap()
        dbg_qg = nc.dram_tensor("dbg_qg", [64, NGRP_H * 128], BF16, kind="ExternalOutput").ap()
        dbg_kg = nc.dram_tensor("dbg_kg", [64, NGRP_H * 128], BF16, kind="ExternalOutput").ap()
        dbg_vg = nc.dram_tensor("dbg_vg", [64, NGRP_H * 128], BF16, kind="ExternalOutput").ap()
        dbg_c2 = nc.dram_tensor("dbg_c2", [128, 8 * SLAB], BF16, kind="ExternalOutput").ap()
        dbg_z = nc.dram_tensor("dbg_z", [128, NCH * SLAB], F32, kind="ExternalOutput").ap()
        dbg_xh = nc.dram_tensor("dbg_xh", [128, NCH * SLAB], BF16, kind="ExternalOutput").ap()
        dbg_rows = nc.dram_tensor("dbg_rows", [128, 2048], F32, kind="ExternalOutput").ap()

    s = Sched()
    with ExitStack() as ctx:
        al = lambda *a, **k: ctx.enter_context(nc.sbuf_tensor(*a, **k))
        alp = lambda *a, **k: ctx.enter_context(nc.psum_tensor(*a, **k))
        stg = al("stg", [128, 3072], F32)
        rowb = al("rowb", [128, 1024], BF16)
        wqkvb = al("wqkvb", [128, 8 * 3072], BF16)
        weffb = al("weffb", [128, 8 * 1024], BF16)
        woutb = al("woutb", [128, 8 * 1024], BF16)
        rbuf = [al(f"rbuf{i}", [128, NCH * SLAB], F32) for i in range(3)]
        xhat = al("xhat", [128, NCH * SLAB], BF16)
        x2b = al("x2b", [128, NCH * SLAB], BF16)
        hbuf = al("hbuf", [128, NCH * SLAB], BF16)
        qg = al("qg", [64, NGRP_H * 128], BF16)
        kg = al("kg", [64, NGRP_H * 128], BF16)
        vg = al("vg", [64, NGRP_H * 128], BF16)
        ebuf = [al(f"ebuf{i}", [128, 512], BF16) for i in range(2)]
        vb1 = [al(f"vb1_{i}", [128, 4 * 65], BF16) for i in range(2)]
        ctxg = [al(f"ctxg{i}", [128, 256], BF16) for i in range(2)]
        c2sb = al("c2sb", [128, 8 * SLAB], BF16)
        bq = al("bq", [128, 24], F32)
        b1 = al("b1", [128, 8], F32)
        bo = al("bo", [128, 8], F32)
        maskf = al("maskf", [128, 128], F32)
        maskb = al("maskb", [128, 128], BF16)
        identf = al("identf", [128, 64], F32)
        idb = al("idb", [128, 64], BF16)
        ones128 = al("ones128", [128, 128], BF16)
        epsc = al("epsc", [128, 1], F32)
        ksc = al("ksc", [128, 1], F32)
        nhc = al("nhc", [128, 1], F32)
        ivdc = al("ivdc", [128, 1], F32)
        rcol = [al(f"rcol{i}", [128, 4], F32) for i in range(2)]
        d4 = [al(f"d4_{i}", [128, 256], BF16) for i in range(2)]
        lora_up = al("lora_up", [50, 4], F32)
        lora_dn = al("lora_dn", [4, 100], F32)
        pbank = [alp(f"pbank{i}", [128, 512], F32) for i in range(2)]
        pstat = [alp(f"pstat{i}", [128, 512], F32) for i in range(2)]
        pscore = alp("pscore", [128, 512], F32)
        pctx = alp("pctx", [128, 512], F32)
        pvgp = alp("pvgp", [128, 1024], BF16)
        pc2 = alp("pc2", [128, 512], F32)

        sem_handles = {}
        for name in ENGS + DMASEMS:
            sem_handles[name] = ctx.enter_context(nc.semaphore(name))
        block = ctx.enter_context(nc.Block())

        c2f32 = c2sb[:].bitcast(F32)
        dna = c2f32[0:100, 0:1024]
        upa = c2f32[0:64, 1024:2048]
        w1t_sl2 = xhat[:].bitcast(F32)[:, 1024:2048]
        rows_m = stg[:, 0:512]
        rows_rstd = stg[:, 512:1024]
        rows_t = stg[:, 1024:1536]
        rows_v = stg[:, 1536:2048]

        def dma(dst, src, sem, deps=()):
            return s.op("sp", lambda e, dst=dst, src=src: e.dma_start(out=dst, in_=src),
                        deps=deps, sem=sem, inc=16)

        t_ones = s.op("dve", lambda e: e.memset(ones128[:], 1.0))
        t_eps = s.op("dve", lambda e: e.memset(epsc[:], EPS))
        t_ksc = s.op("dve", lambda e: e.memset(ksc[:], 0.125))
        t_nh = s.op("dve", lambda e: e.memset(nhc[:], -0.5))
        t_ivd = s.op("dve", lambda e: e.memset(ivdc[:], 1.0 / 1024.0))
        t_v1 = [s.op("dve", lambda e, i=i: e.memset(
            vb1[i][:].rearrange("p (g c) -> p g c", c=65)[:, :, 64:65], 1.0)) for i in range(2)]
        d_lora = [dma(dst, srcd, fam) for dst, srcd, fam in
                  ((lora_up[:], lup, "dbg0"), (lora_dn[:], ldn, "dbg1"),
                   (dna[0:100, :], dnaux, "dbg2"), (upa[0:50, :], upat, "dbg3"))]
        for dst, srcd in ((bq[:], bqd), (b1[:], b1d), (bo[:], bod),
                          (maskf[:], maskd), (identf[:], identd)):
            t_ldc = dma(dst, srcd, "ldc")
        d_b = [t_ldc] * 3
        t_maskb = s.op("dve", lambda e: e.tensor_copy(out=maskb[:], in_=maskf[:]),
                       deps=[t_ldc])
        t_idb = s.op("dve", lambda e: e.tensor_copy(out=idb[:], in_=identf[:]),
                     deps=[t_ldc])
        t_bq = s.op("dve", lambda e: e.tensor_scalar(
            out=bq[:, 8:16], in0=bq[:, 8:16], scalar1=0.125, scalar2=None,
            op0=ALU.mult), deps=[t_ldc])
        t_b1 = s.op("dve", lambda e: e.tensor_scalar(
            out=b1[:], in0=b1[:], scalar1=0.5, scalar2=None, op0=ALU.mult),
            deps=[t_ldc])

        xt_v = xt.rearrange("(c p) n -> p c n", p=128)
        yt_v = yt.rearrange("(c p) n -> p c n", p=128)
        t_x = [None] * NSLAB
        t_x[0] = dma(rbuf[0][:].rearrange("p (c n) -> p c n", n=SLAB),
                     xt_v[:, :, 0:SLAB], "ldx0")

        A_sl = stg[0:4, 0:1024]
        B_sl = stg[0:100, 1024:2048]
        w1t_sl = stg[:, 2048:3072]

        t_A_mm = s.op("pe", lambda e: (
            e.matmul(out=pbank[0][0:4, :], lhsT=lora_up[:], rhs=upa[0:50, 0:512],
                     start=True, stop=True),
            e.matmul(out=pbank[1][0:4, :], lhsT=lora_up[:], rhs=upa[0:50, 512:1024],
                     start=True, stop=True))[-1], deps=d_lora[:2] + [d_lora[3]])
        t_A = s.op("dve", lambda e: (
            e.tensor_copy(out=A_sl[:, 0:512], in_=pbank[0][0:4, :]),
            e.tensor_copy(out=A_sl[:, 512:1024], in_=pbank[1][0:4, :]))[-1],
            deps=[t_A_mm])
        t_B_mm = s.op("pe", lambda e: (
            e.matmul(out=pbank[0][0:100, :], lhsT=lora_dn[:], rhs=A_sl[:, 0:512],
                     start=True, stop=True),
            e.matmul(out=pbank[1][0:100, :], lhsT=lora_dn[:], rhs=A_sl[:, 512:1024],
                     start=True, stop=True))[-1], deps=[t_A, d_lora[1]])
        t_B = s.op("dve", lambda e: (
            e.tensor_copy(out=B_sl[:, 0:512], in_=pbank[0][0:100, :]),
            e.tensor_copy(out=B_sl[:, 512:1024], in_=pbank[1][0:100, :]))[-1],
            deps=[t_B_mm])
        t_weff_parts = []
        for kc in range(NCH):
            slw = w1t_sl if kc % 2 == 0 else w1t_sl2
            war = [t_weff_parts[kc - 2]] if kc >= 2 else []
            d_w1 = dma(slw, w1t.rearrange("(c p) m -> c p m", p=128)[kc],
                       "ldw1" if kc % 2 == 0 else "ldq1", deps=war)
            t_w1h = s.op("dve", lambda e, slw=slw: e.tensor_scalar(
                out=slw, in0=slw, scalar1=0.5, scalar2=None, op0=ALU.mult),
                deps=[d_w1])
            cbk0 = [pbank[0], pbank[1], pscore, pctx][(2 * kc) % 4]
            cbk1 = [pbank[0], pbank[1], pscore, pctx][(2 * kc + 1) % 4]
            t_c = s.op("pe", lambda e, kc=kc, cbk0=cbk0, cbk1=cbk1: (
                e.matmul(out=cbk0[:], lhsT=dna[0:100, kc * 128:(kc + 1) * 128],
                         rhs=B_sl[:, 0:512], start=True, stop=True),
                e.matmul(out=cbk1[:], lhsT=dna[0:100, kc * 128:(kc + 1) * 128],
                         rhs=B_sl[:, 512:1024], start=True, stop=True))[-1],
                deps=[t_B, d_lora[2]] + ([t_weff_parts[kc - 2]] if kc >= 2 else []))
            t_we = s.op("dve", lambda e, kc=kc, slw=slw, cbk0=cbk0, cbk1=cbk1: (
                e.scalar_tensor_tensor(
                    out=weffb[:, kc * 1024:kc * 1024 + 512], in0=cbk0[:],
                    scalar=0.5, in1=slw[:, 0:512], op0=ALU.mult, op1=ALU.add),
                e.scalar_tensor_tensor(
                    out=weffb[:, kc * 1024 + 512:(kc + 1) * 1024], in0=cbk1[:],
                    scalar=0.5, in1=slw[:, 512:1024], op0=ALU.mult, op1=ALU.add))[-1],
                deps=[t_c, t_w1h])
            t_weff_parts.append(t_we)
        t_weffb = t_weff_parts[-1]

        wqkv_v = wqkv.rearrange("(c p) m -> c p m", p=128)
        wout_v = wout.rearrange("(c p) m -> c p m", p=128)
        for kc in range(NCH):
            t_wqkvb = s.op("pool", lambda e, kc=kc: e.dma_start(
                out=wqkvb[:, kc * 3072:(kc + 1) * 3072], in_=wqkv_v[kc]),
                deps=[t_ldc] if kc == 0 else [], sem="ldq0", inc=16)
        for kc in range(NCH):
            t_woutb = s.op("pool", lambda e, kc=kc: e.dma_start(
                out=woutb[:, kc * 1024:(kc + 1) * 1024], in_=wout_v[kc]),
                deps=[], sem="ldwo", inc=16)

        bank_free = {"pscore": [t_woutb], "pstat0": [t_B_mm], "pstat1": [t_B_mm],
                     "pctx": [t_woutb], "pbank0": [t_woutb], "pbank1": [t_woutb]}
        proj_banks = [("pbank0", pbank[0]), ("pbank1", pbank[1]),
                      ("pscore", pscore), ("pctx", pctx)]
        t_xhat_free = [t_ones]
        t_hbuf_free = [t_ones]
        t_rowb_free = [t_ones]
        t_rows_free = [t_woutb]
        t_qkvg_free = [t_ones]
        t_ebuf_free = [[t_ones], [t_ones]]
        t_vb1_free = [[t_v1[0]], [t_v1[1]]]
        t_ctxg_free = [[t_ones], [t_ones]]
        t_rcol_free = [[t_ones], [t_ones]]
        t_d4_free = [[t_ones], [t_ones]]
        t_pvgp_free = [t_ones]
        t_pc2_free = [t_ones]
        t_c2_free = [t_weffb]
        t_store = [None] * NSLAB

        ln_calls = [0]
        t_x2b_free = [t_ones]

        def ln_front_start():
            return dict(cast=[], x2=[], mm_s=None, mm_q=None)

        def ln_front_square(st, c):
            nonlocal t_x2b_free
            t_c = st["cast"][c]
            t_q = s.op("dve", lambda e, c=c: e.tensor_tensor(
                out=x2b[:, c * SLAB:(c + 1) * SLAB],
                in0=xhat[:, c * SLAB:(c + 1) * SLAB],
                in1=xhat[:, c * SLAB:(c + 1) * SLAB], op=ALU.mult),
                deps=([t_c] if not isinstance(t_c, list) else t_c)
                     + (list(t_x2b_free) if c == 0 else []))
            st["x2"].append(t_q)

        def ln_front_chunk(st, c, tok, src, pre_cast=None, sq=True):
            nonlocal t_xhat_free
            if pre_cast is not None:
                t_c = pre_cast
            else:
                extra = list(t_xhat_free) if c == 0 else []
                t_c = s.op("dve", lambda e, c=c: e.tensor_copy(
                    out=xhat[:, c * SLAB:(c + 1) * SLAB],
                    in_=src[:, c * SLAB:(c + 1) * SLAB]),
                    deps=[tok] + extra)
            st["cast"].append(t_c)
            if sq:
                ln_front_square(st, c)

        def ln_front_finish(st, which="sq"):
            for c in range(NCH):
                cst = st["cast"][c]
                dep_s = (cst if isinstance(cst, list) else [cst]) + [t_ones]
                if c == 0:
                    dep_s += bank_free["pstat0"]
                if "s" in which:
                    st["mm_s"] = s.op("pe", lambda e, c=c: e.matmul(
                        out=pstat[0][:], lhsT=ones128[:],
                        rhs=xhat[:, c * SLAB:(c + 1) * SLAB],
                        start=(c == 0), stop=(c == NCH - 1), skip_group_check=True),
                        deps=dep_s, inc=(1 if c == NCH - 1 else 0))
            for c in range(NCH):
                if "q" not in which:
                    break
                dep_q = [st["x2"][c]]
                if c == 0:
                    dep_q += bank_free["pstat1"]
                if True:
                    st["mm_q"] = s.op("pe", lambda e, c=c: e.matmul(
                        out=pstat[1][:], lhsT=ones128[:],
                        rhs=x2b[:, c * SLAB:(c + 1) * SLAB],
                        start=(c == 0), stop=(c == NCH - 1), skip_group_check=True),
                        deps=dep_q, inc=(1 if c == NCH - 1 else 0))
            return st

        def ln_front(chunk_toks, src):
            if not isinstance(chunk_toks, list):
                chunk_toks = [chunk_toks] * NCH
            st = ln_front_start()
            for c in range(NCH):
                ln_front_chunk(st, c, chunk_toks[c], src)
            ln_front_finish(st)
            return st

        def ln_back(fr, dst=None, dst_war=None, per_chunk=False):
            nonlocal t_xhat_free, t_rowb_free, t_rows_free, t_x2b_free
            ln_calls[0] += 1
            t_mm_s, t_mm_q = fr["mm_s"], fr["mm_q"]
            t_cast = fr["cast"][-1]
            if isinstance(t_cast, list):
                t_cast = t_cast[-1]
            t_rm = s.op("dve", lambda e: e.tensor_scalar(
                out=rowb[:, 0:512], in0=pstat[0][:], scalar1=1.0 / 1024.0,
                scalar2=None, op0=ALU.mult), deps=[t_mm_s] + t_rowb_free)
            t_msq = s.op("dve", lambda e: e.tensor_tensor(
                out=rows_t, in0=rowb[:, 0:512], in1=rowb[:, 0:512],
                op=ALU.mult), deps=[t_rm] + t_rows_free)
            t_var = s.op("dve", lambda e: e.scalar_tensor_tensor(
                out=rows_v, in0=pstat[1][:], scalar=ivdc[:, 0:1], in1=rows_t,
                op0=ALU.mult, op1=ALU.subtract),
                deps=[t_mm_q, t_msq] + t_rows_free)
            t_ln = s.op("act", lambda e: e.activation(
                out=rows_m, in_=rows_v, func=AF.Ln, bias=epsc[:, 0:1], scale=1.0),
                deps=[t_var, t_eps] + t_rows_free)
            t_rstd = s.op("act", lambda e: e.activation(
                out=rowb[:, 512:1024], in_=rows_m, func=AF.Exp, bias=0.0,
                scale=nhc[:, 0:1]), deps=[t_ln, t_nh] + t_rowb_free)
            xv = xhat[:].rearrange("p (c n) -> p c n", n=SLAB)
            dstb = hbuf if dst is None else dst
            war = list(t_hbuf_free) if dst_war is None else list(dst_war)
            hv = dstb[:].rearrange("p (c n) -> p c n", n=SLAB)
            t_apps = []
            t_subs = []
            if per_chunk is True:
                for c in range(NCH):
                    cst = fr["cast"][c]
                    cd = cst if isinstance(cst, list) else [cst]
                    t_sub = s.op("dve", lambda e, c=c: e.tensor_tensor(
                        out=dstb[:, c * SLAB:(c + 1) * SLAB],
                        in0=xhat[:, c * SLAB:(c + 1) * SLAB],
                        in1=rowb[:, 0:512], op=ALU.subtract),
                        deps=[t_rm] + cd + (war if c == 0 else []))
                    t_subs.append(t_sub)
                for c in range(NCH):
                    t_app = s.op("dve", lambda e, c=c: e.tensor_tensor(
                        out=dstb[:, c * SLAB:(c + 1) * SLAB],
                        in0=dstb[:, c * SLAB:(c + 1) * SLAB],
                        in1=rowb[:, 512:1024], op=ALU.mult),
                        deps=[t_subs[c], t_rstd])
                    t_apps.append(t_app)
            elif per_chunk == "ch":
                grid = [[None] * NCH, [None] * NCH]
                for hh in range(2):
                    ho = hh * HALF
                    for c in range(NCH):
                        cst = fr["cast"][c]
                        cd = cst if isinstance(cst, list) else [cst]
                        t_sub = s.op("dve", lambda e, c=c, ho=ho: e.tensor_tensor(
                            out=dstb[:, c * SLAB + ho:c * SLAB + ho + HALF],
                            in0=xhat[:, c * SLAB + ho:c * SLAB + ho + HALF],
                            in1=rowb[:, ho:ho + HALF], op=ALU.subtract),
                            deps=[t_rm] + cd
                                 + (war if (hh == 0 and c == 0) else []))
                        t_subs.append(t_sub)
                        grid[hh][c] = t_sub
                for hh in range(2):
                    ho = hh * HALF
                    for c in range(NCH):
                        t_app = s.op("dve", lambda e, c=c, ho=ho: e.tensor_tensor(
                            out=dstb[:, c * SLAB + ho:c * SLAB + ho + HALF],
                            in0=dstb[:, c * SLAB + ho:c * SLAB + ho + HALF],
                            in1=rowb[:, 512 + ho:512 + ho + HALF], op=ALU.mult),
                            deps=[grid[hh][c], t_rstd])
                        t_apps.append(t_app)
                        grid[hh][c] = t_app
                bank_free["pstat0"] = [t_rm, t_msq]
                bank_free["pstat1"] = [t_var]
                t_xhat_free = [t_subs[-1]]
                t_x2b_free = [t_mm_q]
                t_rowb_free = [t_subs[-1], t_apps[-1]]
                t_rows_free = [t_var, t_ln, t_rstd]
                return grid
            else:
                for hh in range(2):
                    ho = hh * HALF
                    mb = rowb[:, ho:ho + HALF].unsqueeze(1).broadcast_to(
                        [128, NCH, HALF])
                    t_sub = s.op("dve", lambda e, ho=ho, mb=mb: e.tensor_tensor(
                        out=hv[:, :, ho:ho + HALF], in0=xv[:, :, ho:ho + HALF],
                        in1=mb, op=ALU.subtract),
                        deps=[t_rm, t_cast] + (war if hh == 0 else []))
                    t_subs.append(t_sub)
                for hh in range(2):
                    ho = hh * HALF
                    rb = rowb[:, 512 + ho:512 + ho + HALF].unsqueeze(1).broadcast_to(
                        [128, NCH, HALF])
                    t_app = s.op("dve", lambda e, ho=ho, rb=rb: e.tensor_tensor(
                        out=hv[:, :, ho:ho + HALF], in0=hv[:, :, ho:ho + HALF],
                        in1=rb, op=ALU.mult), deps=[t_subs[hh], t_rstd])
                    t_apps.append(t_app)
            bank_free["pstat0"] = [t_rm, t_msq]
            bank_free["pstat1"] = [t_var]
            t_xhat_free = [t_subs[-1]]
            t_x2b_free = [t_mm_q]
            t_rowb_free = [t_subs[-1], t_apps[-1]]
            t_rows_free = [t_var, t_ln, t_rstd]
            return t_apps

        def projection(t_h, t_w, wbuf, wbase_stride, nout, bias, t_bias, resid_in,
                       t_resid_in, resid_out, t_resid_out_war, operand=None,
                       ln_next_src=None, ln_next_tok=None, pe_mid=None):
            toks = []
            t_mm = None
            fr = ln_front_start() if ln_next_src is not None else None
            nb = len(proj_banks)
            opnd = hbuf if operand is None else operand
            per_kc = isinstance(t_h, list) and len(t_h) == NCH
            for oc in range(nout):
                bname, bk = proj_banks[oc % nb]
                if per_kc:
                    for kc in range(NCH):
                        t_mm = s.op("pe", lambda e, oc=oc, bk=bk, kc=kc: e.matmul(
                            out=bk[:],
                            lhsT=wbuf[:, kc * wbase_stride + oc * 128:
                                      kc * wbase_stride + oc * 128 + 128],
                            rhs=opnd[:, kc * SLAB:(kc + 1) * SLAB],
                            start=(kc == 0), stop=(kc == NCH - 1),
                            skip_group_check=True),
                            deps=[t_h[kc], t_w]
                                 + (bank_free[bname] if kc == 0 else []),
                            inc=(1 if kc == NCH - 1 else 0))
                else:
                    t_mm = s.op("pe", lambda e, oc=oc, bk=bk: [
                        e.matmul(out=bk[:],
                                 lhsT=wbuf[:, kc * wbase_stride + oc * 128:
                                           kc * wbase_stride + oc * 128 + 128],
                                 rhs=opnd[:, kc * SLAB:(kc + 1) * SLAB],
                                 start=(kc == 0), stop=(kc == NCH - 1))
                        for kc in range(NCH)][-1],
                        deps=[t_h, t_w] + bank_free[bname])
                t_pc = None
                if fr is not None and ln_next_tok is None:
                    xw = list(t_xhat_free) if oc == 0 else []
                    t_pc = s.op("dve", lambda e, oc=oc, bk=bk: e.scalar_tensor_tensor(
                        out=xhat[:, oc * SLAB:(oc + 1) * SLAB],
                        in0=bk[:], scalar=bias[:, oc:oc + 1],
                        in1=resid_in[:, oc * SLAB:(oc + 1) * SLAB],
                        op0=ALU.add, op1=ALU.add),
                        deps=[t_mm, t_resid_in, t_bias] + xw)
                t_ev = s.op("dve", lambda e, oc=oc, bk=bk: e.scalar_tensor_tensor(
                    out=resid_out[:, oc * SLAB:(oc + 1) * SLAB],
                    in0=bk[:], scalar=bias[:, oc:oc + 1],
                    in1=resid_in[:, oc * SLAB:(oc + 1) * SLAB],
                    op0=ALU.add, op1=ALU.add),
                    deps=[t_mm, t_resid_in, t_bias] + t_resid_out_war)
                bank_free[bname] = [t_ev]
                toks.append(t_ev)
                if fr is not None:
                    ln_front_chunk(fr, oc,
                                   (ln_next_tok if ln_next_tok is not None else t_ev),
                                   ln_next_src, pre_cast=t_pc)
                if pe_mid is not None and oc == 1:
                    pe_mid(0)
                if pe_mid is not None and oc == 3:
                    pe_mid(1)
            if fr is not None:
                ln_front_finish(fr)
            return toks, t_mm, fr

        def qkv_and_attention(t_h, slab, resid_y, t_y, resid_z, z_war, bo_, t_ln3_src):
            nonlocal t_qkvg_free, t_pvgp_free, t_pc2_free
            nonlocal t_ebuf_free, t_vb1_free, t_ctxg_free, t_rcol_free, t_d4_free
            last_qkv_mm = [None]
            dbg_qkv_toks = []
            t_c2 = [None]
            c2_half = {}
            ev2 = [None] * NCH
            ev2_h1 = [None] * NCH
            pc3 = [[None] * NCH, [None] * NCH]
            fr3 = ln_front_start()
            nb = len(proj_banks)

            def wout_half(half, c2toks, with_ln):
                hoff = half * HALF
                for oc in range(NCH):
                    bname, bk = proj_banks[oc % nb]
                    t_mm = s.op("pe", lambda e, oc=oc, bk=bk, hoff=hoff: [
                        e.matmul(out=bk[:, 0:HALF],
                                 lhsT=woutb[:, j * 1024 + oc * 128:j * 1024 + oc * 128 + 128],
                                 rhs=c2sb[:, j * SLAB + hoff:j * SLAB + hoff + HALF],
                                 start=(j == 0), stop=(j == NCH - 1))
                        for j in range(NCH)][-1],
                        deps=c2toks + [t_woutb] + bank_free[bname])
                    if with_ln or half == 0:
                        xw = list(t_xhat_free) if (half == 0 and oc == 0) else []
                        t_pc = s.op("dve", lambda e, oc=oc, bk=bk, hoff=hoff:
                                    e.scalar_tensor_tensor(
                            out=xhat[:, oc * SLAB + hoff:oc * SLAB + hoff + HALF],
                            in0=bk[:, 0:HALF], scalar=bo_[:, oc:oc + 1],
                            in1=resid_y[:, oc * SLAB + hoff:oc * SLAB + hoff + HALF],
                            op0=ALU.add, op1=ALU.add),
                            deps=[t_mm, t_y, t_ldc] + xw)
                        pc3[half][oc] = t_pc
                    t_ev = s.op("dve", lambda e, oc=oc, bk=bk, hoff=hoff:
                                e.scalar_tensor_tensor(
                        out=resid_z[:, oc * SLAB + hoff:oc * SLAB + hoff + HALF],
                        in0=bk[:, 0:HALF], scalar=bo_[:, oc:oc + 1],
                        in1=resid_y[:, oc * SLAB + hoff:oc * SLAB + hoff + HALF],
                        op0=ALU.add, op1=ALU.add),
                        deps=[t_mm, t_y, t_ldc] + z_war)
                    bank_free[bname] = [t_ev]
                    if half == 0:
                        ev2_h1[oc] = t_ev
                    else:
                        ev2[oc] = t_ev
                        if with_ln:
                            ln_front_chunk(fr3, oc, None, t_ln3_src,
                                           pre_cast=[pc3[0][oc], pc3[1][oc]])
                if half == 1 and with_ln:
                    ln_front_finish(fr3)

            qkv_banks = proj_banks + [("pstat0", pstat[0]), ("pstat1", pstat[1])]
            for half in range(2):
                hoff = half * HALF
                ev_toks_q, ev_toks_kv = [], []
                for oc in range(24):
                    bname, bk = qkv_banks[oc % 6]
                    for kc in range(NCH):
                        t_mm = s.op("pe", lambda e, oc=oc, bk=bk, hoff=hoff,
                                    kc=kc: e.matmul(
                            out=bk[:, 0:HALF],
                            lhsT=wqkvb[:, kc * 3072 + oc * 128:
                                       kc * 3072 + oc * 128 + 128],
                            rhs=hbuf[:, kc * SLAB + hoff:kc * SLAB + hoff + HALF],
                            start=(kc == 0), stop=(kc == NCH - 1),
                            skip_group_check=True),
                            deps=[t_h[half][kc], t_wqkvb]
                                 + (bank_free[bname] if kc == 0 else []),
                            inc=(1 if kc == NCH - 1 else 0))
                    last_qkv_mm[0] = t_mm
                    kind = oc // 8
                    c = oc % 8
                    if kind == 0:
                        dst = qg
                        offs = (8 * c, 64 + 8 * c)
                    elif kind == 1:
                        dst = kg
                        offs = (16 * c, 16 * c + 8)
                    else:
                        dst = vg
                        offs = (16 * c, 16 * c + 8)
                    dstv = dst[:].rearrange("d (g c) -> d g c", c=128)
                    if kind == 0:
                        t_ev = s.op("act", lambda e, bk=bk, oc=oc, dstv=dstv, offs=offs: (
                            e.activation(out=dstv[:, :, offs[0]:offs[0] + 8],
                                         in_=bk[0:64, 0:HALF].rearrange(
                                             "d (g t) -> d g t", t=8),
                                         func=AF.Identity, bias=bq[0:64, oc:oc + 1],
                                         scale=1.0),
                            e.activation(out=dstv[:, :, offs[1]:offs[1] + 8],
                                         in_=bk[64:128, 0:HALF].rearrange(
                                             "d (g t) -> d g t", t=8),
                                         func=AF.Identity, bias=bq[64:128, oc:oc + 1],
                                         scale=1.0))[-1],
                            deps=[t_mm, t_bq] + t_qkvg_free, inc=2)
                        ev_toks_q.append(t_ev)
                    else:
                        sc = ksc[:, 0:1] if kind == 1 else 1.0
                        t_ev = s.op("act", lambda e, bk=bk, oc=oc, dstv=dstv,
                                    offs=offs, sc=sc: (
                            e.activation(out=dstv[:, :, offs[0]:offs[0] + 8],
                                         in_=bk[0:64, 0:HALF].rearrange(
                                             "d (g t) -> d g t", t=8),
                                         func=AF.Identity, bias=bq[0:64, oc:oc + 1],
                                         scale=(sc if isinstance(sc, float)
                                                else ksc[0:64, 0:1])),
                            e.activation(out=dstv[:, :, offs[1]:offs[1] + 8],
                                         in_=bk[64:128, 0:HALF].rearrange(
                                             "d (g t) -> d g t", t=8),
                                         func=AF.Identity, bias=bq[64:128, oc:oc + 1],
                                         scale=(sc if isinstance(sc, float)
                                                else ksc[64:128, 0:1])))[-1],
                            deps=[t_mm, t_bq, t_ksc] + t_qkvg_free, inc=2)
                        ev_toks_kv.append(t_ev)
                    bank_free[bname] = [t_ev]
                t_q_ready = ev_toks_q[-1]
                t_kv_ready = ev_toks_kv[-1]
                if DEBUG and slab == 0 and half == 0:
                    t_dq = dma(dbg_qg, qg[:], "dbg2", deps=[t_q_ready])
                    t_dk = dma(dbg_kg, kg[:], "dbg3", deps=[t_kv_ready])
                    t_dv = dma(dbg_vg, vg[:], "dbg4", deps=[t_kv_ready])
                    dbg_qkv_toks.extend([t_dq, t_dk, t_dv])

                if half == 1:
                    wout_half(0, c2_half[0], with_ln=False)

                sc_banks = [(pscore, "pscore"), (pstat[0], "pstat0")]
                ctx_banks = [(pctx, "pctx"), (pstat[1], "pstat1")]
                t_sc = {}
                t_exp = {}
                t_mask = {}
                t_vt = {}
                t_vev = {}
                t_ctx = {}
                t_rc = {}
                t_d4 = {}
                t_d4i = {}
                t_cge = {}
                t_ct = {}
                half_c2 = []
                NB = NBATCH_H
                for b in range(NB + 2):
                    par = b % 2
                    if b < NB:
                        g0 = b * 4
                        sb, sbname = sc_banks[par]
                        war_sc = list(bank_free[sbname])
                        t_sc[b] = s.op("pe", lambda e, g0=g0, sb=sb: [
                            e.matmul(out=sb[:, i * 128:(i + 1) * 128],
                                     lhsT=kg[:, (g0 + i) * 128:(g0 + i + 1) * 128],
                                     rhs=qg[:, (g0 + i) * 128:(g0 + i + 1) * 128],
                                     start=True, stop=True) for i in range(4)][-1],
                            deps=[t_q_ready, t_kv_ready] + war_sc, inc=4)
                        war_vt = [t_vev[b - 1]] if b >= 1 else t_pvgp_free
                        t_vt[b] = s.op("pe", lambda e, g0=g0: [
                            e.transpose(out=pvgp[:, i * 64:(i + 1) * 64],
                                        in_=vg[:, (g0 + i) * 128:(g0 + i + 1) * 128],
                                        identity=idb[0:64, :]) for i in range(4)][-1],
                            deps=[t_kv_ready, t_idb] + war_vt, inc=4)
                        war_e = [t_ctx[b - 2]] if b >= 2 else t_ebuf_free[par]
                        t_exp[b] = s.op("act", lambda e, par=par, sb=sb: e.activation(
                            out=ebuf[par][:], in_=sb[:], func=AF.Exp),
                            deps=[t_sc[b]] + war_e)
                        bank_free[sbname] = [t_exp[b]]
                        t_mask[b] = s.op("dve", lambda e, par=par: e.tensor_tensor(
                            out=ebuf[par][:].rearrange("p (g c) -> p g c", c=128),
                            in0=ebuf[par][:].rearrange("p (g c) -> p g c", c=128),
                            in1=maskb[:].unsqueeze(1).broadcast_to([128, 4, 128]),
                            op=ALU.mult), deps=[t_exp[b], t_maskb])
                        war_v = [t_ctx[b - 2]] if b >= 2 else t_vb1_free[par]
                        t_vev[b] = s.op("dve", lambda e, par=par: e.tensor_copy(
                            out=vb1[par][:].rearrange("p (g c) -> p g c", c=65)[:, :, 0:64],
                            in_=pvgp[:, 0:256].rearrange("p (g c) -> p g c", c=64)),
                            deps=[t_vt[b]] + war_v)
                    bb = b - 1
                    if 0 <= bb < NB:
                        par = bb % 2
                        cb, cbname = ctx_banks[par]
                        war_c = list(bank_free[cbname])
                        t_ctx[bb] = s.op("pe", lambda e, par=par, cb=cb: [
                            e.matmul(out=cb[:, i * 65:(i + 1) * 65],
                                     lhsT=ebuf[par][:, i * 128:(i + 1) * 128],
                                     rhs=vb1[par][:, i * 65:(i + 1) * 65],
                                     start=True, stop=True) for i in range(4)][-1],
                            deps=[t_mask[bb], t_vev[bb]] + war_c, inc=4)
                        t_rc[bb] = s.op("dve", lambda e, par=par, cb=cb: e.reciprocal(
                            out=rcol[par][:], in_=cb[:, 0:260].rearrange(
                                "p (g c) -> p g c", c=65)[:, :, 64]),
                            deps=[t_ctx[bb]] + ([t_d4[bb - 2]] if bb >= 2
                                                else t_rcol_free[par]))
                        d4i = []
                        for i in range(4):
                            war_d = ([] if i else ([t_ct[bb - 2]] if bb >= 2
                                                   else t_d4_free[par]))
                            d4i.append(s.op("dve", lambda e, par=par, i=i:
                                            e.tensor_scalar(
                                out=d4[par][:, i * 64:(i + 1) * 64],
                                in0=idb[:], scalar1=rcol[par][:, i:i + 1],
                                scalar2=None, op0=ALU.mult),
                                deps=[t_rc[bb], t_idb] + war_d))
                        t_d4[bb] = d4i[-1]
                        t_d4i[bb] = d4i
                        war_g = [t_ct[bb - 2]] if bb >= 2 else t_ctxg_free[par]
                        t_cge[bb] = s.op("act", lambda e, par=par, cb=cb: e.activation(
                            out=ctxg[par][:].rearrange("p (g c) -> p g c", c=64),
                            in_=cb[:, 0:260].rearrange("p (g c) -> p g c", c=65)[:, :, 0:64],
                            func=AF.Identity, bias=0.0, scale=1.0),
                            deps=[t_ctx[bb]] + war_g)
                        bank_free[cbname] = [t_rc[bb], t_cge[bb]]
                    bc = b - 2
                    if 0 <= bc < NB:
                        par = bc % 2
                        war_t = [t_c2[0]] if t_c2[0] is not None else t_pc2_free
                        for i in range(4):
                            t_ct[bc] = s.op("pe", lambda e, par=par, i=i: [
                                e.matmul(out=pc2[(0 if pp == 0 else 64):
                                                 (64 if pp == 0 else 128),
                                                 i * 64:(i + 1) * 64],
                                         lhsT=ctxg[par][(0 if pp == 0 else 64):
                                                        (64 if pp == 0 else 128),
                                                        i * 64:(i + 1) * 64],
                                         rhs=d4[par][(0 if pp == 0 else 64):
                                                     (64 if pp == 0 else 128),
                                                     i * 64:(i + 1) * 64],
                                         start=True, stop=True,
                                         tile_position=(None if pp == 0
                                                        else (64, 64)))
                                for pp in range(2)][-1],
                                deps=[t_cge[bc], t_d4i[bc][i]]
                                     + (war_t if i == 0 else []), inc=2)
                        coff = hoff + bc * 32
                        t_c2[0] = s.op("dve", lambda e, coff=coff: e.tensor_copy(
                            out=c2sb[:].rearrange("p (j n) -> p j n", n=SLAB)[
                                :, :, coff:coff + 32].rearrange("p j (g t) -> p j g t", t=8),
                            in_=pc2[:, 0:256].rearrange("p (g j t) -> p j g t", g=4, j=8, t=8)),
                            deps=[t_ct[bc]] + t_c2_free)
                        half_c2.append(t_c2[0])
                t_pvgp_free = [t_vev[NB - 1]]
                t_pc2_free = [t_c2[0]]
                for p in range(2):
                    lastb = NB - 2 + p
                    t_ebuf_free[p] = [t_ctx[lastb]]
                    t_vb1_free[p] = [t_ctx[lastb]]
                    t_ctxg_free[p] = [t_ct[lastb]]
                    t_rcol_free[p] = [t_d4[lastb]]
                    t_d4_free[p] = [t_ct[lastb]]
                t_qkvg_free = [t_ct[NB - 1]] + dbg_qkv_toks
                dbg_qkv_toks = []
                c2_half[half] = [half_c2[-1]]
            wout_half(1, c2_half[1], with_ln=True)
            return ev2, fr3, last_qkv_mm[0], c2_half

        ev3_prev = []
        fr1_next = None
        t_h1_next = None
        for sl in range(NSLAB):
            A = rbuf[sl % 3]
            B = rbuf[(sl + 2) % 3]
            tA = t_x[sl]
            b_war = list(ev3_prev)
            if sl == 0:
                fr1 = ln_front(tA, A)
                t_h1 = ln_back(fr1, dst=c2sb, dst_war=t_c2_free, per_chunk=True)
            else:
                t_h1 = t_h1_next
            ev1, t_mm1, fr2 = projection(t_h1, t_weffb, weffb, 1024, NCH, b1, t_b1,
                                         A, tA, B, b_war, operand=c2sb,
                                         ln_next_src=B)
            t_y = ev1[-1]
            t_c2_free = [t_mm1]
            if DEBUG and sl == 0:
                t_d = dma(dbg_y, B[:], "dbg0", deps=ev1)
                ev1 = ev1 + [t_d]
            t_h2 = ln_back(fr2, per_chunk="ch")
            if DEBUG and sl == 0:
                t_dh = dma(dbg_h2, hbuf[:], "dbg1", deps=[t_h2])
                t_hbuf_free_extra = [t_dh]
            else:
                t_hbuf_free_extra = []
            ev2, fr3, t_mmq, c2_half = qkv_and_attention(
                t_h2, sl, B, t_y, A, ev1, bo, A)
            t_hbuf_free = [t_mmq] + t_hbuf_free_extra
            t_z = ev2[-1]
            t_c2_free = list(ev2)
            if DEBUG and sl == 0:
                t_dc2 = dma(dbg_c2, c2sb[:], "dbg5", deps=c2_half[1])
                t_dz = dma(dbg_z, A[:], "dbg6", deps=ev2)
                t_c2_free = t_c2_free + [t_dc2]
                ev2 = ev2 + [t_dz]
            t_h3 = ln_back(fr3, per_chunk=True)
            if sl + 1 < NSLAB:
                pf_deps = [t_store[sl - 1]] if sl >= 1 else []
                t_x[sl + 1] = dma(rbuf[(sl + 1) % 3][:].rearrange(
                    "p (c n) -> p c n", n=SLAB),
                    xt_v[:, :, (sl + 1) * SLAB:(sl + 2) * SLAB],
                    "ldx%d" % ((sl + 1) % 2), deps=pf_deps)
            pe_mid = None
            if sl + 1 < NSLAB:
                nxt = rbuf[(sl + 1) % 3]
                fr1n = ln_front_start()
                for c in range(NCH):
                    ln_front_chunk(fr1n, c, t_x[sl + 1], nxt, sq=False)
                for c in range(NCH):
                    ln_front_square(fr1n, c)
                h1n_holder = []

                h1n_holder = []

                def pe_mid(phase, fr1n=fr1n, h1n_holder=h1n_holder,
                           c2w=list(t_c2_free)):
                    if phase == 0:
                        ln_front_finish(fr1n, which="s")
                    else:
                        ln_front_finish(fr1n, which="q")
                        h1n_holder.append(ln_back(fr1n, dst=c2sb, dst_war=c2w,
                                                  per_chunk=True))
            ev3, t_mm3, _ = projection(t_h3, t_weffb, weffb, 1024, NCH, b1, t_b1,
                                       A, t_z, B, ev2, pe_mid=pe_mid)
            if sl + 1 < NSLAB:
                t_h1_next = h1n_holder[0]
            t_hbuf_free = [t_mm3]
            ev3_prev = ev3
            if sl == NSLAB - 1:
                dma(yt_v[:, 0:4, sl * SLAB:(sl + 1) * SLAB],
                    B[:, 0:4 * SLAB].rearrange("p (c n) -> p c n", n=SLAB),
                    "st0", deps=ev3[0:4])
                t_store[sl] = dma(
                    yt_v[:, 4:8, sl * SLAB:(sl + 1) * SLAB],
                    B[:, 4 * SLAB:8 * SLAB].rearrange("p (c n) -> p c n", n=SLAB),
                    "st1", deps=ev3[4:8])
            else:
                t_store[sl] = dma(
                    yt_v[:, :, sl * SLAB:(sl + 1) * SLAB],
                    B[:].rearrange("p (c n) -> p c n", n=SLAB),
                    "st%d" % (sl % 2), deps=ev3)

        fin_deps = [("st0", s.counts["st0"]), ("st1", s.counts["st1"])]
        if DEBUG:
            fin_deps += [(d, s.counts[d]) for d in DMASEMS if d.startswith("dbg")
                         and s.counts[d] > 0]
        s.op("sp", lambda e: None, deps=fin_deps, inc=0)

        @block.sync
        def _(sync):
            s.replay(sync, "sp", sem_handles)

        @block.vector
        def _(vector):
            s.replay(vector, "dve", sem_handles)

        @block.tensor
        def _(tensor):
            s.replay(tensor, "pe", sem_handles)

        @block.scalar
        def _(scalar):
            s.replay(scalar, "act", sem_handles)

        @block.gpsimd
        def _(gpsimd):
            s.replay(gpsimd, "pool", sem_handles)

    return nc


def _host_inputs(L0_attn_x, L0_attn_W_qkv, L0_attn_b_qkv, L0_attn_W_out, L0_attn_b_out,
                 L0_ffn_W, L0_ffn_b, L0_ffn_lora_down, L0_ffn_lora_up,
                 L0_ffn_down_aux, L0_ffn_up_aux):
    x = np.asarray(L0_attn_x, dtype=np.float32).reshape(NCORES * TCORE, D)
    kindv, hv, dv = np.meshgrid(np.arange(3), np.arange(16), np.arange(64),
                                indexing="ij")
    perm = (192 * hv + 64 * kindv + dv).reshape(-1)
    wqkv = np.ascontiguousarray(np.asarray(L0_attn_W_qkv, dtype=np.float32)[:, perm])
    w1t = np.ascontiguousarray(np.asarray(L0_ffn_W, dtype=np.float32).T)
    wout = np.ascontiguousarray(np.asarray(L0_attn_W_out, dtype=np.float32))
    upat = np.ascontiguousarray(np.asarray(L0_ffn_up_aux, dtype=np.float32).T)
    lup = np.ascontiguousarray(np.asarray(L0_ffn_lora_up, dtype=np.float32))
    ldn = np.ascontiguousarray(np.asarray(L0_ffn_lora_down, dtype=np.float32))
    dnaux = np.ascontiguousarray(np.asarray(L0_ffn_down_aux, dtype=np.float32))
    bqd = np.ascontiguousarray(
        np.asarray(L0_attn_b_qkv, dtype=np.float32)[perm].reshape(24, 128).T)
    b1d = np.ascontiguousarray(
        np.asarray(L0_ffn_b, dtype=np.float32).reshape(8, 128).T)
    bod = np.ascontiguousarray(
        np.asarray(L0_attn_b_out, dtype=np.float32).reshape(8, 128).T)
    mask = np.zeros((128, 128), dtype=np.float32)
    for g in range(16):
        for t in range(8):
            for p in range(2):
                for j in range(8):
                    mask[g * 8 + t, p * 64 + j * 8 + t] = 1.0
    ident = np.zeros((128, 64), dtype=np.float32)
    ident[0:64] = np.eye(64, dtype=np.float32)
    ident[64:128] = np.eye(64, dtype=np.float32)
    shared = dict(wqkv=wqkv, w1t=w1t, wout=wout, upat=upat, lup=lup, ldn=ldn,
                  dnaux=dnaux, bqd=bqd, b1d=b1d, bod=bod, maskd=mask, identd=ident)
    in_maps = []
    for c in range(NCORES):
        shard = x[c * TCORE:(c + 1) * TCORE]
        xt = np.ascontiguousarray(shard.T)
        m = dict(shared)
        m["xt"] = xt
        in_maps.append(m)
    return in_maps


def kernel(L0_attn_x, L0_attn_W_qkv, L0_attn_b_qkv, L0_attn_W_out, L0_attn_b_out,
           L0_ffn_W, L0_ffn_b, L0_ffn_lora_down, L0_ffn_lora_up,
           L0_ffn_down_aux, L0_ffn_up_aux):
    in_maps = _host_inputs(L0_attn_x, L0_attn_W_qkv, L0_attn_b_qkv, L0_attn_W_out,
                           L0_attn_b_out, L0_ffn_W, L0_ffn_b, L0_ffn_lora_down,
                           L0_ffn_lora_up, L0_ffn_down_aux, L0_ffn_up_aux)
    nc = build_nc()
    res = run_bass_kernel_spmd(nc, in_maps, core_ids=list(range(NCORES)))
    outs = []
    for c in range(NCORES):
        yt = res.results[c]["yt"]
        outs.append(np.ascontiguousarray(yt.T))
    full = np.concatenate(outs, axis=0)
    return full.reshape(np.asarray(L0_attn_x).shape).astype(np.float32)
